# version 05
# baseline (speedup 1.0000x reference)
import numpy as np

import concourse.bass as bass
import concourse.mybir as mybir
from concourse.bass_utils import run_bass_kernel_spmd

F32 = mybir.dt.float32
BF16 = mybir.dt.bfloat16
AF = mybir.ActivationFunctionType
ALU = mybir.AluOpType
AX = mybir.AxisListType.X

T = 2048
D = 1024
DI = 2048
DFF = 4096
NH = 16
DH = 64
NTD = D // 128
NTI = DI // 128
NTF = DFF // 128
EPSA = 1e-6
EPSLN = 1e-5

ENGINES = ("sync", "tensor", "vector", "scalar", "gpsimd")
_DMA_QUEUES = ("sync", "gpsimd")


DMA_POOL = {"sync": 18, "gpsimd": 8}


class Op:
    __slots__ = ("eng", "fn", "reads", "writes", "idx", "is_dma", "need_inc",
                 "cum", "deps", "sem_name", "pin", "epos")

    def __init__(self, eng, fn, reads, writes, idx, is_dma):
        self.eng = eng
        self.fn = fn
        self.reads = reads
        self.writes = writes
        self.idx = idx
        self.is_dma = is_dma
        self.need_inc = False
        self.cum = 0
        self.deps = []
        self.sem_name = None
        self.pin = False
        self.epos = 0


class Sched:

    MILE_EVERY = 12

    def __init__(self):
        self.ops = []
        self.last_w = {}
        self.readers = {}
        self.gfence = None
        self.ecount = {e: 0 for e in ENGINES}
        self.miles = {e: [] for e in ENGINES}
        self.last_dma = {}
        self.dcnt = {}
        self.last_real = {}
        self.barriers = []

    def prewait(self, eng, keys):
        return self._add(eng, lambda e: e.nop(), tuple(keys), (), False)

    def fence(self, fdum=None):
        keys = tuple(self.last_w.keys())
        fn = (lambda e: e.nop()) if fdum is None else (
            lambda e: e.memset(fdum, 0.0))
        op = self._add("vector", fn, (), keys, False)
        ded = {d.idx: d for d in op.deps}
        for d in self.last_dma.values():
            ded[d.idx] = d
        op.deps = [ded[i] for i in sorted(ded)]
        self.barriers.append(dict(idx=op.idx, last=dict(self.last_real),
                                  dmas=list(self.last_dma.values())))
        return op

    def _add(self, eng, fn, reads, writes, is_dma, _mile=False):
        op = Op(eng, fn, tuple(reads), tuple(writes), len(self.ops), is_dma)
        op.epos = self.ecount[eng]
        self.ecount[eng] += 1
        if not is_dma and op.epos % self.MILE_EVERY == self.MILE_EVERY - 1:
            self.miles[eng].append(op)
        deps = []
        for k in op.reads:
            w = self.last_w.get(k)
            if w is not None:
                deps.append(w)
        for k in op.writes:
            w = self.last_w.get(k)
            if w is not None:
                deps.append(w)
            deps.extend(self.readers.get(k, ()))

        ded = {}
        for d in deps:
            if d is op:
                continue
            if (d.eng == eng and not d.is_dma and not is_dma
                    and eng in ("tensor", "gpsimd")):
                continue
            ded[d.idx] = d
        op.deps = [ded[i] for i in sorted(ded)]
        for k in op.reads:
            self.readers.setdefault(k, []).append(op)
        for k in op.writes:
            self.last_w[k] = op
            self.readers[k] = [op]
        self.ops.append(op)
        if is_dma:
            self.dcnt[eng] = self.dcnt.get(eng, 0) + 1
            self.last_dma[(eng, (self.dcnt[eng] - 1) % DMA_POOL[eng])] = op
        else:
            self.last_real[eng] = op
        return op

    def t(self, fn, r=(), w=()):
        return self._add("tensor", fn, r, w, False)

    def v(self, fn, r=(), w=()):
        return self._add("vector", fn, r, w, False)

    def s(self, fn, r=(), w=()):
        return self._add("scalar", fn, r, w, False)

    def g(self, fn, r=(), w=()):
        return self._add("gpsimd", fn, r, w, False)

    def dma(self, queue, out, in_, r=(), w=(), slow=False):
        if slow:
            fn = lambda e, o=out, i=in_: e.dma_start(
                out=o, in_=i, allow_slow_non_contiguous=True)
        else:
            fn = lambda e, o=out, i=in_: e.dma_start(out=o, in_=i)
        return self._add(queue, fn, r, w, True)

    def finalize(self, sems):
        import bisect
        mpos = {e: [m.epos for m in self.miles[e]] for e in ENGINES}
        for op in self.ops:
            nd = {}
            for d in op.deps:
                if not d.is_dma:
                    i = bisect.bisect_right(mpos[d.eng], d.epos)
                    if i < len(self.miles[d.eng]):
                        m = self.miles[d.eng][i]
                        if m.idx < op.idx:
                            d = m
                nd[d.idx] = d
            op.deps = [nd[i] for i in sorted(nd)]
        for spec in self.barriers:
            for o in spec["last"].values():
                o.need_inc = True
                o.pin = True
        for op in self.ops:
            for d in op.deps:
                d.need_inc = True
                if op.eng != d.eng or op.is_dma or d.is_dma:
                    d.pin = True
        cums = {e: 0 for e in ENGINES}
        qidx = {q: 0 for q in _DMA_QUEUES}
        finals = {}
        for op in self.ops:
            if op.is_dma:
                qi = qidx[op.eng]
                qidx[op.eng] += 1
                P = DMA_POOL[op.eng]
                op.sem_name = f"dq_{op.eng}_{qi % P}"
                op.cum = 16 * (qi // P + 1)
                op.need_inc = True
                finals[op.sem_name] = op.cum
            elif op.need_inc:
                cums[op.eng] += 1
                op.cum = cums[op.eng]
        self.totals = cums
        bodies = {}
        for e in ENGINES:
            eops = [op for op in self.ops if op.eng == e]

            barr = list(self.barriers)

            def body(eng, eops=eops, e=e, barr=barr):
                seen = {}
                bi = 0
                for op in eops:
                    while bi < len(barr) and barr[bi]["idx"] < op.idx:
                        for e2, o in barr[bi]["last"].items():
                            if e2 == e or not o.need_inc:
                                continue
                            if seen.get(e2, -1) < o.cum:
                                eng.wait_ge(sems[e2], o.cum)
                                seen[e2] = o.cum
                        for d in barr[bi]["dmas"]:
                            if seen.get(d.sem_name, -1) < d.cum:
                                eng.wait_ge(sems[d.sem_name], d.cum)
                                seen[d.sem_name] = d.cum
                        bi += 1
                    if op.is_dma and op.cum > 16:
                        if seen.get(op.sem_name, -1) < op.cum - 16:
                            eng.wait_ge(sems[op.sem_name], op.cum - 16)
                            seen[op.sem_name] = op.cum - 16
                    for d in op.deps:
                        sname = d.sem_name if d.is_dma else d.eng
                        if seen.get(sname, -1) < d.cum:
                            eng.wait_ge(sems[sname], d.cum)
                            seen[sname] = d.cum
                    try:
                        inst = op.fn(eng)
                    except Exception:
                        print("FAILING OP idx", op.idx, "eng", op.eng,
                              "reads", op.reads, "writes", op.writes)
                        raise
                    if op.need_inc:
                        if op.is_dma:
                            inst.then_inc(sems[op.sem_name], 16)
                        else:
                            if inst is None:
                                inst = eng.nop()
                            inst.then_inc(sems[e], 1)
                            if op.pin:
                                eng.wait_ge(sems[e], op.cum)
                                if seen.get(e, -1) < op.cum:
                                    seen[e] = op.cum
                if e == "sync":
                    for sn, v in sorted(finals.items()):
                        if seen.get(sn, -1) < v:
                            eng.wait_ge(sems[sn], v)
            bodies[e] = body
        return bodies


def build_program(n_sub=4):
    nc = bass.Bass("TRN2", target_bir_lowering=False)

    def din(name, shape, dt=F32):
        return nc.dram_tensor(name, list(shape), dt, kind="ExternalInput").ap()

    I = {}
    for nm, shp in (("x_cm", (D, T)), ("WqT", (D, D)), ("WkT", (D, D)),
                    ("WvT", (D, D)), ("WoT", (D, D)), ("W1T", (D, DFF)),
                    ("W2T", (DFF, D)), ("WinT", (D, 2 * DI)), ("WdtT", (DI, DI)),
                    ("WBT", (DI, 16)), ("WCT", (DI, 16)), ("WoutT", (DI, D)),
                    ("b1v", (DFF, 1)), ("b2v", (D, 1)), ("nwv", (D, 1)),
                    ("nbv", (D, 1)), ("convw", (DI, 4)), ("convb", (DI, 1)),
                    ("bdtv", (DI, 1)), ("Dv", (DI, 1)), ("Alog", (DI, 16))):
        I[nm] = din(nm, shp)
    for nm, shp in (("identH", (128, 128)), ("place2H", (2, 128)),
                    ("mask2H", (128, 2)), ("maskPH", (128, 1)),
                    ("maskDH", (128, 1)), ("selH", (16, NTD * 128)),
                    ("sel16H", (16, 16 * 128))):
        I[nm] = din(nm, shp, BF16)
    I["selHF"] = din("selHF", (16, NTD * 128))

    y_cm = nc.dram_tensor("y_cm", [D, T], F32, kind="ExternalOutput").ap()

    xres = nc.dram_tensor("xres", [D, T], F32, kind="Internal").ap()
    Osp = nc.dram_tensor("Osp", [D, T], BF16, kind="Internal").ap()
    Wq1s = nc.dram_tensor("Wq1s", [D, DFF], BF16, kind="Internal").ap()
    Wq2s = nc.dram_tensor("Wq2s", [DFF, D], BF16, kind="Internal").ap()
    xcs = nc.dram_tensor("xcs", [DI, T], BF16, kind="Internal").ap()
    szs = nc.dram_tensor("szs", [DI, T], BF16, kind="Internal").ap()
    dts = nc.dram_tensor("dts", [DI, T], BF16, kind="Internal").ap()
    xlns = nc.dram_tensor("xlns", [D, T], F32, kind="Internal").ap()

    from contextlib import ExitStack
    st = ExitStack()
    with st:
        sb = lambda *a: st.enter_context(nc.sbuf_tensor(*a))
        ps_ = lambda *a: st.enter_context(nc.psum_tensor(*a))
        E = {}
        E["arW1"] = sb("arW1", [128, 32768], BF16)
        E["arW2"] = sb("arW2", [128, 24576], BF16)
        E["arA"] = sb("arA", [128, 16640], BF16)
        E["arS"] = sb("arS", [128, 4096], BF16)
        E["arFS"] = sb("arFS", [128, 2048], F32)
        E["arRow"] = sb("arRow", [128, 2048], F32)
        E["arR2"] = sb("arR2", [128, 1024], F32)
        E["den16"] = sb("den16", [16, 2048], F32)
        E["arQK2"] = sb("arQK2", [128, 4096], BF16)
        E["sqS"] = sb("sqS", [128, 1024], BF16)
        E["cells"] = sb("cells", [128, 16], F32)
        E["cst"] = sb("cst", [128, 8], F32)
        E["onesb"] = sb("onesb", [128, 1], BF16)
        E["onesrF"] = sb("onesrF", [1, 128], F32)
        E["onesbF"] = sb("onesbF", [128, 1], F32)
        E["onesr"] = sb("onesr", [1, 128], BF16)
        E["ident"] = sb("ident", [128, 128], BF16)
        E["place2"] = sb("place2", [2, 128], BF16)
        E["mask2"] = sb("mask2", [128, 2], BF16)
        E["maskP"] = sb("maskP", [128, 1], BF16)
        E["maskD"] = sb("maskD", [128, 1], BF16)
        E["selO"] = sb("selO", [16, NTD * 128], BF16)
        E["selOF"] = sb("selOF", [16, NTD * 128], F32)
        E["sel16"] = sb("sel16", [16, 16 * 128], BF16)
        E["nw8"] = sb("nw8", [128, 8], F32)
        E["nb8"] = sb("nb8", [128, 8], F32)
        E["b18"] = sb("b18", [128, 32], F32)
        E["b28"] = sb("b28", [128, 8], F32)
        E["cw16"] = sb("cw16", [128, 64], F32)
        E["cb16"] = sb("cb16", [128, 16], F32)
        E["bdt16"] = sb("bdt16", [128, 16], F32)
        E["dd16"] = sb("dd16", [128, 16], F32)
        E["negA"] = sb("negA", [128, 256], F32)
        E["gamr"] = sb("gamr", [128, 4], F32)
        E["biasx"] = sb("biasx", [128, 8], F32)
        E["carry"] = sb("carry", [128, 256], F32)
        E["brow"] = sb("brow", [16, 1024], BF16)
        E["wbc"] = sb("wbc", [128, 512], BF16)

        E["psA"] = ps_("psA", [128, 512], F32)
        E["psB"] = ps_("psB", [128, 512], F32)
        E["psR"] = ps_("psR", [128, 512], F32)
        E["psL"] = ps_("psL", [128, 512], F32)
        E["psC"] = ps_("psC", [128, 1024], F32)
        E["psT"] = ps_("psT", [128, 2048], BF16)

        E.update(I=I, y_cm=y_cm, xres=xres, Osp=Osp, Wq1s=Wq1s, Wq2s=Wq2s,
                 xcs=xcs, szs=szs, dts=dts, xlns=xlns)

        sems = {}
        for e in ENGINES:
            sems[e] = st.enter_context(nc.semaphore("s_" + e))
        for q in _DMA_QUEUES:
            for i in range(DMA_POOL[q]):
                sems[f"dq_{q}_{i}"] = st.enter_context(
                    nc.semaphore(f"dq_{q}_{i}"))

        S = Sched()
        _emit(S, E, n_sub)
        bodies = S.finalize(sems)
        with nc.Block() as block:
            block.sync(bodies["sync"])
            block.tensor(bodies["tensor"])
            block.vector(bodies["vector"])
            block.scalar(bodies["scalar"])
            block.gpsimd(bodies["gpsimd"])
    return nc


def _emit(S, E, n_sub):
    I = E["I"]
    arW1, arW2, arA, arS, arFS = (E["arW1"], E["arW2"], E["arA"], E["arS"],
                                  E["arFS"])
    arRow, arR2, den16, cells = (E["arRow"], E["arR2"], E["den16"],
                                 E["cells"])
    arQK2 = E["arQK2"]
    sqS = E["sqS"]
    cst, onesb, onesr, onesrF = E["cst"], E["onesb"], E["onesr"], E["onesrF"]
    onesbF = E["onesbF"]
    ident, place2, mask2, maskP, maskD = (E["ident"], E["place2"], E["mask2"],
                                          E["maskP"], E["maskD"])
    selO, sel16, selOF = E["selO"], E["sel16"], E["selOF"]
    nw8, nb8, b18, b28 = E["nw8"], E["nb8"], E["b18"], E["b28"]
    cw16, cb16, bdt16, dd16 = E["cw16"], E["cb16"], E["bdt16"], E["dd16"]
    negA, gamr, biasx, carry = E["negA"], E["gamr"], E["biasx"], E["carry"]
    brow, wbc = E["brow"], E["wbc"]
    psA, psB, psR, psL, psC, psT = (E["psA"], E["psB"], E["psR"], E["psL"],
                                    E["psC"], E["psT"])
    y_cm, xres, Osp = E["y_cm"], E["xres"], E["Osp"]
    Wq1s, Wq2s, xcs, szs, dts, xlns = (E["Wq1s"], E["Wq2s"], E["xcs"],
                                       E["szs"], E["dts"], E["xlns"])

    RS = [arRow[0:1, 512 * i:512 * (i + 1)] for i in range(4)]
    RK = [f"RS{i}" for i in range(4)]
    QR = [arR2[0:2, 512 * i:512 * (i + 1)] for i in range(2)]
    QK = [f"QR{i}" for i in range(2)]
    CL = lambda i: cells[0:1, i:i + 1]
    CK = lambda i: f"CL{i}"

    def mm(out, lhsT, rhs, *, st, sp, r, w):
        return S.t(lambda e, o=out, l=lhsT, rr=rhs, st_=st, sp_=sp:
                   e.matmul(o, l, rr, start=st_, stop=sp_),
                   r=tuple(r), w=tuple(w))

    def act(out, in_, func, *, r, w, bias=None, scale=1.0):
        if bias is None:
            fn = lambda e, o=out, i=in_, f=func, sc=scale: e.activation(
                o, i, f, scale=sc)
        else:
            fn = lambda e, o=out, i=in_, f=func, bb=bias, sc=scale: e.activation(
                o, i, f, bias=bb, scale=sc)
        return S.s(fn, r=tuple(r), w=tuple(w))

    def vop(fn, r, w):
        return S.v(fn, r=tuple(r), w=tuple(w))

    csts = {0: 1.0, 1: EPSLN, 2: 1.0 + EPSA, 3: EPSA, 4: 0.001, 5: 5.0,
            6: 0.125, 7: 1e-30}
    for i, vv in csts.items():
        S.g(lambda e, i=i, vv=vv: e.memset(cst[:, i:i + 1], vv), w=[f"cst{i}"])
    S.g(lambda e: e.memset(onesb[:], 1.0), w=["onesb"])
    S.g(lambda e: e.memset(onesr[:], 1.0), w=["onesr"])
    S.g(lambda e: e.memset(onesrF[:], 1.0), w=["onesrF"])
    S.g(lambda e: e.memset(onesbF[:], 1.0), w=["onesbF"])
    S.g(lambda e: e.memset(carry[:], 0.0),
        w=["carry"] + [f"cr{s}" for s in range(256)])

    for nm, dst in (("identH", ident), ("place2H", place2), ("mask2H", mask2),
                    ("maskPH", maskP), ("maskDH", maskD), ("selH", selO),
                    ("selHF", E["selOF"]), ("sel16H", sel16)):
        S.dma("sync", dst[:], I[nm], w=[nm])
    for nm, dst in (("nwv", nw8), ("nbv", nb8), ("b1v", b18), ("b2v", b28),
                    ("convb", cb16), ("bdtv", bdt16), ("Dv", dd16)):
        S.dma("sync", dst[:], I[nm].rearrange("(j p) o -> p (j o)", p=128),
              w=[nm], slow=True)
    S.dma("sync", cw16[:].rearrange("p (j k) -> p j k", k=4),
          I["convw"].rearrange("(j p) k -> p j k", p=128), w=["cw16"])
    S.dma("sync", negA[:].rearrange("p (j n) -> p j n", n=16),
          I["Alog"].rearrange("(j p) n -> p j n", p=128), w=["negA"])
    S.s(lambda e: e.activation(negA[:], negA[:], AF.Exp), r=["negA"], w=["negA"])
    S.v(lambda e: e.tensor_scalar(negA[:], negA[:], -1.0, None, ALU.mult),
        r=["negA"], w=["negA"])
    S.dma("gpsimd", wbc[:, 0:256].rearrange("p (j n) -> p j n", n=16),
          I["WBT"].rearrange("(j p) n -> p j n", p=128), w=["wbc"])
    S.dma("gpsimd", wbc[:, 256:512].rearrange("p (j n) -> p j n", n=16),
          I["WCT"].rearrange("(j p) n -> p j n", p=128), w=["wbc"])
    for wi, nm in enumerate(("WqT", "WkT", "WvT", "WoT")):
        S.dma("gpsimd", arW1[:, 8192 * wi:8192 * (wi + 1)].rearrange(
                  "p (j o) -> p j o", o=D),
              I[nm].rearrange("(j p) o -> p j o", p=128), w=[f"aW{wi}"])

    def ln_chunk(tag, src, c0, CH, out_bf, out_key, *, clip8, fkey=None,
                 sqb=None):
        fk = fkey or (lambda j: f"{tag}x{j}")
        if sqb is None:
            sqb = lambda i: arS[:, 256 + 512 * i:256 + 512 * i + CH]
        R = [arRow[0:1, 512 * i:512 * i + CH] for i in range(4)]
        for j in range(NTD):
            S.dma("sync", arFS[:, CH * j:CH * (j + 1)],
                  src[128 * j:128 * (j + 1), c0:c0 + CH], w=[fk(j)])
            S.dma("gpsimd", out_bf(j), src[128 * j:128 * (j + 1), c0:c0 + CH],
                  w=[out_key(j)])
        S.prewait("tensor", ["onesb"] + [out_key(j) for j in range(NTD)])
        for j in range(NTD):
            mm(psR[0:1, 0:CH], onesb[:], out_bf(j), st=(j == 0),
               sp=(j == NTD - 1), r=["onesb", out_key(j)], w=["psR"])
        vop(lambda e: e.tensor_scalar(R[0], psR[0:1, 0:CH], 1.0 / D, None,
                                      ALU.mult), r=["psR"], w=[RK[0]])
        for j in range(NTD):
            sq = sqb(j % 2)
            S.v(lambda e, j=j, sq=sq: e.tensor_mul(sq, out_bf(j), out_bf(j)),
                r=[out_key(j)], w=[f"sqt{j % 2}"])
            mm(psA[0:1, 0:CH], onesb[:], sq, st=(j == 0), sp=(j == NTD - 1),
               r=["onesb", f"sqt{j % 2}"], w=["psA"])
        vop(lambda e: e.tensor_scalar(R[1], psA[0:1, 0:CH], 1.0 / D, None,
                                      ALU.mult), r=["psA"], w=[RK[1]])
        vop(lambda e: e.tensor_mul(R[2], R[0], R[0]), r=[RK[0]], w=[RK[2]])
        vop(lambda e: e.tensor_sub(R[1], R[1], R[2]), r=[RK[1], RK[2]],
            w=[RK[1]])
        act(R[1], R[1], AF.Ln, bias=cst[0:1, 1:2], r=[RK[1], "cst1"], w=[RK[1]])
        act(R[1], R[1], AF.Exp, scale=-0.5, r=[RK[1]], w=[RK[1]])
        vop(lambda e: e.tensor_mul(R[0], R[0], R[1]), r=[RK[0], RK[1]],
            w=[RK[0]])
        vop(lambda e: e.tensor_scalar(R[0], R[0], -1.0, None, ALU.mult),
            r=[RK[0]], w=[RK[0]])
        mm(psL[:, 0:CH], onesrF[0:1, :], R[1], st=True, sp=True,
           r=["onesrF", RK[1]], w=["psL0"])
        mm(psL[:, CH:2 * CH], onesrF[0:1, :], R[0], st=True, sp=True,
           r=["onesrF", RK[0]], w=["psL0"])
        for j in range(NTD):
            fs = arFS[:, CH * j:CH * (j + 1)]
            S.v(lambda e, fs=fs: e.tensor_mul(fs, fs, psL[:, 0:CH]),
                r=[fk(j), "psL0"], w=[fk(j)])
            S.v(lambda e, fs=fs: e.tensor_add(fs, fs, psL[:, CH:2 * CH]),
                r=[fk(j), "psL0"], w=[fk(j)])
            if clip8:
                S.v(lambda e, j=j, fs=fs: e.tensor_scalar(out_bf(j), fs, -8.0,
                                                          8.0, ALU.max, ALU.min),
                    r=[fk(j)], w=[out_key(j)])
            else:
                S.v(lambda e, j=j, fs=fs: e.tensor_copy(out_bf(j), fs),
                    r=[fk(j)], w=[out_key(j)])

    def pipe2(gens, depth=2):
        from collections import deque
        act = deque()
        gens = list(gens)
        while gens or act:
            while len(act) < depth and gens:
                act.append(gens.pop(0))
            g = act.popleft()
            try:
                next(g)
                act.append(g)
            except StopIteration:
                pass

    def ln_bf(tag, src, c0, CH, xb, xbk, par, *, clip8, sqb):
        yield from _ln_bf(tag, src, c0, CH, xb, xbk, par, clip8, sqb)

    def _ln_bf(tag, src, c0, CH, xb, xbk, par, clip8, sqb):
        R = [arRow[0:1, 256 * (4 * par + i):256 * (4 * par + i) + CH]
             for i in range(4)]
        RKp = [f"row{par}{i}" for i in range(4)]
        pstat = psC[0:1, 512 * par:512 * par + CH]
        psk = f"psC{par}"
        for j in range(NTD):
            S.dma("gpsimd", xb(j), src[128 * j:128 * (j + 1), c0:c0 + CH],
                  w=[xbk(j)])
        yield
        S.prewait("tensor", ["onesb"] + [xbk(j) for j in range(NTD)])
        for j in range(NTD):
            mm(pstat, onesb[:], xb(j), st=(j == 0), sp=(j == NTD - 1),
               r=["onesb", xbk(j)], w=[psk])
        vop(lambda e: e.tensor_scalar(R[0], pstat, 1.0 / D, None, ALU.mult),
            r=[psk], w=[RKp[0]])
        yield
        for j in range(NTD):
            sq = sqb(j % 2)
            S.v(lambda e, j=j, sq=sq: e.tensor_mul(sq, xb(j), xb(j)),
                r=[xbk(j)], w=[f"{tag}sq{par}{j % 2}"])
            mm(pstat, onesb[:], sq, st=(j == 0), sp=(j == NTD - 1),
               r=["onesb", f"{tag}sq{par}{j % 2}"], w=[psk])
        vop(lambda e: e.tensor_scalar(R[1], pstat, 1.0 / D, None, ALU.mult),
            r=[psk], w=[RKp[1]])
        yield
        vop(lambda e: e.tensor_mul(R[2], R[0], R[0]), r=[RKp[0]], w=[RKp[2]])
        vop(lambda e: e.tensor_sub(R[1], R[1], R[2]), r=[RKp[1], RKp[2]],
            w=[RKp[1]])
        act(R[1], R[1], AF.Ln, bias=cst[0:1, 1:2], r=[RKp[1], "cst1"], w=[RKp[1]])
        act(R[1], R[1], AF.Exp, scale=-0.5, r=[RKp[1]], w=[RKp[1]])
        vop(lambda e: e.tensor_mul(R[0], R[0], R[1]), r=[RKp[0], RKp[1]],
            w=[RKp[0]])
        vop(lambda e: e.tensor_scalar(R[0], R[0], -1.0, None, ALU.mult),
            r=[RKp[0]], w=[RKp[0]])
        pbank = psL if par == 0 else psR
        pb1 = pbank[:, 0:CH]
        pb2 = pbank[:, 256:256 + CH]
        bk1 = ("bcL0", "bcR0")[par]
        bk2 = ("bcL1", "bcR1")[par]
        mm(pb1, onesrF[0:1, :], R[1], st=True, sp=True, r=["onesrF", RKp[1]],
           w=[bk1])
        mm(pb2, onesrF[0:1, :], R[0], st=True, sp=True, r=["onesrF", RKp[0]],
           w=[bk2])
        yield
        for j in range(NTD):
            S.v(lambda e, j=j, pb1=pb1: e.tensor_mul(xb(j), xb(j), pb1),
                r=[xbk(j), bk1], w=[xbk(j)])
            S.v(lambda e, j=j, pb2=pb2: e.tensor_add(xb(j), xb(j), pb2),
                r=[xbk(j), bk2], w=[xbk(j)])
            if clip8:
                S.v(lambda e, j=j: e.tensor_scalar(xb(j), xb(j), -8.0, 8.0,
                                                   ALU.max, ALU.min),
                    r=[xbk(j)], w=[xbk(j)])
            if j % 3 == 2:
                yield

    def sub1():
        CH = 256
        vk = lambda j: f"v{j}"

        for tt in range(16):
            ap = arA[:, 1040 * tt:1040 * tt + 1040].rearrange(
                "p (h c) -> p h c", c=65)[:, :, 64:65]
            S.g(lambda e, ap=ap: e.memset(ap, 1.0),
                w=[f"V65:{tt}.{pp}" for pp in range(8)])

        def a1_chunk(c):
            c0 = c * CH
            par = c % 2
            spb = lambda j, par=par: arS[:, 256 * (2 * j + par):256 * (2 * j + par) + CH]
            spk = lambda j, par=par: f"sp{par}.{j}"
            RL = [arRow[0:1, 256 * (4 * par + i):256 * (4 * par + i) + CH]
                  for i in range(4)]
            RKL = [f"row{par}{i}" for i in range(4)]
            yield from ln_bf("A1", I["x_cm"], c0, CH, spb, spk, par, clip8=True,
                  sqb=lambda i, par=par: arW2[:, 20480 + 256 * (2 * par + i):
                                              20480 + 256 * (2 * par + i) + CH])
            for j in range(NTD):
                sq = arW2[:, 20480 + 256 * (2 * par + j % 2):20480 + 256 * (2 * par + j % 2) + CH]
                S.v(lambda e, j=j, sq=sq: e.tensor_mul(sq, spb(j), spb(j)),
                    r=[spk(j)], w=[f"A1sq{par}{j % 2}"])
                mm(psC[0:1, 512 * par:512 * par + CH], maskD[:] if j == 0 else onesb[:], sq,
                   st=(j == 0), sp=(j == NTD - 1),
                   r=["maskDH" if j == 0 else "onesb", f"A1sq{par}{j % 2}"], w=[f"psC{par}"])
            yield
            vop(lambda e, par=par: e.tensor_copy(RL[0], psC[0:1, 512 * par:512 * par + CH]), r=[f"psC{par}"], w=[RKL[0]])
            act(RL[1], RL[0], AF.Sqrt, bias=cst[0:1, 2:3], r=[RKL[0], "cst2"],
                w=[RKL[1]])
            vop(lambda e: e.tensor_scalar(RL[2], RL[1], 1.0 + EPSA, None,
                                          ALU.max), r=[RKL[1]], w=[RKL[2]])
            vop(lambda e: e.tensor_mul(RL[3], RL[2], RL[2]), r=[RKL[2]], w=[RKL[3]])
            vop(lambda e: e.tensor_scalar(RL[3], RL[3], -1.0, None, ALU.add),
                r=[RKL[3]], w=[RKL[3]])
            act(RL[3], RL[3], AF.Sqrt, r=[RKL[3]], w=[RKL[3]])
            vop(lambda e: e.tensor_add(RL[3], RL[3], RL[2]), r=[RKL[3], RKL[2]],
                w=[RKL[3]])
            act(RL[3], RL[3], AF.Ln, r=[RKL[3]], w=[RKL[3]])
            vop(lambda e: e.tensor_scalar(RL[3], RL[3], 0.001, None, ALU.max),
                r=[RKL[3]], w=[RKL[3]])
            yield
            vop(lambda e: e.tensor_scalar(RL[2], RL[1], -(1.0 + EPSA), 0.0,
                                          ALU.add, ALU.min), r=[RKL[1]], w=[RKL[2]])
            vop(lambda e: e.tensor_mul(RL[1], RL[2], RL[2]), r=[RKL[2]], w=[RKL[1]])
            vop(lambda e: e.tensor_sub(RL[0], RL[0], RL[1]), r=[RKL[0], RKL[1]],
                w=[RKL[0]])
            vop(lambda e: e.tensor_scalar(RL[0], RL[0], EPSA, None, ALU.max),
                r=[RKL[0]], w=[RKL[0]])
            act(RL[0], RL[0], AF.Sqrt, r=[RKL[0]], w=[RKL[0]])
            vop(lambda e: e.tensor_scalar(RL[0], RL[0], EPSA, None, ALU.add),
                r=[RKL[0]], w=[RKL[0]])
            vop(lambda e: e.reciprocal(RL[0], RL[0]), r=[RKL[0]], w=[RKL[0]])
            vop(lambda e: e.tensor_mul(RL[0], RL[0], RL[3]), r=[RKL[0], RKL[3]],
                w=[RKL[0]])
            vop(lambda e: e.tensor_mul(RL[2], RL[2], RL[0]), r=[RKL[2], RKL[0]],
                w=[RKL[2]])
            yield
            pbs = (psL if par == 0 else psR)[:, 0:CH]
            pbsk = ("bcL0", "bcR0")[par]
            mm(pbs, onesrF[0:1, :], RL[0], st=True, sp=True,
               r=["onesrF", RKL[0]], w=[pbsk])
            for j in range(NTD):
                S.v(lambda e, j=j, c0=c0, pbs=pbs: e.tensor_mul(
                    arW2[:, 2048 * j + c0:2048 * j + c0 + CH], spb(j), pbs),
                    r=[spk(j), pbsk], w=[vk(j)])
            S.v(lambda e, c0=c0: e.tensor_copy(arW2[0:1, c0:c0 + CH], RL[2]),
                r=[RKL[2]], w=[vk(0)])


        pipe2([a1_chunk(c) for c in range(T // CH)], depth=1)

        S.fence(cells[0:1, 14:15])
        QpB = [arW2[:, 16384:18432], arQK2[:, 0:2048]]
        KpB = [arW2[:, 18432:20480], arQK2[:, 2048:4096]]
        Pb = [arW2[:, 20480 + 512 * i:20992 + 512 * i] for i in range(2)] + [
            arW2[:, 23040 + 512 * i:23552 + 512 * i] for i in range(2)]
        Vcm = arS[:, 0:2048]
        Ostg2 = [arS[:, 2048:2560], arS[:, 2560:3072]]
        OsK = ["Ostg", "OstgB"]
        sqp2 = [arW2[:, 21504 + 512 * i:22016 + 512 * i] for i in range(2)]
        x0c = arW2[0:2, 22528:23040]
        PP = [(psA, "psA"), (psB, "psB")]

        def proj_pair(p, wslot, dstap, dstkey, mode):
            S.prewait("tensor", [f"aW{wslot}"] + [vk(k) for k in range(NTD)])
            for nq in range(4):
                pp, pk = PP[nq % 2]
                for k in range(NTD):
                    mm(pp[:, 0:512], arW1[:, 8192 * wslot + 1024 * k + 128 * p:
                                          8192 * wslot + 1024 * k + 128 * p + 128],
                       arW2[:, 2048 * k + 512 * nq:2048 * k + 512 * nq + 512],
                       st=(k == 0), sp=(k == NTD - 1),
                       r=[f"aW{wslot}", vk(k)], w=[pk])
                if mode == "clip5":
                    S.v(lambda e, nq=nq, pp=pp: e.tensor_scalar(
                        dstap[:, 512 * nq:512 * nq + 512], pp[:, 0:512], -5.0,
                        5.0, ALU.max, ALU.min), r=[pk], w=[dstkey])
                else:
                    S.v(lambda e, nq=nq, pp=pp: e.tensor_copy(
                        dstap[:, 512 * nq:512 * nq + 512], pp[:, 0:512]),
                        r=[pk], w=[dstkey])

        def head_rows(p, srcap, srckey, negate, maxcell):
            for nq in range(4):
                sqp = sqp2[nq % 2]
                S.v(lambda e, nq=nq, sqp=sqp: e.tensor_mul(
                    sqp, srcap[:, 512 * nq:512 * nq + 512],
                    srcap[:, 512 * nq:512 * nq + 512]), r=[srckey],
                    w=[f"sqp{nq % 2}"])
                mm(psR[0:2, 0:512], mask2[:], sqp, st=True, sp=True,
                   r=["mask2H", f"sqp{nq % 2}"], w=["psR"])
                mm(psL[0:1, 0:512], maskP[:], sqp, st=True, sp=True,
                   r=["maskPH", f"sqp{nq % 2}"], w=["psL0"])
                vop(lambda e: e.reduce_max(CL(15), psL[0:1, 0:512], axis=AX),
                    r=["psL0"], w=[CK(15)])
                vop(lambda e, mc=maxcell: e.tensor_max(CL(mc), CL(mc), CL(15)),
                    r=[CK(15), CK(maxcell)], w=[CK(maxcell)])
                act(x0c, psR[0:2, 0:512], AF.Sqrt, bias=cst[0:2, 2:3],
                    r=["psR", "cst2"], w=["x0c"])
                if negate:
                    vop(lambda e: e.tensor_scalar(x0c, x0c, -1.0, None,
                                                  ALU.mult), r=["x0c"], w=["x0c"])
                mm(psR[:, 0:512], place2[:], x0c, st=True, sp=True,
                   r=["place2H", "x0c"], w=["psR"])
                S.v(lambda e, nq=nq: e.tensor_copy(
                    srcap[0:1, 512 * nq:512 * nq + 512], psR[0:1, 0:512]),
                    r=["psR"], w=[srckey])
                S.v(lambda e, nq=nq: e.tensor_copy(
                    srcap[64:65, 512 * nq:512 * nq + 512], psR[64:65, 0:512]),
                    r=["psR"], w=[srckey])

        def proj_nq(p, wslot, dstap, dstkey, mode, nq):
            pp, pk = PP[nq % 2]
            for k in range(NTD):
                mm(pp[:, 0:512], arW1[:, 8192 * wslot + 1024 * k + 128 * p:
                                      8192 * wslot + 1024 * k + 128 * p + 128],
                   arW2[:, 2048 * k + 512 * nq:2048 * k + 512 * nq + 512],
                   st=(k == 0), sp=(k == NTD - 1),
                   r=[f"aW{wslot}", vk(k)], w=[pk])
            if mode == "clip5":
                S.v(lambda e, nq=nq, pp=pp: e.tensor_scalar(
                    dstap[:, 512 * nq:512 * nq + 512], pp[:, 0:512], -5.0,
                    5.0, ALU.max, ALU.min), r=[pk], w=[dstkey])
            else:
                S.v(lambda e, nq=nq, pp=pp: e.tensor_copy(
                    dstap[:, 512 * nq:512 * nq + 512], pp[:, 0:512]),
                    r=[pk], w=[dstkey])

        def head_rows_nq(p, srcap, srckey, negate, maxcell, nq):
            sqp = sqp2[nq % 2]
            S.v(lambda e, nq=nq, sqp=sqp: e.tensor_mul(
                sqp, srcap[:, 512 * nq:512 * nq + 512],
                srcap[:, 512 * nq:512 * nq + 512]), r=[srckey],
                w=[f"sqp{nq % 2}"])
            mm(psA[0:2, 0:512], mask2[:], sqp, st=True, sp=True,
               r=["mask2H", f"sqp{nq % 2}"], w=["psA"])
            mm(psB[0:1, 0:512], maskP[:], sqp, st=True, sp=True,
               r=["maskPH", f"sqp{nq % 2}"], w=["psB"])
            vop(lambda e: e.reduce_max(CL(15), psB[0:1, 0:512], axis=AX),
                r=["psB"], w=[CK(15)])
            vop(lambda e, mc=maxcell: e.tensor_max(CL(mc), CL(mc), CL(15)),
                r=[CK(15), CK(maxcell)], w=[CK(maxcell)])
            act(x0c, psA[0:2, 0:512], AF.Sqrt, bias=cst[0:2, 2:3],
                r=["psA", "cst2"], w=["x0c"])
            if negate:
                vop(lambda e: e.tensor_scalar(x0c, x0c, -1.0, None,
                                              ALU.mult), r=["x0c"], w=["x0c"])
            mm(psA[:, 0:512], place2[:], x0c, st=True, sp=True,
               r=["place2H", "x0c"], w=["psA"])
            S.v(lambda e, nq=nq: e.tensor_copy(
                srcap[0:1, 512 * nq:512 * nq + 512], psA[0:1, 0:512]),
                r=["psA"], w=[srckey])
            S.v(lambda e, nq=nq: e.tensor_copy(
                srcap[64:65, 512 * nq:512 * nq + 512], psA[64:65, 0:512]),
                r=["psA"], w=[srckey])

        def prep_thunks(p):
            par = p % 2
            Qp, Kp = QpB[par], KpB[par]
            qk, kk = f"Qp{par}", f"Kp{par}"
            th = []
            th.append(lambda: (S.prewait("tensor", ["aW0", "aW1", "aW2"] +
                                         [vk(k) for k in range(NTD)]),
                               S.v(lambda e: e.memset(CL(0), 0.0), r=(), w=[CK(0)]),
                               S.v(lambda e: e.memset(CL(1), 0.0), r=(), w=[CK(1)])))
            for nq in range(4):
                th.append(lambda nq=nq: proj_nq(p, 0, Qp, qk, "clip5", nq))
            for nq in range(4):
                th.append(lambda nq=nq: head_rows_nq(p, Qp, qk, False, 0, nq))
            for nq in range(4):
                th.append(lambda nq=nq: proj_nq(p, 1, Kp, kk, "clip5", nq))
            for nq in range(4):
                th.append(lambda nq=nq: head_rows_nq(p, Kp, kk, True, 1, nq))

            def bounds():
                vop(lambda e: e.tensor_scalar(CL(0), CL(0), 2.0, 1.0 + EPSA,
                                              ALU.mult, ALU.add), r=[CK(0)], w=[CK(0)])
                vop(lambda e: e.tensor_scalar(CL(1), CL(1), 2.0, 1.0 + EPSA,
                                              ALU.mult, ALU.add), r=[CK(1)], w=[CK(1)])
                vop(lambda e: e.tensor_mul(CL(0), CL(0), CL(1)), r=[CK(0), CK(1)],
                    w=[CK(0)])
                act(CL(0), CL(0), AF.Sqrt, r=[CK(0)], w=[CK(0)])
                vop(lambda e: e.tensor_scalar(CL(0), CL(0), -0.125, None, ALU.mult),
                    r=[CK(0)], w=[CK(0)])
                mm(psA[:, 0:1], onesrF[0:1, :], CL(0), st=True, sp=True,
                   r=["onesrF", CK(0)], w=["psA"])
                S.v(lambda e, p=p: e.tensor_copy(biasx[:, p:p + 1], psA[:, 0:1]),
                    r=["psA"], w=[f"bx{p}"])
            th.append(bounds)
            for nq in range(4):
                th.append(lambda nq=nq: proj_nq(p, 2, Vcm, "Vcm", "copy", nq))

            def vtrans(tt):
                half = tt % 2
                S.t(lambda e, tt=tt, half=half: e.transpose(
                    psT[:, 1024 * half:1024 * half + 128],
                    Vcm[:, 128 * tt:128 * tt + 128], ident[:]),
                    r=["Vcm", "identH"], w=[f"psT{half}"])
                for hl in range(2):
                    h = 2 * p + hl
                    S.v(lambda e, tt=tt, half=half, hl=hl, h=h: e.tensor_copy(
                        arA[:, 1040 * tt + 65 * h:1040 * tt + 65 * h + 64],
                        psT[:, 1024 * half + 64 * hl:1024 * half + 64 * hl + 64]),
                        r=[f"psT{half}"], w=[f"V65:{tt}.{p}"])
            for tt in range(16):
                th.append(lambda tt=tt: vtrans(tt))
            return th

        def attention(p, prep_iter):
            par = p % 2
            Qp, Kp = QpB[par], KpB[par]
            qk, kk = f"Qp{par}", f"Kp{par}"
            AVb = [(psC[0:65, 0:512], "psC0"), (psC[0:65, 512:1024], "psC1"),
                   (psR[0:65, 0:512], "psR"), (psL[0:65, 0:512], "psL0")]
            SBb = [(psA, "psA"), (psB, "psB")]

            def hblock(hl):
                h = 2 * p + hl

                def emit_S(i, q):
                    pp, pk = SBb[q % 2]
                    mm(pp[0:128, 0:512],
                       Kp[64 * hl:64 * hl + 64, 128 * i:128 * i + 128],
                       Qp[64 * hl:64 * hl + 64, 512 * q:512 * q + 512],
                       st=True, sp=True, r=[kk, qk], w=[pk])

                def emit_E(i, q):
                    pp, pk = SBb[q % 2]
                    act(Pb[q], pp[0:128, 0:512], AF.Exp,
                        bias=biasx[:, p:p + 1], scale=0.125,
                        r=[pk, f"bx{p}"], w=[f"Pb{q}"])

                def emit_AV(i, q):
                    ap, ak = AVb[q]
                    mm(ap, arA[:, 1040 * i + 65 * h:1040 * i + 65 * h + 65],
                       Pb[q], st=(i == 0), sp=(i == 15),
                       r=[f"V65:{i}.{p}", f"Pb{q}"], w=[ak])

                for i in range(16):
                    for q in range(4):
                        emit_S(i, q)
                        emit_E(i, q)
                    if i >= 1:
                        for q in range(4):
                            emit_AV(i - 1, q)
                    t = next(prep_iter, None)
                    if t is not None:
                        t()
                for q in range(4):
                    emit_AV(15, q)
                for q in range(4):
                    ap, ak = AVb[q]
                    S.v(lambda e, ap=ap: e.tensor_copy(
                        arR2[64:65, 0:512], ap[64:65, 0:512]),
                        r=[ak], w=["dstg"])
                    S.dma("sync", den16[h:h + 1, 512 * q:512 * q + 512],
                          arR2[64:65, 0:512], r=["dstg"], w=["den16"])
                    S.v(lambda e, ap=ap, hl=hl: e.tensor_copy(
                        Ostg2[hl][0:64, 0:512], ap[0:64, 0:512]),
                        r=[ak], w=[OsK[hl]])
                    S.dma("sync", Osp[64 * h:64 * h + 64, 512 * q:512 * q + 512],
                          Ostg2[hl][0:64, 0:512], r=[OsK[hl]],
                          w=[f"Osp{h}"])
            return [lambda hl=hl: hblock(hl) for hl in range(2)]

        for fn in prep_thunks(0):
            fn()
        for p in range(8):
            pi = iter(prep_thunks(p + 1) if p < 7 else [])
            for qf in attention(p, pi):
                qf()
            for t in pi:
                t()

        S.fence(cells[0:1, 14:15])
        vop(lambda e: e.tensor_scalar(den16[:], den16[:], 1e-30, None, ALU.add),
            r=["den16"], w=["den16"])
        vop(lambda e: e.reciprocal(den16[:], den16[:]), r=["den16"], w=["den16"])

        for c in range(4):
            c0 = 512 * c
            for k in range(NTD):
                S.dma("sync", arS[:, 512 * k:512 * k + 512],
                      Osp[128 * k:128 * k + 128, c0:c0 + 512],
                      r=[f"Osp{2 * k}", f"Osp{2 * k + 1}", "fence_a3"],
                      w=[f"Ob{k}"])
                mm(psL[:, 0:512], selOF[:, 128 * k:128 * k + 128],
                   den16[0:16, c0:c0 + 512], st=True, sp=True,
                   r=["selHF", "den16"], w=["psL0"])
                S.v(lambda e, k=k: e.tensor_mul(arS[:, 512 * k:512 * k + 512],
                                                arS[:, 512 * k:512 * k + 512],
                                                psL[:, 0:512]),
                    r=[f"Ob{k}", "psL0"], w=[f"Ob{k}"])
            S.prewait("tensor", ["aW3"] + [f"Ob{k}" for k in range(NTD)])
            for m in range(NTD):
                pp, pk = (psA, "psA") if m % 2 == 0 else (psB, "psB")
                for k in range(NTD):
                    mm(pp[:, 0:512], arW1[:, 8192 * 3 + 1024 * k + 128 * m:
                                          8192 * 3 + 1024 * k + 128 * m + 128],
                       arS[:, 512 * k:512 * k + 512], st=(k == 0),
                       sp=(k == NTD - 1), r=["aW3", f"Ob{k}"], w=[pk])
                S.v(lambda e, m=m, pp=pp: e.tensor_copy(
                    arA[:, 512 * m:512 * m + 512], pp[:, 0:512]),
                    r=[pk, "fence_a3"], w=[f"o2{m}"])
            for m in range(NTD):
                sq = arW2[:, 20480 + 512 * (m % 2):20480 + 512 * (m % 2) + 512]
                S.v(lambda e, m=m, sq=sq: e.tensor_mul(
                    sq, arA[:, 512 * m:512 * m + 512], arA[:, 512 * m:512 * m + 512]),
                    r=[f"o2{m}"], w=[f"sqt{m % 2}"])
                mm(psR[0:1, 0:512], maskD[:] if m == 0 else onesb[:], sq,
                   st=(m == 0), sp=(m == NTD - 1),
                   r=["maskDH" if m == 0 else "onesb", f"sqt{m % 2}"], w=["psR"])
            vop(lambda e: e.tensor_scalar(RS[0], psR[0:1, 0:512], 0.0, None,
                                          ALU.max), r=["psR"], w=[RK[0]])
            act(RS[1], RS[0], AF.Sqrt, bias=cst[0:1, 3:4], r=[RK[0], "cst3"],
                w=[RK[1]])
            vop(lambda e: e.tensor_scalar(RS[1], RS[1], 5.0, None, ALU.min),
                r=[RK[1]], w=[RK[1]])
            act(RS[2], RS[1], AF.Exp, r=[RK[1]], w=[RK[2]])
            act(RS[3], RS[1], AF.Exp, scale=-1.0, r=[RK[1]], w=[RK[3]])
            vop(lambda e: e.tensor_sub(RS[2], RS[2], RS[3]), r=[RK[2], RK[3]],
                w=[RK[2]])
            vop(lambda e: e.tensor_scalar(RS[1], RS[1], EPSA, None, ALU.add),
                r=[RK[1]], w=[RK[1]])
            vop(lambda e: e.reciprocal(RS[1], RS[1]), r=[RK[1]], w=[RK[1]])
            vop(lambda e: e.tensor_mul(RS[2], RS[2], RS[1]), r=[RK[2], RK[1]],
                w=[RK[2]])
            vop(lambda e: e.tensor_scalar(RS[2], RS[2], 0.5, None, ALU.mult),
                r=[RK[2]], w=[RK[2]])
            mm(psL[:, 0:512], onesrF[0:1, :], RS[2], st=True, sp=True,
               r=["onesrF", RK[2]], w=["psL0"])
            for m in range(NTD):
                S.v(lambda e, m=m: e.tensor_mul(arA[:, 512 * m:512 * m + 512],
                                                arA[:, 512 * m:512 * m + 512],
                                                psL[:, 0:512]),
                    r=[f"o2{m}", "psL0"], w=[f"o2{m}"])
                S.v(lambda e, m=m: e.tensor_scalar(arA[:, 512 * m:512 * m + 512],
                                                   arA[:, 512 * m:512 * m + 512],
                                                   -8.0, 8.0, ALU.max, ALU.min),
                    r=[f"o2{m}"], w=[f"o2{m}"])
                sq = arW2[:, 20480 + 512 * (m % 2):20480 + 512 * (m % 2) + 512]
                S.v(lambda e, m=m, sq=sq: e.tensor_mul(
                    sq, arA[:, 512 * m:512 * m + 512], arA[:, 512 * m:512 * m + 512]),
                    r=[f"o2{m}"], w=[f"sqt{m % 2}"])
                mm(psR[0:1, 0:512], maskD[:] if m == 0 else onesb[:], sq,
                   st=(m == 0), sp=(m == NTD - 1),
                   r=["maskDH" if m == 0 else "onesb", f"sqt{m % 2}"], w=["psR"])
            act(RS[0], psR[0:1, 0:512], AF.Sqrt, bias=cst[0:1, 2:3],
                r=["psR", "cst2"], w=[RK[0]])
            for m in range(NTD):
                xb = arFS[:, 512 * (m % 4):512 * (m % 4) + 512]
                xk = f"A3x{m % 4}"
                S.dma("sync", xb, I["x_cm"][128 * m:128 * m + 128, c0:c0 + 512],
                      w=[xk])
                if m == 0:
                    S.v(lambda e, xb=xb: e.tensor_add(RS[1], xb[0:1, :], RS[0]),
                        r=[xk, RK[0]], w=[RK[1]])
                S.v(lambda e, m=m, xb=xb: e.tensor_add(
                    xb, xb, arA[:, 512 * m:512 * m + 512]),
                    r=[xk, f"o2{m}"], w=[xk])
                if m == 0:
                    S.v(lambda e, xb=xb: e.tensor_copy(xb[0:1, :], RS[1]),
                        r=[RK[1]], w=[xk])
                S.dma("sync", xres[128 * m:128 * m + 128, c0:c0 + 512], xb,
                      r=[xk], w=[f"xr{m}.{c}"])

    def sub_ffn(first, last):
        CH = 256
        wq2t = lambda k: (arW2[:, 1024 * k:1024 * k + 1024] if k < 24 else
                          arA[:, 1024 * (k - 24):1024 * (k - 24) + 1024])
        wq2k = lambda k: f"wq2:{k}"
        if first:
            acc1 = arR2[:, 768:769]
            acc2 = arR2[:, 769:770]
            tmp1 = arR2[:, 770:771]
            S.v(lambda e: e.memset(acc1, 0.0), w=["acc1"])
            S.v(lambda e: e.memset(acc2, 0.0), w=["acc2"])
            for k in range(NTD):
                for hh in range(2):
                    S.dma("sync", arFS[:, 0:2048],
                          I["W1T"][128 * k:128 * k + 128, 2048 * hh:2048 * hh + 2048],
                          w=["gstg"])
                    S.v(lambda e: e.reduce_sum(tmp1, arFS[:, 0:2048], axis=AX,
                                               apply_absolute_value=True),
                        r=["gstg"], w=["tmp1"])
                    S.v(lambda e: e.tensor_add(acc1, acc1, tmp1),
                        r=["acc1", "tmp1"], w=["acc1"])
            for k in range(NTF):
                S.dma("sync", arFS[:, 0:1024],
                      I["W2T"][128 * k:128 * k + 128, :], w=["gstg"])
                S.v(lambda e: e.reduce_sum(tmp1, arFS[:, 0:1024], axis=AX,
                                           apply_absolute_value=True),
                    r=["gstg"], w=["tmp1"])
                S.v(lambda e: e.tensor_add(acc2, acc2, tmp1),
                    r=["acc2", "tmp1"], w=["acc2"])
            for gi, (acck, col0) in enumerate((("acc1", 0), ("acc2", 2))):
                accap = arR2[:, 768 + gi:769 + gi]
                mm(psR[0:1, 0:1], onesbF[:], accap, st=True, sp=True,
                   r=["onesbF", acck], w=["psR"])
                S.v(lambda e: e.tensor_scalar(CL(2), psR[0:1, 0:1],
                                              1.0 / (DFF * D), EPSLN,
                                              ALU.mult, ALU.add),
                    r=["psR"], w=[CK(2)])
                S.v(lambda e: e.reciprocal(CL(2), CL(2)), r=[CK(2)], w=[CK(2)])
                S.v(lambda e: e.tensor_scalar(CL(3), CL(2), -1.0, None,
                                              ALU.mult), r=[CK(2)], w=[CK(3)])
                for dcol, ck in ((col0, CK(2)), (col0 + 1, CK(3))):
                    cap = cells[0:1, (2 if dcol in (col0,) else 3):
                                (3 if dcol in (col0,) else 4)]
                    mm(psR[:, 1:2], onesrF[0:1, :],
                       CL(2) if dcol == col0 else CL(3), st=True, sp=True,
                       r=["onesrF", ck], w=["psRb"])
                    S.v(lambda e, dcol=dcol: e.tensor_copy(
                        gamr[:, dcol:dcol + 1], psR[:, 1:2]),
                        r=["psRb"], w=[f"gamr{dcol}"])
            S.fence(cells[0:1, 14:15])
            for k in range(NTD):
                for hh in range(2):
                    S.dma("sync", arFS[:, 0:2048],
                          I["W1T"][128 * k:128 * k + 128, 2048 * hh:2048 * hh + 2048],
                          w=["qstg"])
                    S.v(lambda e: e.tensor_scalar(arS[:, 0:2048], arFS[:, 0:2048],
                                                  gamr[:, 0:1], 0.5, ALU.mult,
                                                  ALU.is_ge),
                        r=["qstg", "gamr0"], w=["cA"])
                    S.v(lambda e: e.tensor_scalar(arS[:, 2048:4096], arFS[:, 0:2048],
                                                  gamr[:, 1:2], 0.5, ALU.mult,
                                                  ALU.is_ge),
                        r=["qstg", "gamr1"], w=["cB"])
                    dst = arW1[:, 4096 * k + 2048 * hh:4096 * k + 2048 * hh + 2048]
                    S.v(lambda e, dst=dst: e.tensor_sub(dst, arS[:, 0:2048],
                                                        arS[:, 2048:4096]),
                        r=["cA", "cB"], w=[f"wq1:{k}"])
                    S.dma("sync", Wq1s[128 * k:128 * k + 128,
                                       2048 * hh:2048 * hh + 2048], dst,
                          r=[f"wq1:{k}"], w=["wq1s"])
            for k in range(NTF):
                S.dma("sync", arFS[:, 0:1024],
                      I["W2T"][128 * k:128 * k + 128, :], w=["qstg"])
                S.v(lambda e: e.tensor_scalar(arS[:, 0:1024], arFS[:, 0:1024],
                                              gamr[:, 2:3], 0.5, ALU.mult,
                                              ALU.is_ge),
                    r=["qstg", "gamr2"], w=["cA"])
                S.v(lambda e: e.tensor_scalar(arS[:, 1024:2048], arFS[:, 0:1024],
                                              gamr[:, 3:4], 0.5, ALU.mult,
                                              ALU.is_ge),
                    r=["qstg", "gamr3"], w=["cB"])
                S.v(lambda e, k=k: e.tensor_sub(wq2t(k), arS[:, 0:1024],
                                                arS[:, 1024:2048]),
                    r=["cA", "cB"], w=[wq2k(k)])
                S.dma("sync", Wq2s[128 * k:128 * k + 128, :], wq2t(k),
                      r=[wq2k(k)], w=["wq2s"])
        else:
            for k in range(NTD):
                S.dma("sync", arW1[:, 4096 * k:4096 * k + 4096],
                      Wq1s[128 * k:128 * k + 128, :], w=[f"wq1:{k}"])
            for k in range(NTF):
                S.dma("sync", wq2t(k), Wq2s[128 * k:128 * k + 128, :],
                      w=[wq2k(k)])
        S.fence(cells[0:1, 14:15])

        out_dram = y_cm if last else xres

        def ffn_chunk(c):
            c0 = c * CH
            par = c % 2
            xnb = lambda j, par=par: arS[:, 256 * (2 * j + par):256 * (2 * j + par) + CH]
            xnk = lambda j, par=par: f"fx{par}.{j}"
            yield from ln_bf("F", xres, c0, CH, xnb, xnk, par, clip8=False,
                  sqb=lambda i, par=par: sqS[:, 256 * (2 * par + i):
                                             256 * (2 * par + i) + 256])
            S.prewait("tensor", [f"wq1:{k}" for k in range(NTD)] +
                      [xnk(k) for k in range(NTD)])
            for m in range(NTF):
                pp, pk = (psA, "psA") if m % 2 == 0 else (psB, "psB")
                for k in range(NTD):
                    mm(pp[:, 0:CH], arW1[:, 4096 * k + 128 * m:4096 * k + 128 * m + 128],
                       xnb(k), st=(k == 0), sp=(k == NTD - 1),
                       r=[f"wq1:{k}", xnk(k)], w=[pk])
                act(arA[:, 8192 + 256 * m:8192 + 256 * m + CH], pp[:, 0:CH],
                    AF.Relu, bias=b18[:, m:m + 1], r=[pk, "b1v"], w=[f"fh{m}"])
            S.prewait("tensor", [wq2k(k) for k in range(NTF)] +
                      [f"fh{k}" for k in range(NTF)])
            for m in range(NTD):
                pp, pk = (psA, "psA") if m % 2 == 0 else (psB, "psB")
                for k in range(NTF):
                    mm(pp[:, 0:CH], wq2t(k)[:, 128 * m:128 * m + 128],
                       arA[:, 8192 + 256 * k:8192 + 256 * k + CH],
                       st=(k == 0), sp=(k == NTF - 1),
                       r=[wq2k(k), f"fh{k}"], w=[pk])
                fxo = arR2[:, 256 * par:256 * par + CH]
                fxn = arR2[:, 512 + 256 * par:512 + 256 * par + CH]
                S.dma("sync", fxo,
                      xres[128 * m:128 * m + 128, c0:c0 + CH], w=[f"fxo{par}"])
                S.v(lambda e, m=m, pp=pp, fxo=fxo, fxn=fxn: e.scalar_tensor_tensor(
                    fxn, pp[:, 0:CH], b28[:, m:m + 1],
                    fxo, ALU.add, ALU.add),
                    r=[pk, "b2v", f"fxo{par}"], w=[f"fxn{par}"])
                S.dma("sync", out_dram[128 * m:128 * m + 128, c0:c0 + CH],
                      fxn, r=[f"fxn{par}"],
                      w=[f"fout{m}.{c}"])

        pipe2([ffn_chunk(c) for c in range(T // CH)], depth=1)

    sub1()
    S.fence(cells[0:1, 14:15])
    if n_sub < 2:
        _copy_out(S, E, xres, y_cm)
        return
    def sub3():
        S.dma("gpsimd", arW1[:, 0:32768].rearrange("p (j o) -> p j o", o=2 * DI),
              I["WinT"].rearrange("(j p) o -> p j o", p=128), w=["mWin"])
        for j in range(NTI):
            S.v(lambda e, j=j: e.memset(arW2[0:128, 515 * j:515 * j + 3], 0.0),
                w=[f"xp{j}"])
        for c in range(4):
            c0 = 512 * c
            for half in range(2):
                cc0 = c0 + 256 * half
                xlb = lambda j: arA[:, 256 * j:256 * j + 256]
                xlk = lambda j: f"xlb{j}"
                ln_chunk("M1", xres, cc0, 256, xlb, xlk, clip8=False,
                         sqb=lambda i: arA[:, 4096 + 256 * i:4096 + 256 * i + 256])
                for j in range(NTD):
                    S.dma("sync", xlns[128 * j:128 * j + 128, cc0:cc0 + 256],
                          arFS[:, 256 * j:256 * j + 256],
                          r=[f"M1x{j}"], w=[f"xlns{j}"])
                R = [arRow[0:1, 512 * i:512 * i + 256] for i in range(4)]
                for j in range(NTD):
                    mm(psR[0:1, 0:256], onesb[:], xlb(j), st=(j == 0),
                       sp=(j == NTD - 1), r=["onesb", xlk(j)], w=["psR"])
                S.v(lambda e: e.tensor_scalar(R[0], psR[0:1, 0:256], 1.0 / D,
                                              None, ALU.mult), r=["psR"], w=["L2mu"])
                for j in range(NTD):
                    sq = arW2[:, 20480 + 256 * (j % 2):20480 + 256 * (j % 2) + 256]
                    S.v(lambda e, j=j, sq=sq: e.tensor_mul(sq, xlb(j), xlb(j)),
                        r=[xlk(j)], w=[f"sqt{j % 2}"])
                    mm(psA[0:1, 0:256], onesb[:], sq, st=(j == 0),
                       sp=(j == NTD - 1), r=["onesb", f"sqt{j % 2}"], w=["psA"])
                S.v(lambda e: e.tensor_scalar(R[1], psA[0:1, 0:256], 1.0 / D,
                                              None, ALU.mult), r=["psA"], w=["L2e2"])
                S.v(lambda e: e.tensor_mul(R[2], R[0], R[0]), r=["L2mu"], w=["L2m2"])
                S.v(lambda e: e.tensor_sub(R[1], R[1], R[2]), r=["L2e2", "L2m2"],
                    w=["L2v"])
                act(R[1], R[1], AF.Ln, bias=cst[0:1, 1:2], r=["L2v", "cst1"],
                    w=["L2v"])
                act(R[1], R[1], AF.Exp, scale=-0.5, r=["L2v"], w=["L2rs"])
                S.v(lambda e: e.tensor_mul(R[0], R[0], R[1]), r=["L2mu", "L2rs"],
                    w=["L2mr"])
                S.v(lambda e: e.tensor_scalar(R[0], R[0], -1.0, None, ALU.mult),
                    r=["L2mr"], w=["L2nm"])
                mm(psL[:, 0:256], onesrF[0:1, :], R[1], st=True, sp=True,
                   r=["onesrF", "L2rs"], w=["psL0"])
                mm(psL[:, 256:512], onesrF[0:1, :], R[0], st=True, sp=True,
                   r=["onesrF", "L2nm"], w=["psL0"])
                for j in range(NTD):
                    fs = arFS[:, 256 * j:256 * j + 256]
                    fk2 = f"M1x{j}"
                    S.v(lambda e, fs=fs: e.tensor_mul(fs, fs, psL[:, 0:256]),
                        r=[fk2, "psL0", f"xlns{j}"], w=[fk2])
                    S.v(lambda e, fs=fs: e.tensor_add(fs, fs, psL[:, 256:512]),
                        r=[fk2, "psL0"], w=[fk2])
                    dst = arS[:, 512 * j + 256 * half:512 * j + 256 * half + 256]
                    S.v(lambda e, fs=fs, dst=dst, j=j: e.tensor_scalar(
                        dst, fs, nw8[:, j:j + 1], nb8[:, j:j + 1], ALU.mult,
                        ALU.add), r=[fk2, "nwv", "nbv"], w=[f"xn2{j}"])
            S.prewait("tensor", ["mWin"] + [f"xn2{k}" for k in range(NTD)])
            for m in range(2 * NTI):
                pp, pk = (psA, "psA") if m % 2 == 0 else (psB, "psB")
                for k in range(NTD):
                    mm(pp[:, 0:512], arW1[:, 4096 * k + 128 * m:4096 * k + 128 * m + 128],
                       arS[:, 512 * k:512 * k + 512], st=(k == 0),
                       sp=(k == NTD - 1), r=["mWin", f"xn2{k}"], w=[pk])
                if m < NTI:
                    S.v(lambda e, m=m, pp=pp: e.tensor_copy(
                        arW2[:, 515 * m + 3:515 * m + 515], pp[:, 0:512]),
                        r=[pk], w=[f"xp{m}"])
                else:
                    mz = m - NTI
                    zb = arA[:, 2048 + 512 * (mz % 2):2048 + 512 * (mz % 2) + 512]
                    S.s(lambda e, pp=pp, zb=zb: e.activation(zb, pp[:, 0:512],
                                                             AF.Silu),
                        r=[pk], w=[f"zsl{mz % 2}"])
                    S.dma("sync", szs[128 * mz:128 * mz + 128, c0:c0 + 512], zb,
                          r=[f"zsl{mz % 2}"], w=["szs"])
            for j in range(NTI):
                acc = arR2[:, 0:512]
                S.v(lambda e, j=j, acc=acc: e.tensor_scalar(
                    acc, arW2[:, 515 * j:515 * j + 512], cw16[:, 4 * j:4 * j + 1],
                    cb16[:, j:j + 1], ALU.mult, ALU.add),
                    r=[f"xp{j}", "cw16", "convb"], w=["cacc"])
                for kk in range(1, 4):
                    S.v(lambda e, j=j, kk=kk, acc=acc: e.scalar_tensor_tensor(
                        acc, arW2[:, 515 * j + kk:515 * j + kk + 512],
                        cw16[:, 4 * j + kk:4 * j + kk + 1], acc, ALU.mult,
                        ALU.add), r=[f"xp{j}", "cw16", "cacc"], w=["cacc"])
                xct = arA[:, 3072 + 512 * (j % 2):3072 + 512 * (j % 2) + 512]
                S.s(lambda e, acc=acc, xct=xct: e.activation(xct, acc, AF.Silu),
                    r=["cacc"], w=[f"xcb{j % 2}"])
                S.dma("sync", xcs[128 * j:128 * j + 128, c0:c0 + 512], xct,
                      r=[f"xcb{j % 2}"], w=["xcs"])
                S.v(lambda e, j=j: e.tensor_copy(arW2[:, 515 * j:515 * j + 3],
                                                 arW2[:, 515 * j + 512:515 * j + 515]),
                    r=[f"xp{j}"], w=[f"xp{j}"])
        S.fence(cells[0:1, 14:15])
        S.dma("gpsimd", arW1[:, 0:32768].rearrange("p (j o) -> p j o", o=DI),
              I["WdtT"].rearrange("(j p) o -> p j o", p=128), w=["mWdt"])
        for c in range(4):
            c0 = 512 * c
            for k in range(NTI):
                S.dma("sync", arW2[:, 512 * k:512 * k + 512],
                      xcs[128 * k:128 * k + 128, c0:c0 + 512], w=[f"m2xc{k}"])
            S.prewait("tensor", ["mWdt"] + [f"m2xc{k}" for k in range(NTI)])
            for m in range(NTI):
                pp, pk = (psA, "psA") if m % 2 == 0 else (psB, "psB")
                for k in range(NTI):
                    mm(pp[:, 0:512], arW1[:, 2048 * k + 128 * m:2048 * k + 128 * m + 128],
                       arW2[:, 512 * k:512 * k + 512], st=(k == 0),
                       sp=(k == NTI - 1), r=["mWdt", f"m2xc{k}"], w=[pk])
                tb = arS[:, 512 * (m % 2):512 * (m % 2) + 512]
                S.s(lambda e, pp=pp, tb=tb, m=m: e.activation(
                    tb, pp[:, 0:512], AF.Exp, bias=bdt16[:, m:m + 1], scale=1.0),
                    r=[pk, "bdtv"], w=[f"m2e{m % 2}"])
                dtt = arS[:, 1024 + 512 * (m % 2):1024 + 512 * (m % 2) + 512]
                S.s(lambda e, tb=tb, dtt=dtt: e.activation(
                    dtt, tb, AF.Ln, bias=cst[:, 0:1], scale=1.0),
                    r=[f"m2e{m % 2}", "cst0"], w=[f"m2d{m % 2}"])
                S.dma("sync", dts[128 * m:128 * m + 128, c0:c0 + 512], dtt,
                      r=[f"m2d{m % 2}"], w=["dts"])
        S.fence(cells[0:1, 14:15])
        S.dma("gpsimd", arW1[:, 0:16384].rearrange("p (j o) -> p j o", o=D),
              I["WoutT"].rearrange("(j p) o -> p j o", p=128), w=["mWo"])
        for c in range(4):
            c0 = 512 * c
            for k in range(NTI):
                S.dma("sync", arW2[:, 512 * k:512 * k + 512],
                      xcs[128 * k:128 * k + 128, c0:c0 + 512], w=[f"m3xc{k}"])
                S.dma("sync", arW2[:, 8192 + 512 * k:8192 + 512 * k + 512],
                      dts[128 * k:128 * k + 128, c0:c0 + 512], w=[f"m3dt{k}"])
            S.prewait("tensor", ["wbc"] + [f"m3xc{k}" for k in range(NTI)])
            for bi, wcol in ((0, 0), (1, 256)):
                for k in range(NTI):
                    mm(psR[0:16, 0:512], wbc[:, wcol + 16 * k:wcol + 16 * k + 16],
                       arW2[:, 512 * k:512 * k + 512], st=(k == 0),
                       sp=(k == NTI - 1), r=["wbc", f"m3xc{k}"], w=["psR"])
                S.v(lambda e, bi=bi: e.tensor_copy(
                    brow[0:16, 512 * bi:512 * bi + 512], psR[0:16, 0:512]),
                    r=["psR"], w=[f"brow{bi}"])
            for n in range(16):
                for bi, dst0 in ((0, 0), (1, 8192)):
                    mm(psL[:, 0:512], sel16[:, 128 * n:128 * n + 128],
                       brow[0:16, 512 * bi:512 * bi + 512], st=True, sp=True,
                       r=["sel16H", f"brow{bi}"], w=["psL0"])
                    S.v(lambda e, n=n, dst0=dst0: e.tensor_copy(
                        arA[:, dst0 + 512 * n:dst0 + 512 * n + 512], psL[:, 0:512]),
                        r=["psL0"], w=[f"bc{dst0}:{n}"])
            LB = [dict(u=arS[:, 0:512],
                       a=[arS[:, 512:1024], arS[:, 1024:1536]],
                       b=arS[:, 1536:2048],
                       h=[arS[:, 2048:2560], arS[:, 2560:3072]],
                       sz=arS[:, 3072:3584],
                       y=arW2[:, 24064:24576]),
                  dict(u=arQK2[:, 0:512],
                       a=[arQK2[:, 512:1024], arQK2[:, 1024:1536]],
                       b=arQK2[:, 1536:2048],
                       h=[arQK2[:, 2048:2560], arQK2[:, 2560:3072]],
                       sz=arQK2[:, 3072:3584],
                       y=arQK2[:, 3584:4096])]

            def scan_u(j, L):
                B = LB[L]
                S.v(lambda e, j=j, B=B: e.tensor_mul(
                    B["u"], arW2[:, 8192 + 512 * j:8192 + 512 * j + 512],
                    arW2[:, 512 * j:512 * j + 512]),
                    r=[f"m3dt{j}", f"m3xc{j}"], w=[f"m3u{L}"])
                S.dma("sync", B["sz"], szs[128 * j:128 * j + 128, c0:c0 + 512],
                      w=[f"m3sz{L}"])

            def scan_n_front(j, n, L):
                B = LB[L]
                ab, hb, bb = B["a"][n % 2], B["h"][n % 2], B["b"]
                crk = f"cr{16 * j + n}"
                S.s(lambda e, j=j, n=n, ab=ab: e.activation(
                    ab, arW2[:, 8192 + 512 * j:8192 + 512 * j + 512], AF.Exp,
                    scale=negA[:, 16 * j + n:16 * j + n + 1]),
                    r=[f"m3dt{j}", "negA"], w=[f"m3a{L}{n % 2}"])
                S.v(lambda e, n=n, bb=bb, B=B: e.tensor_tensor(
                    bb, B["u"], arA[:, 512 * n:512 * n + 512], ALU.mult),
                    r=[f"m3u{L}", f"bc0:{n}"], w=[f"m3b{L}"])
                S.v(lambda e, j=j, n=n, ab=ab, bb=bb, hb=hb: e.tensor_tensor_scan(
                    hb, ab, bb, carry[:, 16 * j + n:16 * j + n + 1], ALU.mult,
                    ALU.add), r=[f"m3a{L}{n % 2}", f"m3b{L}", crk],
                    w=[f"m3h{L}{n % 2}"])

            def scan_n_back(j, n, L):
                B = LB[L]
                hb = B["h"][n % 2]
                crk = f"cr{16 * j + n}"
                S.s(lambda e, j=j, n=n, hb=hb: e.activation(
                    carry[:, 16 * j + n:16 * j + n + 1], hb[:, 511:512],
                    AF.Copy), r=[f"m3h{L}{n % 2}"], w=[crk])
                S.v(lambda e, n=n, hb=hb: e.tensor_mul(
                    hb, hb, arA[:, 8192 + 512 * n:8192 + 512 * n + 512]),
                    r=[f"m3h{L}{n % 2}", f"bc8192:{n}"], w=[f"m3h{L}{n % 2}"])
                yac = B["y"]
                if n == 0:
                    S.v(lambda e, hb=hb, yac=yac: e.tensor_copy(yac, hb),
                        r=[f"m3h{L}{n % 2}"], w=[f"m3y{L}"])
                else:
                    S.v(lambda e, hb=hb, yac=yac: e.tensor_add(yac, yac, hb),
                        r=[f"m3h{L}{n % 2}", f"m3y{L}"], w=[f"m3y{L}"])

            def scan_tail(j, L):
                B = LB[L]
                yac = B["y"]
                S.v(lambda e, j=j, yac=yac: e.scalar_tensor_tensor(
                    yac, arW2[:, 512 * j:512 * j + 512],
                    dd16[:, j:j + 1], yac, ALU.mult, ALU.add),
                    r=[f"m3xc{j}", "Dv", f"m3y{L}"], w=[f"m3y{L}"])
                S.v(lambda e, j=j, B=B, yac=yac: e.tensor_mul(
                    arW2[:, 16384 + 512 * j:16384 + 512 * j + 512], yac,
                    B["sz"]), r=[f"m3y{L}", f"m3sz{L}"], w=[f"y3{j}"])

            for jp in range(NTI // 2):
                jA, jB = 2 * jp, 2 * jp + 1
                scan_u(jA, 0)
                scan_u(jB, 1)
                for n in range(16):
                    scan_n_front(jA, n, 0)
                    scan_n_front(jB, n, 1)
                    scan_n_back(jA, n, 0)
                    scan_n_back(jB, n, 1)
                scan_tail(jA, 0)
                scan_tail(jB, 1)
            S.prewait("tensor", ["mWo"] + [f"y3{k}" for k in range(NTI)])
            for m in range(NTD):
                pp, pk = (psA, "psA") if m % 2 == 0 else (psB, "psB")
                for k in range(NTI):
                    mm(pp[:, 0:512], arW1[:, 1024 * k + 128 * m:1024 * k + 128 * m + 128],
                       arW2[:, 16384 + 512 * k:16384 + 512 * k + 512],
                       st=(k == 0), sp=(k == NTI - 1),
                       r=["mWo", f"y3{k}"], w=[pk])
                S.dma("sync", arR2[:, 0:512],
                      xres[128 * m:128 * m + 128, c0:c0 + 512], w=["m3x"])
                S.dma("sync", arR2[:, 512:1024],
                      xlns[128 * m:128 * m + 128, c0:c0 + 512], w=["m3xl"])
                S.v(lambda e, pp=pp: e.tensor_add(arR2[:, 0:512], pp[:, 0:512],
                                                  arR2[:, 0:512]),
                    r=[pk, "m3x"], w=["m3x"])
                S.v(lambda e: e.tensor_add(arR2[:, 0:512], arR2[:, 0:512],
                                           arR2[:, 512:1024]),
                    r=["m3x", "m3xl"], w=["m3x"])
                S.dma("sync", xres[128 * m:128 * m + 128, c0:c0 + 512],
                      arR2[:, 0:512], r=["m3x"], w=[f"m3out{m}.{c}"])

    sub_ffn(True, last=False)
    S.fence(cells[0:1, 14:15])
    if n_sub < 3:
        _copy_out(S, E, xres, y_cm)
        return
    sub3()
    S.fence(cells[0:1, 14:15])
    if n_sub < 4:
        _copy_out(S, E, xres, y_cm)
        return
    sub_ffn(False, last=True)


def _copy_out(S, E, src, y_cm):
    arFS = E["arFS"]
    for c in range(8):
        c0 = 256 * c
        for m in range(NTD):
            S.dma("sync", arFS[:, 256 * m:256 * m + 256],
                  src[128 * m:128 * m + 128, c0:c0 + 256], w=[f"co{m}"])
            S.dma("sync", y_cm[128 * m:128 * m + 128, c0:c0 + 256],
                  arFS[:, 256 * m:256 * m + 256], r=[f"co{m}"],
                  w=[f"yo{m}.{c}"])


def _host_inputs(ins):
    import ml_dtypes
    bf = ml_dtypes.bfloat16
    tr = lambda a: np.ascontiguousarray(np.asarray(a).T)
    col = lambda a: np.ascontiguousarray(np.asarray(a).reshape(-1, 1))
    selH = np.zeros((16, NTD * 128), dtype=np.float32)
    for k in range(NTD):
        selH[2 * k, 128 * k:128 * k + 64] = 1.0
        selH[2 * k + 1, 128 * k + 64:128 * k + 128] = 1.0
    sel16H = np.zeros((16, 16 * 128), dtype=np.float32)
    for n in range(16):
        sel16H[n, 128 * n:128 * n + 128] = 1.0
    place2H = np.zeros((2, 128), dtype=np.float32)
    place2H[0, 0] = 1.0
    place2H[1, 64] = 1.0
    mask2H = np.zeros((128, 2), dtype=np.float32)
    mask2H[1:64, 0] = 1.0
    mask2H[65:128, 1] = 1.0
    maskPH = (mask2H[:, 0] + mask2H[:, 1]).reshape(128, 1)
    maskDH = np.ones((128, 1), dtype=np.float32)
    maskDH[0, 0] = 0.0
    com = dict(
        WqT=tr(ins["L0_attn_a_Wq"]), WkT=tr(ins["L0_attn_a_Wk"]),
        WvT=tr(ins["L0_attn_a_Wv"]), WoT=tr(ins["L0_attn_a_Wo"]),
        W1T=tr(ins["L0_ffn_W1"]), W2T=tr(ins["L0_ffn_W2"]),
        WinT=tr(ins["L0_attn_b_W_in"]), WdtT=tr(ins["L0_attn_b_W_dt"]),
        WBT=tr(ins["L0_attn_b_W_B"]), WCT=tr(ins["L0_attn_b_W_C"]),
        WoutT=tr(ins["L0_attn_b_W_out"]),
        b1v=col(ins["L0_ffn_b1"]), b2v=col(ins["L0_ffn_b2"]),
        nwv=col(ins["L0_attn_b_norm_w"]), nbv=col(ins["L0_attn_b_norm_b"]),
        convw=np.ascontiguousarray(np.asarray(ins["L0_attn_b_conv_w"])),
        convb=col(ins["L0_attn_b_conv_b"]), bdtv=col(ins["L0_attn_b_b_dt"]),
        Dv=col(ins["L0_attn_b_D"]),
        Alog=np.ascontiguousarray(np.asarray(ins["L0_attn_b_A_log"])),
        identH=np.eye(128, dtype=bf), place2H=place2H.astype(bf),
        selHF=np.ascontiguousarray(selH),
        mask2H=mask2H.astype(bf), maskPH=maskPH.astype(bf),
        maskDH=maskDH.astype(bf), selH=selH.astype(bf),
        sel16H=sel16H.astype(bf),
    )
    xs = np.asarray(ins["L0_attn_a_x"])
    in_maps = []
    for c in range(8):
        m = dict(com)
        m["x_cm"] = np.ascontiguousarray(xs[c].T)
        in_maps.append(m)
    return in_maps


N_SUB = 4


def kernel(**ins):
    in_maps = _host_inputs(ins)
    nc = build_program(n_sub=N_SUB)
    res = run_bass_kernel_spmd(nc, in_maps, core_ids=list(range(8)))
    out = np.stack([np.asarray(res.results[c]["y_cm"]).T for c in range(8)], 0)
    return np.ascontiguousarray(out.astype(np.float32))


# version 06
# speedup vs baseline: 1.0295x; 1.0295x over previous
import numpy as np

import concourse.bass as bass
import concourse.mybir as mybir
from concourse.bass_utils import run_bass_kernel_spmd

F32 = mybir.dt.float32
BF16 = mybir.dt.bfloat16
AF = mybir.ActivationFunctionType
ALU = mybir.AluOpType
AX = mybir.AxisListType.X

T = 2048
D = 1024
DI = 2048
DFF = 4096
NH = 16
DH = 64
NTD = D // 128
NTI = DI // 128
NTF = DFF // 128
EPSA = 1e-6
EPSLN = 1e-5

ENGINES = ("sync", "tensor", "vector", "scalar", "gpsimd")
_DMA_QUEUES = ("sync", "gpsimd")


DMA_POOL = {"sync": 18, "gpsimd": 8}


class Op:
    __slots__ = ("eng", "fn", "reads", "writes", "idx", "is_dma", "need_inc",
                 "cum", "deps", "sem_name", "pin", "epos")

    def __init__(self, eng, fn, reads, writes, idx, is_dma):
        self.eng = eng
        self.fn = fn
        self.reads = reads
        self.writes = writes
        self.idx = idx
        self.is_dma = is_dma
        self.need_inc = False
        self.cum = 0
        self.deps = []
        self.sem_name = None
        self.pin = False
        self.epos = 0


class Sched:

    MILE_EVERY = 12

    def __init__(self):
        self.ops = []
        self.last_w = {}
        self.readers = {}
        self.gfence = None
        self.ecount = {e: 0 for e in ENGINES}
        self.miles = {e: [] for e in ENGINES}
        self.last_dma = {}
        self.dcnt = {}
        self.last_real = {}
        self.barriers = []

    def prewait(self, eng, keys):
        return self._add(eng, lambda e: e.nop(), tuple(keys), (), False)

    def fence(self, fdum=None):
        keys = tuple(self.last_w.keys())
        fn = (lambda e: e.nop()) if fdum is None else (
            lambda e: e.memset(fdum, 0.0))
        op = self._add("vector", fn, (), keys, False)
        ded = {d.idx: d for d in op.deps}
        for d in self.last_dma.values():
            ded[d.idx] = d
        op.deps = [ded[i] for i in sorted(ded)]
        self.barriers.append(dict(idx=op.idx, last=dict(self.last_real),
                                  dmas=list(self.last_dma.values())))
        return op

    def _add(self, eng, fn, reads, writes, is_dma, _mile=False):
        op = Op(eng, fn, tuple(reads), tuple(writes), len(self.ops), is_dma)
        op.epos = self.ecount[eng]
        self.ecount[eng] += 1
        if not is_dma and op.epos % self.MILE_EVERY == self.MILE_EVERY - 1:
            self.miles[eng].append(op)
        deps = []
        for k in op.reads:
            w = self.last_w.get(k)
            if w is not None:
                deps.append(w)
        for k in op.writes:
            w = self.last_w.get(k)
            if w is not None:
                deps.append(w)
            deps.extend(self.readers.get(k, ()))

        ded = {}
        for d in deps:
            if d is op:
                continue
            if (d.eng == eng and not d.is_dma and not is_dma
                    and eng in ("tensor", "gpsimd")):
                continue
            ded[d.idx] = d
        op.deps = [ded[i] for i in sorted(ded)]
        for k in op.reads:
            self.readers.setdefault(k, []).append(op)
        for k in op.writes:
            self.last_w[k] = op
            self.readers[k] = [op]
        self.ops.append(op)
        if is_dma:
            self.dcnt[eng] = self.dcnt.get(eng, 0) + 1
            self.last_dma[(eng, (self.dcnt[eng] - 1) % DMA_POOL[eng])] = op
        else:
            self.last_real[eng] = op
        return op

    def t(self, fn, r=(), w=()):
        return self._add("tensor", fn, r, w, False)

    def v(self, fn, r=(), w=()):
        return self._add("vector", fn, r, w, False)

    def s(self, fn, r=(), w=()):
        return self._add("scalar", fn, r, w, False)

    def g(self, fn, r=(), w=()):
        return self._add("gpsimd", fn, r, w, False)

    def dma(self, queue, out, in_, r=(), w=(), slow=False):
        if slow:
            fn = lambda e, o=out, i=in_: e.dma_start(
                out=o, in_=i, allow_slow_non_contiguous=True)
        else:
            fn = lambda e, o=out, i=in_: e.dma_start(out=o, in_=i)
        return self._add(queue, fn, r, w, True)

    def finalize(self, sems):
        import bisect
        mpos = {e: [m.epos for m in self.miles[e]] for e in ENGINES}
        for op in self.ops:
            nd = {}
            for d in op.deps:
                if not d.is_dma:
                    i = bisect.bisect_right(mpos[d.eng], d.epos)
                    if i < len(self.miles[d.eng]):
                        m = self.miles[d.eng][i]
                        if m.idx < op.idx:
                            d = m
                nd[d.idx] = d
            op.deps = [nd[i] for i in sorted(nd)]
        for spec in self.barriers:
            for o in spec["last"].values():
                o.need_inc = True
                o.pin = True
        for op in self.ops:
            for d in op.deps:
                d.need_inc = True
                if op.eng != d.eng or op.is_dma or d.is_dma:
                    d.pin = True
        cums = {e: 0 for e in ENGINES}
        qidx = {q: 0 for q in _DMA_QUEUES}
        finals = {}
        for op in self.ops:
            if op.is_dma:
                qi = qidx[op.eng]
                qidx[op.eng] += 1
                P = DMA_POOL[op.eng]
                op.sem_name = f"dq_{op.eng}_{qi % P}"
                op.cum = 16 * (qi // P + 1)
                op.need_inc = True
                finals[op.sem_name] = op.cum
            elif op.need_inc:
                cums[op.eng] += 1
                op.cum = cums[op.eng]
        self.totals = cums
        bodies = {}
        for e in ENGINES:
            eops = [op for op in self.ops if op.eng == e]

            barr = list(self.barriers)

            def body(eng, eops=eops, e=e, barr=barr):
                seen = {}
                bi = 0
                for op in eops:
                    while bi < len(barr) and barr[bi]["idx"] < op.idx:
                        for e2, o in barr[bi]["last"].items():
                            if e2 == e or not o.need_inc:
                                continue
                            if seen.get(e2, -1) < o.cum:
                                eng.wait_ge(sems[e2], o.cum)
                                seen[e2] = o.cum
                        for d in barr[bi]["dmas"]:
                            if seen.get(d.sem_name, -1) < d.cum:
                                eng.wait_ge(sems[d.sem_name], d.cum)
                                seen[d.sem_name] = d.cum
                        bi += 1
                    if op.is_dma and op.cum > 16:
                        if seen.get(op.sem_name, -1) < op.cum - 16:
                            eng.wait_ge(sems[op.sem_name], op.cum - 16)
                            seen[op.sem_name] = op.cum - 16
                    for d in op.deps:
                        sname = d.sem_name if d.is_dma else d.eng
                        if seen.get(sname, -1) < d.cum:
                            eng.wait_ge(sems[sname], d.cum)
                            seen[sname] = d.cum
                    try:
                        inst = op.fn(eng)
                    except Exception:
                        print("FAILING OP idx", op.idx, "eng", op.eng,
                              "reads", op.reads, "writes", op.writes)
                        raise
                    if op.need_inc:
                        if op.is_dma:
                            inst.then_inc(sems[op.sem_name], 16)
                        else:
                            if inst is None:
                                inst = eng.nop()
                            inst.then_inc(sems[e], 1)
                            if op.pin:
                                eng.wait_ge(sems[e], op.cum)
                                if seen.get(e, -1) < op.cum:
                                    seen[e] = op.cum
                if e == "sync":
                    for sn, v in sorted(finals.items()):
                        if seen.get(sn, -1) < v:
                            eng.wait_ge(sems[sn], v)
            bodies[e] = body
        return bodies


def build_program(n_sub=4):
    nc = bass.Bass("TRN2", target_bir_lowering=False)

    def din(name, shape, dt=F32):
        return nc.dram_tensor(name, list(shape), dt, kind="ExternalInput").ap()

    I = {}
    for nm, shp in (("x_cm", (D, T)), ("WqT", (D, D)), ("WkT", (D, D)),
                    ("WvT", (D, D)), ("WoT", (D, D)), ("W1T", (D, DFF)),
                    ("W2T", (DFF, D)), ("WinT", (D, 2 * DI)), ("WdtT", (DI, DI)),
                    ("WBT", (DI, 16)), ("WCT", (DI, 16)), ("WoutT", (DI, D)),
                    ("b1v", (DFF, 1)), ("b2v", (D, 1)), ("nwv", (D, 1)),
                    ("nbv", (D, 1)), ("convw", (DI, 4)), ("convb", (DI, 1)),
                    ("bdtv", (DI, 1)), ("Dv", (DI, 1)), ("Alog", (DI, 16))):
        I[nm] = din(nm, shp)
    for nm, shp in (("identH", (128, 128)), ("place2H", (2, 128)),
                    ("mask2H", (128, 2)), ("maskPH", (128, 1)),
                    ("maskDH", (128, 1)), ("selH", (16, NTD * 128)),
                    ("sel16H", (16, 16 * 128))):
        I[nm] = din(nm, shp, BF16)
    I["selHF"] = din("selHF", (16, NTD * 128))

    y_cm = nc.dram_tensor("y_cm", [D, T], F32, kind="ExternalOutput").ap()

    xres = nc.dram_tensor("xres", [D, T], F32, kind="Internal").ap()
    Osp = nc.dram_tensor("Osp", [D, T], BF16, kind="Internal").ap()
    Wq1s = nc.dram_tensor("Wq1s", [D, DFF], BF16, kind="Internal").ap()
    Wq2s = nc.dram_tensor("Wq2s", [DFF, D], BF16, kind="Internal").ap()
    xcs = nc.dram_tensor("xcs", [DI, T], BF16, kind="Internal").ap()
    szs = nc.dram_tensor("szs", [DI, T], BF16, kind="Internal").ap()
    dts = nc.dram_tensor("dts", [DI, T], BF16, kind="Internal").ap()
    xlns = nc.dram_tensor("xlns", [D, T], F32, kind="Internal").ap()

    from contextlib import ExitStack
    st = ExitStack()
    with st:
        sb = lambda *a: st.enter_context(nc.sbuf_tensor(*a))
        ps_ = lambda *a: st.enter_context(nc.psum_tensor(*a))
        E = {}
        E["arW1"] = sb("arW1", [128, 32768], BF16)
        E["arW2"] = sb("arW2", [128, 24576], BF16)
        E["arA"] = sb("arA", [128, 16640], BF16)
        E["arS"] = sb("arS", [128, 4096], BF16)
        E["arFS"] = sb("arFS", [128, 2048], F32)
        E["arRow"] = sb("arRow", [128, 2048], F32)
        E["arR2"] = sb("arR2", [128, 1024], F32)
        E["den16"] = sb("den16", [16, 2048], F32)
        E["arQK2"] = sb("arQK2", [128, 4096], BF16)
        E["sqS"] = sb("sqS", [128, 1024], BF16)
        E["cells"] = sb("cells", [128, 16], F32)
        E["cst"] = sb("cst", [128, 8], F32)
        E["onesb"] = sb("onesb", [128, 1], BF16)
        E["onesrF"] = sb("onesrF", [1, 128], F32)
        E["onesbF"] = sb("onesbF", [128, 1], F32)
        E["onesr"] = sb("onesr", [1, 128], BF16)
        E["ident"] = sb("ident", [128, 128], BF16)
        E["place2"] = sb("place2", [2, 128], BF16)
        E["mask2"] = sb("mask2", [128, 2], BF16)
        E["maskP"] = sb("maskP", [128, 1], BF16)
        E["maskD"] = sb("maskD", [128, 1], BF16)
        E["selO"] = sb("selO", [16, NTD * 128], BF16)
        E["selOF"] = sb("selOF", [16, NTD * 128], F32)
        E["sel16"] = sb("sel16", [16, 16 * 128], BF16)
        E["nw8"] = sb("nw8", [128, 8], F32)
        E["nb8"] = sb("nb8", [128, 8], F32)
        E["b18"] = sb("b18", [128, 32], F32)
        E["b28"] = sb("b28", [128, 8], F32)
        E["cw16"] = sb("cw16", [128, 64], F32)
        E["cb16"] = sb("cb16", [128, 16], F32)
        E["bdt16"] = sb("bdt16", [128, 16], F32)
        E["dd16"] = sb("dd16", [128, 16], F32)
        E["negA"] = sb("negA", [128, 256], F32)
        E["gamr"] = sb("gamr", [128, 4], F32)
        E["biasx"] = sb("biasx", [128, 8], F32)
        E["carry"] = sb("carry", [128, 256], F32)
        E["brow"] = sb("brow", [16, 1024], BF16)
        E["wbc"] = sb("wbc", [128, 512], BF16)

        E["psA"] = ps_("psA", [128, 512], F32)
        E["psB"] = ps_("psB", [128, 512], F32)
        E["psR"] = ps_("psR", [128, 512], F32)
        E["psL"] = ps_("psL", [128, 512], F32)
        E["psC"] = ps_("psC", [128, 1024], F32)
        E["psT"] = ps_("psT", [128, 2048], BF16)

        E.update(I=I, y_cm=y_cm, xres=xres, Osp=Osp, Wq1s=Wq1s, Wq2s=Wq2s,
                 xcs=xcs, szs=szs, dts=dts, xlns=xlns)

        sems = {}
        for e in ENGINES:
            sems[e] = st.enter_context(nc.semaphore("s_" + e))
        for q in _DMA_QUEUES:
            for i in range(DMA_POOL[q]):
                sems[f"dq_{q}_{i}"] = st.enter_context(
                    nc.semaphore(f"dq_{q}_{i}"))

        S = Sched()
        _emit(S, E, n_sub)
        bodies = S.finalize(sems)
        with nc.Block() as block:
            block.sync(bodies["sync"])
            block.tensor(bodies["tensor"])
            block.vector(bodies["vector"])
            block.scalar(bodies["scalar"])
            block.gpsimd(bodies["gpsimd"])
    return nc


def _emit(S, E, n_sub):
    I = E["I"]
    arW1, arW2, arA, arS, arFS = (E["arW1"], E["arW2"], E["arA"], E["arS"],
                                  E["arFS"])
    arRow, arR2, den16, cells = (E["arRow"], E["arR2"], E["den16"],
                                 E["cells"])
    arQK2 = E["arQK2"]
    sqS = E["sqS"]
    cst, onesb, onesr, onesrF = E["cst"], E["onesb"], E["onesr"], E["onesrF"]
    onesbF = E["onesbF"]
    ident, place2, mask2, maskP, maskD = (E["ident"], E["place2"], E["mask2"],
                                          E["maskP"], E["maskD"])
    selO, sel16, selOF = E["selO"], E["sel16"], E["selOF"]
    nw8, nb8, b18, b28 = E["nw8"], E["nb8"], E["b18"], E["b28"]
    cw16, cb16, bdt16, dd16 = E["cw16"], E["cb16"], E["bdt16"], E["dd16"]
    negA, gamr, biasx, carry = E["negA"], E["gamr"], E["biasx"], E["carry"]
    brow, wbc = E["brow"], E["wbc"]
    psA, psB, psR, psL, psC, psT = (E["psA"], E["psB"], E["psR"], E["psL"],
                                    E["psC"], E["psT"])
    y_cm, xres, Osp = E["y_cm"], E["xres"], E["Osp"]
    Wq1s, Wq2s, xcs, szs, dts, xlns = (E["Wq1s"], E["Wq2s"], E["xcs"],
                                       E["szs"], E["dts"], E["xlns"])

    RS = [arRow[0:1, 512 * i:512 * (i + 1)] for i in range(4)]
    RK = [f"RS{i}" for i in range(4)]
    QR = [arR2[0:2, 512 * i:512 * (i + 1)] for i in range(2)]
    QK = [f"QR{i}" for i in range(2)]
    CL = lambda i: cells[0:1, i:i + 1]
    CK = lambda i: f"CL{i}"

    def mm(out, lhsT, rhs, *, st, sp, r, w):
        return S.t(lambda e, o=out, l=lhsT, rr=rhs, st_=st, sp_=sp:
                   e.matmul(o, l, rr, start=st_, stop=sp_),
                   r=tuple(r), w=tuple(w))

    def act(out, in_, func, *, r, w, bias=None, scale=1.0):
        if bias is None:
            fn = lambda e, o=out, i=in_, f=func, sc=scale: e.activation(
                o, i, f, scale=sc)
        else:
            fn = lambda e, o=out, i=in_, f=func, bb=bias, sc=scale: e.activation(
                o, i, f, bias=bb, scale=sc)
        return S.s(fn, r=tuple(r), w=tuple(w))

    def vop(fn, r, w):
        return S.v(fn, r=tuple(r), w=tuple(w))

    csts = {0: 1.0, 1: EPSLN, 2: 1.0 + EPSA, 3: EPSA, 4: 0.001, 5: 5.0,
            6: 0.125, 7: 1e-30}
    for i, vv in csts.items():
        S.g(lambda e, i=i, vv=vv: e.memset(cst[:, i:i + 1], vv), w=[f"cst{i}"])
    S.g(lambda e: e.memset(onesb[:], 1.0), w=["onesb"])
    S.g(lambda e: e.memset(onesr[:], 1.0), w=["onesr"])
    S.g(lambda e: e.memset(onesrF[:], 1.0), w=["onesrF"])
    S.g(lambda e: e.memset(onesbF[:], 1.0), w=["onesbF"])
    S.g(lambda e: e.memset(carry[:], 0.0),
        w=["carry"] + [f"cr{s}" for s in range(256)])

    for nm, dst in (("identH", ident), ("place2H", place2), ("mask2H", mask2),
                    ("maskPH", maskP), ("maskDH", maskD), ("selH", selO),
                    ("selHF", E["selOF"]), ("sel16H", sel16)):
        S.dma("sync", dst[:], I[nm], w=[nm])
    for nm, dst in (("nwv", nw8), ("nbv", nb8), ("b1v", b18), ("b2v", b28),
                    ("convb", cb16), ("bdtv", bdt16), ("Dv", dd16)):
        S.dma("sync", dst[:], I[nm].rearrange("(j p) o -> p (j o)", p=128),
              w=[nm], slow=True)
    S.dma("sync", cw16[:].rearrange("p (j k) -> p j k", k=4),
          I["convw"].rearrange("(j p) k -> p j k", p=128), w=["cw16"])
    S.dma("sync", negA[:].rearrange("p (j n) -> p j n", n=16),
          I["Alog"].rearrange("(j p) n -> p j n", p=128), w=["negA"])
    S.s(lambda e: e.activation(negA[:], negA[:], AF.Exp), r=["negA"], w=["negA"])
    S.v(lambda e: e.tensor_scalar(negA[:], negA[:], -1.0, None, ALU.mult),
        r=["negA"], w=["negA"])
    S.dma("gpsimd", wbc[:, 0:256].rearrange("p (j n) -> p j n", n=16),
          I["WBT"].rearrange("(j p) n -> p j n", p=128), w=["wbc"])
    S.dma("gpsimd", wbc[:, 256:512].rearrange("p (j n) -> p j n", n=16),
          I["WCT"].rearrange("(j p) n -> p j n", p=128), w=["wbc"])
    for wi, nm in enumerate(("WqT", "WkT", "WvT", "WoT")):
        S.dma("gpsimd", arW1[:, 8192 * wi:8192 * (wi + 1)].rearrange(
                  "p (j o) -> p j o", o=D),
              I[nm].rearrange("(j p) o -> p j o", p=128), w=[f"aW{wi}"])

    def ln_chunk(tag, src, c0, CH, out_bf, out_key, *, clip8, fkey=None,
                 sqb=None):
        fk = fkey or (lambda j: f"{tag}x{j}")
        if sqb is None:
            sqb = lambda i: arS[:, 256 + 512 * i:256 + 512 * i + CH]
        R = [arRow[0:1, 512 * i:512 * i + CH] for i in range(4)]
        for j in range(NTD):
            S.dma("sync", arFS[:, CH * j:CH * (j + 1)],
                  src[128 * j:128 * (j + 1), c0:c0 + CH], w=[fk(j)])
            S.dma("gpsimd", out_bf(j), src[128 * j:128 * (j + 1), c0:c0 + CH],
                  w=[out_key(j)])
        S.prewait("tensor", ["onesb"] + [out_key(j) for j in range(NTD)])
        for j in range(NTD):
            mm(psR[0:1, 0:CH], onesb[:], out_bf(j), st=(j == 0),
               sp=(j == NTD - 1), r=["onesb", out_key(j)], w=["psR"])
        vop(lambda e: e.tensor_scalar(R[0], psR[0:1, 0:CH], 1.0 / D, None,
                                      ALU.mult), r=["psR"], w=[RK[0]])
        for j in range(NTD):
            sq = sqb(j % 2)
            S.v(lambda e, j=j, sq=sq: e.tensor_mul(sq, out_bf(j), out_bf(j)),
                r=[out_key(j)], w=[f"sqt{j % 2}"])
            mm(psA[0:1, 0:CH], onesb[:], sq, st=(j == 0), sp=(j == NTD - 1),
               r=["onesb", f"sqt{j % 2}"], w=["psA"])
        vop(lambda e: e.tensor_scalar(R[1], psA[0:1, 0:CH], 1.0 / D, None,
                                      ALU.mult), r=["psA"], w=[RK[1]])
        vop(lambda e: e.tensor_mul(R[2], R[0], R[0]), r=[RK[0]], w=[RK[2]])
        vop(lambda e: e.tensor_sub(R[1], R[1], R[2]), r=[RK[1], RK[2]],
            w=[RK[1]])
        act(R[1], R[1], AF.Ln, bias=cst[0:1, 1:2], r=[RK[1], "cst1"], w=[RK[1]])
        act(R[1], R[1], AF.Exp, scale=-0.5, r=[RK[1]], w=[RK[1]])
        vop(lambda e: e.tensor_mul(R[0], R[0], R[1]), r=[RK[0], RK[1]],
            w=[RK[0]])
        vop(lambda e: e.tensor_scalar(R[0], R[0], -1.0, None, ALU.mult),
            r=[RK[0]], w=[RK[0]])
        mm(psL[:, 0:CH], onesrF[0:1, :], R[1], st=True, sp=True,
           r=["onesrF", RK[1]], w=["psL0"])
        mm(psL[:, CH:2 * CH], onesrF[0:1, :], R[0], st=True, sp=True,
           r=["onesrF", RK[0]], w=["psL0"])
        for j in range(NTD):
            fs = arFS[:, CH * j:CH * (j + 1)]
            S.v(lambda e, fs=fs: e.tensor_mul(fs, fs, psL[:, 0:CH]),
                r=[fk(j), "psL0"], w=[fk(j)])
            S.v(lambda e, fs=fs: e.tensor_add(fs, fs, psL[:, CH:2 * CH]),
                r=[fk(j), "psL0"], w=[fk(j)])
            if clip8:
                S.v(lambda e, j=j, fs=fs: e.tensor_scalar(out_bf(j), fs, -8.0,
                                                          8.0, ALU.max, ALU.min),
                    r=[fk(j)], w=[out_key(j)])
            else:
                S.v(lambda e, j=j, fs=fs: e.tensor_copy(out_bf(j), fs),
                    r=[fk(j)], w=[out_key(j)])

    def pipe2(gens, depth=2):
        from collections import deque
        act = deque()
        gens = list(gens)
        while gens or act:
            while len(act) < depth and gens:
                act.append(gens.pop(0))
            g = act.popleft()
            try:
                next(g)
                act.append(g)
            except StopIteration:
                pass

    def ln_bf(tag, src, c0, CH, xb, xbk, par, *, clip8, sqb):
        yield from _ln_bf(tag, src, c0, CH, xb, xbk, par, clip8, sqb)

    def _ln_bf(tag, src, c0, CH, xb, xbk, par, clip8, sqb):
        R = [arRow[0:1, 256 * (4 * par + i):256 * (4 * par + i) + CH]
             for i in range(4)]
        RKp = [f"row{par}{i}" for i in range(4)]
        pstat = psC[0:1, 512 * par:512 * par + CH]
        psk = f"psC{par}"
        for j in range(NTD):
            S.dma("gpsimd", xb(j), src[128 * j:128 * (j + 1), c0:c0 + CH],
                  w=[xbk(j)])
        yield
        S.prewait("tensor", ["onesb"] + [xbk(j) for j in range(NTD)])
        for j in range(NTD):
            mm(pstat, onesb[:], xb(j), st=(j == 0), sp=(j == NTD - 1),
               r=["onesb", xbk(j)], w=[psk])
        vop(lambda e: e.tensor_scalar(R[0], pstat, 1.0 / D, None, ALU.mult),
            r=[psk], w=[RKp[0]])
        yield
        for j in range(NTD):
            sq = sqb(j % 2)
            S.v(lambda e, j=j, sq=sq: e.tensor_mul(sq, xb(j), xb(j)),
                r=[xbk(j)], w=[f"{tag}sq{par}{j % 2}"])
            mm(pstat, onesb[:], sq, st=(j == 0), sp=(j == NTD - 1),
               r=["onesb", f"{tag}sq{par}{j % 2}"], w=[psk])
        vop(lambda e: e.tensor_scalar(R[1], pstat, 1.0 / D, None, ALU.mult),
            r=[psk], w=[RKp[1]])
        yield
        vop(lambda e: e.tensor_mul(R[2], R[0], R[0]), r=[RKp[0]], w=[RKp[2]])
        vop(lambda e: e.tensor_sub(R[1], R[1], R[2]), r=[RKp[1], RKp[2]],
            w=[RKp[1]])
        act(R[1], R[1], AF.Ln, bias=cst[0:1, 1:2], r=[RKp[1], "cst1"], w=[RKp[1]])
        act(R[1], R[1], AF.Exp, scale=-0.5, r=[RKp[1]], w=[RKp[1]])
        vop(lambda e: e.tensor_mul(R[0], R[0], R[1]), r=[RKp[0], RKp[1]],
            w=[RKp[0]])
        vop(lambda e: e.tensor_scalar(R[0], R[0], -1.0, None, ALU.mult),
            r=[RKp[0]], w=[RKp[0]])
        pbank = psL if par == 0 else psR
        pb1 = pbank[:, 0:CH]
        pb2 = pbank[:, 256:256 + CH]
        bk1 = ("bcL0", "bcR0")[par]
        bk2 = ("bcL1", "bcR1")[par]
        mm(pb1, onesrF[0:1, :], R[1], st=True, sp=True, r=["onesrF", RKp[1]],
           w=[bk1])
        mm(pb2, onesrF[0:1, :], R[0], st=True, sp=True, r=["onesrF", RKp[0]],
           w=[bk2])
        yield
        for j in range(NTD):
            S.v(lambda e, j=j, pb1=pb1: e.tensor_mul(xb(j), xb(j), pb1),
                r=[xbk(j), bk1], w=[xbk(j)])
            S.v(lambda e, j=j, pb2=pb2: e.tensor_add(xb(j), xb(j), pb2),
                r=[xbk(j), bk2], w=[xbk(j)])
            if clip8:
                S.v(lambda e, j=j: e.tensor_scalar(xb(j), xb(j), -8.0, 8.0,
                                                   ALU.max, ALU.min),
                    r=[xbk(j)], w=[xbk(j)])
            if j % 3 == 2:
                yield

    def sub1():
        CH = 256
        vk = lambda j: f"v{j}"

        for tt in range(16):
            ap = arA[:, 1040 * tt:1040 * tt + 1040].rearrange(
                "p (h c) -> p h c", c=65)[:, :, 64:65]
            S.g(lambda e, ap=ap: e.memset(ap, 1.0),
                w=[f"V65:{tt}.{pp}" for pp in range(8)])

        def a1_chunk(c):
            c0 = c * CH
            par = c % 2
            spb = lambda j, par=par: arS[:, 256 * (2 * j + par):256 * (2 * j + par) + CH]
            spk = lambda j, par=par: f"sp{par}.{j}"
            RL = [arRow[0:1, 256 * (4 * par + i):256 * (4 * par + i) + CH]
                  for i in range(4)]
            RKL = [f"row{par}{i}" for i in range(4)]
            yield from ln_bf("A1", I["x_cm"], c0, CH, spb, spk, par, clip8=True,
                  sqb=lambda i, par=par: arW2[:, 20480 + 256 * (2 * par + i):
                                              20480 + 256 * (2 * par + i) + CH])
            for j in range(NTD):
                sq = arW2[:, 20480 + 256 * (2 * par + j % 2):20480 + 256 * (2 * par + j % 2) + CH]
                S.v(lambda e, j=j, sq=sq: e.tensor_mul(sq, spb(j), spb(j)),
                    r=[spk(j)], w=[f"A1sq{par}{j % 2}"])
                mm(psC[0:1, 512 * par:512 * par + CH], maskD[:] if j == 0 else onesb[:], sq,
                   st=(j == 0), sp=(j == NTD - 1),
                   r=["maskDH" if j == 0 else "onesb", f"A1sq{par}{j % 2}"], w=[f"psC{par}"])
            yield
            vop(lambda e, par=par: e.tensor_copy(RL[0], psC[0:1, 512 * par:512 * par + CH]), r=[f"psC{par}"], w=[RKL[0]])
            act(RL[1], RL[0], AF.Ln, bias=cst[0:1, 2:3], r=[RKL[0], "cst2"],
                w=[RKL[1]])
            act(RL[1], RL[1], AF.Exp, scale=0.5, r=[RKL[1]], w=[RKL[1]])
            vop(lambda e: e.tensor_scalar(RL[2], RL[1], 1.0 + EPSA, None,
                                          ALU.max), r=[RKL[1]], w=[RKL[2]])
            vop(lambda e: e.tensor_mul(RL[3], RL[2], RL[2]), r=[RKL[2]], w=[RKL[3]])
            vop(lambda e: e.tensor_scalar(RL[3], RL[3], -1.0, None, ALU.add),
                r=[RKL[3]], w=[RKL[3]])
            act(RL[3], RL[3], AF.Ln, r=[RKL[3]], w=[RKL[3]])
            act(RL[3], RL[3], AF.Exp, scale=0.5, r=[RKL[3]], w=[RKL[3]])
            vop(lambda e: e.tensor_add(RL[3], RL[3], RL[2]), r=[RKL[3], RKL[2]],
                w=[RKL[3]])
            act(RL[3], RL[3], AF.Ln, r=[RKL[3]], w=[RKL[3]])
            vop(lambda e: e.tensor_scalar(RL[3], RL[3], 0.001, None, ALU.max),
                r=[RKL[3]], w=[RKL[3]])
            yield
            vop(lambda e: e.tensor_scalar(RL[2], RL[1], -(1.0 + EPSA), 0.0,
                                          ALU.add, ALU.min), r=[RKL[1]], w=[RKL[2]])
            vop(lambda e: e.tensor_mul(RL[1], RL[2], RL[2]), r=[RKL[2]], w=[RKL[1]])
            vop(lambda e: e.tensor_sub(RL[0], RL[0], RL[1]), r=[RKL[0], RKL[1]],
                w=[RKL[0]])
            vop(lambda e: e.tensor_scalar(RL[0], RL[0], EPSA, None, ALU.max),
                r=[RKL[0]], w=[RKL[0]])
            act(RL[0], RL[0], AF.Ln, r=[RKL[0]], w=[RKL[0]])
            act(RL[0], RL[0], AF.Exp, scale=0.5, r=[RKL[0]], w=[RKL[0]])
            vop(lambda e: e.tensor_scalar(RL[0], RL[0], EPSA, None, ALU.add),
                r=[RKL[0]], w=[RKL[0]])
            vop(lambda e: e.reciprocal(RL[0], RL[0]), r=[RKL[0]], w=[RKL[0]])
            vop(lambda e: e.tensor_mul(RL[0], RL[0], RL[3]), r=[RKL[0], RKL[3]],
                w=[RKL[0]])
            vop(lambda e: e.tensor_mul(RL[2], RL[2], RL[0]), r=[RKL[2], RKL[0]],
                w=[RKL[2]])
            yield
            pbs = (psL if par == 0 else psR)[:, 0:CH]
            pbsk = ("bcL0", "bcR0")[par]
            mm(pbs, onesrF[0:1, :], RL[0], st=True, sp=True,
               r=["onesrF", RKL[0]], w=[pbsk])
            for j in range(NTD):
                S.v(lambda e, j=j, c0=c0, pbs=pbs: e.tensor_mul(
                    arW2[:, 2048 * j + c0:2048 * j + c0 + CH], spb(j), pbs),
                    r=[spk(j), pbsk], w=[vk(j)])
            S.v(lambda e, c0=c0: e.tensor_copy(arW2[0:1, c0:c0 + CH], RL[2]),
                r=[RKL[2]], w=[vk(0)])


        pipe2([a1_chunk(c) for c in range(T // CH)], depth=1)

        S.fence(cells[0:1, 14:15])
        QpB = [arW2[:, 16384:18432], arQK2[:, 0:2048]]
        KpB = [arW2[:, 18432:20480], arQK2[:, 2048:4096]]
        Pb = [arW2[:, 20480 + 512 * i:20992 + 512 * i] for i in range(2)] + [
            arW2[:, 23040 + 512 * i:23552 + 512 * i] for i in range(2)]
        Vcm = arS[:, 0:2048]
        Ostg2 = [arS[:, 2048:2560], arS[:, 2560:3072]]
        OsK = ["Ostg", "OstgB"]
        sqp2 = [arW2[:, 21504 + 512 * i:22016 + 512 * i] for i in range(2)]
        x0c = arW2[0:2, 22528:23040]
        x0cs = [x0c, arW2[0:2, 24064:24576]]
        PP = [(psA, "psA"), (psB, "psB")]

        def proj_nq(p, wslot, dstap, dstkey, mode, nq):
            pp, pk = PP[nq % 2]
            for k in range(NTD):
                mm(pp[:, 0:512], arW1[:, 8192 * wslot + 1024 * k + 128 * p:
                                      8192 * wslot + 1024 * k + 128 * p + 128],
                   arW2[:, 2048 * k + 512 * nq:2048 * k + 512 * nq + 512],
                   st=(k == 0), sp=(k == NTD - 1),
                   r=[f"aW{wslot}", vk(k)], w=[pk])
            if mode == "clip5":
                S.v(lambda e, nq=nq, pp=pp: e.tensor_scalar(
                    dstap[:, 512 * nq:512 * nq + 512], pp[:, 0:512], -5.0,
                    5.0, ALU.max, ALU.min), r=[pk], w=[dstkey])
            else:
                S.v(lambda e, nq=nq, pp=pp: e.tensor_copy(
                    dstap[:, 512 * nq:512 * nq + 512], pp[:, 0:512]),
                    r=[pk], w=[dstkey])

        def head_rows_nq(p, srcap, srckey, negate, maxcell, nq):
            sqp = sqp2[nq % 2]
            xc = x0cs[nq % 2]
            xck = f"x0c{nq % 2}"
            S.v(lambda e, nq=nq, sqp=sqp: e.tensor_mul(
                sqp, srcap[:, 512 * nq:512 * nq + 512],
                srcap[:, 512 * nq:512 * nq + 512]), r=[srckey],
                w=[f"sqp{nq % 2}"])
            mm(psA[0:2, 0:512], mask2[:], sqp, st=True, sp=True,
               r=["mask2H", f"sqp{nq % 2}"], w=["psA"])
            mm(psB[0:1, 0:512], maskP[:], sqp, st=True, sp=True,
               r=["maskPH", f"sqp{nq % 2}"], w=["psB"])
            vop(lambda e: e.reduce_max(CL(15), psB[0:1, 0:512], axis=AX),
                r=["psB"], w=[CK(15)])
            vop(lambda e, mc=maxcell: e.tensor_max(CL(mc), CL(mc), CL(15)),
                r=[CK(15), CK(maxcell)], w=[CK(maxcell)])
            act(xc, psA[0:2, 0:512], AF.Ln, bias=cst[0:2, 2:3],
                r=["psA", "cst2"], w=[xck])
            act(xc, xc, AF.Exp, scale=0.5, r=[xck], w=[xck])
            if negate:
                vop(lambda e, xc=xc: e.tensor_scalar(xc, xc, -1.0, None,
                                                     ALU.mult), r=[xck], w=[xck])
            S.v(lambda e, nq=nq, xc=xc: e.tensor_copy(
                srcap[0:1, 512 * nq:512 * nq + 512], xc[0:1, :]),
                r=[xck], w=[srckey])
            S.dma("sync", srcap[64:65, 512 * nq:512 * nq + 512], xc[1:2, :],
                  r=[xck], w=[srckey], slow=True)

        def prep_thunks(p):
            par = p % 2
            Qp, Kp = QpB[par], KpB[par]
            qk, kk = f"Qp{par}", f"Kp{par}"
            th = []
            th.append(lambda: (S.prewait("tensor", ["aW0", "aW1", "aW2"] +
                                         [vk(k) for k in range(NTD)]),
                               S.v(lambda e: e.memset(CL(0), 0.0), r=(), w=[CK(0)]),
                               S.v(lambda e: e.memset(CL(1), 0.0), r=(), w=[CK(1)])))
            for nq in range(4):
                th.append(lambda nq=nq: proj_nq(p, 0, Qp, qk, "clip5", nq))
            for nq in range(4):
                th.append(lambda nq=nq: head_rows_nq(p, Qp, qk, False, 0, nq))
            for nq in range(4):
                th.append(lambda nq=nq: proj_nq(p, 1, Kp, kk, "clip5", nq))
            for nq in range(4):
                th.append(lambda nq=nq: head_rows_nq(p, Kp, kk, True, 1, nq))

            def bounds():
                vop(lambda e: e.tensor_scalar(CL(0), CL(0), 2.0, 1.0 + EPSA,
                                              ALU.mult, ALU.add), r=[CK(0)], w=[CK(0)])
                vop(lambda e: e.tensor_scalar(CL(1), CL(1), 2.0, 1.0 + EPSA,
                                              ALU.mult, ALU.add), r=[CK(1)], w=[CK(1)])
                vop(lambda e: e.tensor_mul(CL(0), CL(0), CL(1)), r=[CK(0), CK(1)],
                    w=[CK(0)])
                act(CL(0), CL(0), AF.Ln, bias=cst[0:1, 7:8],
                    r=[CK(0), "cst7"], w=[CK(0)])
                act(CL(0), CL(0), AF.Exp, scale=0.5, r=[CK(0)], w=[CK(0)])
                vop(lambda e: e.tensor_scalar(CL(0), CL(0), -0.125, None, ALU.mult),
                    r=[CK(0)], w=[CK(0)])
                mm(psA[:, 0:1], onesrF[0:1, :], CL(0), st=True, sp=True,
                   r=["onesrF", CK(0)], w=["psA"])
                S.v(lambda e, p=p: e.tensor_copy(biasx[:, p:p + 1], psA[:, 0:1]),
                    r=["psA"], w=[f"bx{p}"])
            th.append(bounds)
            for nq in range(4):
                th.append(lambda nq=nq: proj_nq(p, 2, Vcm, "Vcm", "copy", nq))

            def vtrans(tt):
                half = tt % 2
                S.t(lambda e, tt=tt, half=half: e.transpose(
                    psT[:, 1024 * half:1024 * half + 128],
                    Vcm[:, 128 * tt:128 * tt + 128], ident[:]),
                    r=["Vcm", "identH"], w=[f"psT{half}"])
                for hl in range(2):
                    h = 2 * p + hl
                    S.v(lambda e, tt=tt, half=half, hl=hl, h=h: e.tensor_copy(
                        arA[:, 1040 * tt + 65 * h:1040 * tt + 65 * h + 64],
                        psT[:, 1024 * half + 64 * hl:1024 * half + 64 * hl + 64]),
                        r=[f"psT{half}"], w=[f"V65:{tt}.{p}"])
            for tt in range(16):
                th.append(lambda tt=tt: vtrans(tt))
            return th

        def attention(p, prep_iter):
            par = p % 2
            Qp, Kp = QpB[par], KpB[par]
            qk, kk = f"Qp{par}", f"Kp{par}"
            AVb = [(psC[0:65, 0:512], "psC0"), (psC[0:65, 512:1024], "psC1"),
                   (psR[0:65, 0:512], "psR"), (psL[0:65, 0:512], "psL0")]
            SBb = [(psA, "psA"), (psB, "psB")]

            def hblock(hl):
                h = 2 * p + hl

                def emit_S(i, q):
                    pp, pk = SBb[q % 2]
                    mm(pp[0:128, 0:512],
                       Kp[64 * hl:64 * hl + 64, 128 * i:128 * i + 128],
                       Qp[64 * hl:64 * hl + 64, 512 * q:512 * q + 512],
                       st=True, sp=True, r=[kk, qk], w=[pk])

                def emit_E(i, q):
                    pp, pk = SBb[q % 2]
                    act(Pb[q], pp[0:128, 0:512], AF.Exp,
                        bias=biasx[:, p:p + 1], scale=0.125,
                        r=[pk, f"bx{p}"], w=[f"Pb{q}"])

                def emit_AV(i, q):
                    ap, ak = AVb[q]
                    mm(ap, arA[:, 1040 * i + 65 * h:1040 * i + 65 * h + 65],
                       Pb[q], st=(i == 0), sp=(i == 15),
                       r=[f"V65:{i}.{p}", f"Pb{q}"], w=[ak])

                for i in range(16):
                    for q in range(4):
                        emit_S(i, q)
                        emit_E(i, q)
                    if i >= 1:
                        for q in range(4):
                            emit_AV(i - 1, q)
                    t = next(prep_iter, None)
                    if t is not None:
                        t()
                for q in range(4):
                    emit_AV(15, q)
                for q in range(4):
                    ap, ak = AVb[q]
                    S.v(lambda e, ap=ap: e.tensor_copy(
                        arR2[64:65, 0:512], ap[64:65, 0:512]),
                        r=[ak], w=["dstg"])
                    S.dma("sync", den16[h:h + 1, 512 * q:512 * q + 512],
                          arR2[64:65, 0:512], r=["dstg"], w=["den16"])
                    S.v(lambda e, ap=ap, hl=hl: e.tensor_copy(
                        Ostg2[hl][0:64, 0:512], ap[0:64, 0:512]),
                        r=[ak], w=[OsK[hl]])
                    S.dma("sync", Osp[64 * h:64 * h + 64, 512 * q:512 * q + 512],
                          Ostg2[hl][0:64, 0:512], r=[OsK[hl]],
                          w=[f"Osp{h}"])
            return [lambda hl=hl: hblock(hl) for hl in range(2)]

        for fn in prep_thunks(0):
            fn()
        for p in range(8):
            pi = iter(prep_thunks(p + 1) if p < 7 else [])
            for qf in attention(p, pi):
                qf()
            for t in pi:
                t()

        S.fence(cells[0:1, 14:15])
        vop(lambda e: e.tensor_scalar(den16[:], den16[:], 1e-30, None, ALU.add),
            r=["den16"], w=["den16"])
        vop(lambda e: e.reciprocal(den16[:], den16[:]), r=["den16"], w=["den16"])

        for c in range(4):
            c0 = 512 * c
            for k in range(NTD):
                S.dma("sync", arS[:, 512 * k:512 * k + 512],
                      Osp[128 * k:128 * k + 128, c0:c0 + 512],
                      r=[f"Osp{2 * k}", f"Osp{2 * k + 1}", "fence_a3"],
                      w=[f"Ob{k}"])
                mm(psL[:, 0:512], selOF[:, 128 * k:128 * k + 128],
                   den16[0:16, c0:c0 + 512], st=True, sp=True,
                   r=["selHF", "den16"], w=["psL0"])
                S.v(lambda e, k=k: e.tensor_mul(arS[:, 512 * k:512 * k + 512],
                                                arS[:, 512 * k:512 * k + 512],
                                                psL[:, 0:512]),
                    r=[f"Ob{k}", "psL0"], w=[f"Ob{k}"])
            S.prewait("tensor", ["aW3"] + [f"Ob{k}" for k in range(NTD)])
            for m in range(NTD):
                pp, pk = (psA, "psA") if m % 2 == 0 else (psB, "psB")
                for k in range(NTD):
                    mm(pp[:, 0:512], arW1[:, 8192 * 3 + 1024 * k + 128 * m:
                                          8192 * 3 + 1024 * k + 128 * m + 128],
                       arS[:, 512 * k:512 * k + 512], st=(k == 0),
                       sp=(k == NTD - 1), r=["aW3", f"Ob{k}"], w=[pk])
                S.v(lambda e, m=m, pp=pp: e.tensor_copy(
                    arA[:, 512 * m:512 * m + 512], pp[:, 0:512]),
                    r=[pk, "fence_a3"], w=[f"o2{m}"])
            for m in range(NTD):
                sq = arW2[:, 20480 + 512 * (m % 2):20480 + 512 * (m % 2) + 512]
                S.v(lambda e, m=m, sq=sq: e.tensor_mul(
                    sq, arA[:, 512 * m:512 * m + 512], arA[:, 512 * m:512 * m + 512]),
                    r=[f"o2{m}"], w=[f"sqt{m % 2}"])
                mm(psR[0:1, 0:512], maskD[:] if m == 0 else onesb[:], sq,
                   st=(m == 0), sp=(m == NTD - 1),
                   r=["maskDH" if m == 0 else "onesb", f"sqt{m % 2}"], w=["psR"])
            vop(lambda e: e.tensor_scalar(RS[0], psR[0:1, 0:512], 0.0, None,
                                          ALU.max), r=["psR"], w=[RK[0]])
            act(RS[1], RS[0], AF.Ln, bias=cst[0:1, 3:4], r=[RK[0], "cst3"],
                w=[RK[1]])
            act(RS[1], RS[1], AF.Exp, scale=0.5, r=[RK[1]], w=[RK[1]])
            vop(lambda e: e.tensor_scalar(RS[1], RS[1], 5.0, None, ALU.min),
                r=[RK[1]], w=[RK[1]])
            act(RS[2], RS[1], AF.Exp, r=[RK[1]], w=[RK[2]])
            act(RS[3], RS[1], AF.Exp, scale=-1.0, r=[RK[1]], w=[RK[3]])
            vop(lambda e: e.tensor_sub(RS[2], RS[2], RS[3]), r=[RK[2], RK[3]],
                w=[RK[2]])
            vop(lambda e: e.tensor_scalar(RS[1], RS[1], EPSA, None, ALU.add),
                r=[RK[1]], w=[RK[1]])
            vop(lambda e: e.reciprocal(RS[1], RS[1]), r=[RK[1]], w=[RK[1]])
            vop(lambda e: e.tensor_mul(RS[2], RS[2], RS[1]), r=[RK[2], RK[1]],
                w=[RK[2]])
            vop(lambda e: e.tensor_scalar(RS[2], RS[2], 0.5, None, ALU.mult),
                r=[RK[2]], w=[RK[2]])
            mm(psL[:, 0:512], onesrF[0:1, :], RS[2], st=True, sp=True,
               r=["onesrF", RK[2]], w=["psL0"])
            for m in range(NTD):
                S.v(lambda e, m=m: e.tensor_mul(arA[:, 512 * m:512 * m + 512],
                                                arA[:, 512 * m:512 * m + 512],
                                                psL[:, 0:512]),
                    r=[f"o2{m}", "psL0"], w=[f"o2{m}"])
                S.v(lambda e, m=m: e.tensor_scalar(arA[:, 512 * m:512 * m + 512],
                                                   arA[:, 512 * m:512 * m + 512],
                                                   -8.0, 8.0, ALU.max, ALU.min),
                    r=[f"o2{m}"], w=[f"o2{m}"])
                sq = arW2[:, 20480 + 512 * (m % 2):20480 + 512 * (m % 2) + 512]
                S.v(lambda e, m=m, sq=sq: e.tensor_mul(
                    sq, arA[:, 512 * m:512 * m + 512], arA[:, 512 * m:512 * m + 512]),
                    r=[f"o2{m}"], w=[f"sqt{m % 2}"])
                mm(psR[0:1, 0:512], maskD[:] if m == 0 else onesb[:], sq,
                   st=(m == 0), sp=(m == NTD - 1),
                   r=["maskDH" if m == 0 else "onesb", f"sqt{m % 2}"], w=["psR"])
            act(RS[0], psR[0:1, 0:512], AF.Ln, bias=cst[0:1, 2:3],
                r=["psR", "cst2"], w=[RK[0]])
            act(RS[0], RS[0], AF.Exp, scale=0.5, r=[RK[0]], w=[RK[0]])
            for m in range(NTD):
                xb = arFS[:, 512 * (m % 4):512 * (m % 4) + 512]
                xk = f"A3x{m % 4}"
                S.dma("sync", xb, I["x_cm"][128 * m:128 * m + 128, c0:c0 + 512],
                      w=[xk])
                if m == 0:
                    S.v(lambda e, xb=xb: e.tensor_add(RS[1], xb[0:1, :], RS[0]),
                        r=[xk, RK[0]], w=[RK[1]])
                S.v(lambda e, m=m, xb=xb: e.tensor_add(
                    xb, xb, arA[:, 512 * m:512 * m + 512]),
                    r=[xk, f"o2{m}"], w=[xk])
                if m == 0:
                    S.v(lambda e, xb=xb: e.tensor_copy(xb[0:1, :], RS[1]),
                        r=[RK[1]], w=[xk])
                S.dma("sync", xres[128 * m:128 * m + 128, c0:c0 + 512], xb,
                      r=[xk], w=[f"xr{m}.{c}"])

    def sub_ffn(first, last):
        CH = 256
        wq2t = lambda k: (arW2[:, 1024 * k:1024 * k + 1024] if k < 24 else
                          arA[:, 1024 * (k - 24):1024 * (k - 24) + 1024])
        wq2k = lambda k: f"wq2:{k}"
        if first:
            acc1 = arR2[:, 768:769]
            acc2 = arR2[:, 769:770]
            tmp1 = arR2[:, 770:771]
            S.v(lambda e: e.memset(acc1, 0.0), w=["acc1"])
            S.v(lambda e: e.memset(acc2, 0.0), w=["acc2"])
            for k in range(NTD):
                for hh in range(2):
                    S.dma("sync", arFS[:, 0:2048],
                          I["W1T"][128 * k:128 * k + 128, 2048 * hh:2048 * hh + 2048],
                          w=["gstg"])
                    S.v(lambda e: e.reduce_sum(tmp1, arFS[:, 0:2048], axis=AX,
                                               apply_absolute_value=True),
                        r=["gstg"], w=["tmp1"])
                    S.v(lambda e: e.tensor_add(acc1, acc1, tmp1),
                        r=["acc1", "tmp1"], w=["acc1"])
            for k in range(NTF):
                S.dma("sync", arFS[:, 0:1024],
                      I["W2T"][128 * k:128 * k + 128, :], w=["gstg"])
                S.v(lambda e: e.reduce_sum(tmp1, arFS[:, 0:1024], axis=AX,
                                           apply_absolute_value=True),
                    r=["gstg"], w=["tmp1"])
                S.v(lambda e: e.tensor_add(acc2, acc2, tmp1),
                    r=["acc2", "tmp1"], w=["acc2"])
            for gi, (acck, col0) in enumerate((("acc1", 0), ("acc2", 2))):
                accap = arR2[:, 768 + gi:769 + gi]
                mm(psR[0:1, 0:1], onesbF[:], accap, st=True, sp=True,
                   r=["onesbF", acck], w=["psR"])
                S.v(lambda e: e.tensor_scalar(CL(2), psR[0:1, 0:1],
                                              1.0 / (DFF * D), EPSLN,
                                              ALU.mult, ALU.add),
                    r=["psR"], w=[CK(2)])
                S.v(lambda e: e.reciprocal(CL(2), CL(2)), r=[CK(2)], w=[CK(2)])
                S.v(lambda e: e.tensor_scalar(CL(3), CL(2), -1.0, None,
                                              ALU.mult), r=[CK(2)], w=[CK(3)])
                for dcol, ck in ((col0, CK(2)), (col0 + 1, CK(3))):
                    cap = cells[0:1, (2 if dcol in (col0,) else 3):
                                (3 if dcol in (col0,) else 4)]
                    mm(psR[:, 1:2], onesrF[0:1, :],
                       CL(2) if dcol == col0 else CL(3), st=True, sp=True,
                       r=["onesrF", ck], w=["psRb"])
                    S.v(lambda e, dcol=dcol: e.tensor_copy(
                        gamr[:, dcol:dcol + 1], psR[:, 1:2]),
                        r=["psRb"], w=[f"gamr{dcol}"])
            S.fence(cells[0:1, 14:15])
            for k in range(NTD):
                for hh in range(2):
                    S.dma("sync", arFS[:, 0:2048],
                          I["W1T"][128 * k:128 * k + 128, 2048 * hh:2048 * hh + 2048],
                          w=["qstg"])
                    S.v(lambda e: e.tensor_scalar(arS[:, 0:2048], arFS[:, 0:2048],
                                                  gamr[:, 0:1], 0.5, ALU.mult,
                                                  ALU.is_ge),
                        r=["qstg", "gamr0"], w=["cA"])
                    S.v(lambda e: e.tensor_scalar(arS[:, 2048:4096], arFS[:, 0:2048],
                                                  gamr[:, 1:2], 0.5, ALU.mult,
                                                  ALU.is_ge),
                        r=["qstg", "gamr1"], w=["cB"])
                    dst = arW1[:, 4096 * k + 2048 * hh:4096 * k + 2048 * hh + 2048]
                    S.v(lambda e, dst=dst: e.tensor_sub(dst, arS[:, 0:2048],
                                                        arS[:, 2048:4096]),
                        r=["cA", "cB"], w=[f"wq1:{k}"])
                    S.dma("sync", Wq1s[128 * k:128 * k + 128,
                                       2048 * hh:2048 * hh + 2048], dst,
                          r=[f"wq1:{k}"], w=["wq1s"])
            for k in range(NTF):
                S.dma("sync", arFS[:, 0:1024],
                      I["W2T"][128 * k:128 * k + 128, :], w=["qstg"])
                S.v(lambda e: e.tensor_scalar(arS[:, 0:1024], arFS[:, 0:1024],
                                              gamr[:, 2:3], 0.5, ALU.mult,
                                              ALU.is_ge),
                    r=["qstg", "gamr2"], w=["cA"])
                S.v(lambda e: e.tensor_scalar(arS[:, 1024:2048], arFS[:, 0:1024],
                                              gamr[:, 3:4], 0.5, ALU.mult,
                                              ALU.is_ge),
                    r=["qstg", "gamr3"], w=["cB"])
                S.v(lambda e, k=k: e.tensor_sub(wq2t(k), arS[:, 0:1024],
                                                arS[:, 1024:2048]),
                    r=["cA", "cB"], w=[wq2k(k)])
                S.dma("sync", Wq2s[128 * k:128 * k + 128, :], wq2t(k),
                      r=[wq2k(k)], w=["wq2s"])
        else:
            for k in range(NTD):
                S.dma("sync", arW1[:, 4096 * k:4096 * k + 4096],
                      Wq1s[128 * k:128 * k + 128, :], w=[f"wq1:{k}"])
            for k in range(NTF):
                S.dma("sync", wq2t(k), Wq2s[128 * k:128 * k + 128, :],
                      w=[wq2k(k)])
        S.fence(cells[0:1, 14:15])

        out_dram = y_cm if last else xres

        def ffn_chunk(c):
            c0 = c * CH
            par = c % 2
            xnb = lambda j, par=par: arS[:, 256 * (2 * j + par):256 * (2 * j + par) + CH]
            xnk = lambda j, par=par: f"fx{par}.{j}"
            yield from ln_bf("F", xres, c0, CH, xnb, xnk, par, clip8=False,
                  sqb=lambda i, par=par: sqS[:, 256 * (2 * par + i):
                                             256 * (2 * par + i) + 256])
            S.prewait("tensor", [f"wq1:{k}" for k in range(NTD)] +
                      [xnk(k) for k in range(NTD)])
            for m in range(NTF):
                pp, pk = (psA, "psA") if m % 2 == 0 else (psB, "psB")
                for k in range(NTD):
                    mm(pp[:, 0:CH], arW1[:, 4096 * k + 128 * m:4096 * k + 128 * m + 128],
                       xnb(k), st=(k == 0), sp=(k == NTD - 1),
                       r=[f"wq1:{k}", xnk(k)], w=[pk])
                act(arA[:, 8192 + 256 * m:8192 + 256 * m + CH], pp[:, 0:CH],
                    AF.Relu, bias=b18[:, m:m + 1], r=[pk, "b1v"], w=[f"fh{m}"])
            S.prewait("tensor", [wq2k(k) for k in range(NTF)] +
                      [f"fh{k}" for k in range(NTF)])
            for m in range(NTD):
                pp, pk = (psA, "psA") if m % 2 == 0 else (psB, "psB")
                for k in range(NTF):
                    mm(pp[:, 0:CH], wq2t(k)[:, 128 * m:128 * m + 128],
                       arA[:, 8192 + 256 * k:8192 + 256 * k + CH],
                       st=(k == 0), sp=(k == NTF - 1),
                       r=[wq2k(k), f"fh{k}"], w=[pk])
                fxo = arR2[:, 256 * par:256 * par + CH]
                fxn = arR2[:, 512 + 256 * par:512 + 256 * par + CH]
                S.dma("sync", fxo,
                      xres[128 * m:128 * m + 128, c0:c0 + CH], w=[f"fxo{par}"])
                S.v(lambda e, m=m, pp=pp, fxo=fxo, fxn=fxn: e.scalar_tensor_tensor(
                    fxn, pp[:, 0:CH], b28[:, m:m + 1],
                    fxo, ALU.add, ALU.add),
                    r=[pk, "b2v", f"fxo{par}"], w=[f"fxn{par}"])
                S.dma("sync", out_dram[128 * m:128 * m + 128, c0:c0 + CH],
                      fxn, r=[f"fxn{par}"],
                      w=[f"fout{m}.{c}"])

        pipe2([ffn_chunk(c) for c in range(T // CH)], depth=1)

    sub1()
    S.fence(cells[0:1, 14:15])
    if n_sub < 2:
        _copy_out(S, E, xres, y_cm)
        return
    def sub3():
        S.dma("gpsimd", arW1[:, 0:32768].rearrange("p (j o) -> p j o", o=2 * DI),
              I["WinT"].rearrange("(j p) o -> p j o", p=128), w=["mWin"])
        for j in range(NTI):
            S.v(lambda e, j=j: e.memset(arW2[0:128, 515 * j:515 * j + 3], 0.0),
                w=[f"xp{j}"])
        for c in range(4):
            c0 = 512 * c
            for half in range(2):
                cc0 = c0 + 256 * half
                xlb = lambda j: arA[:, 256 * j:256 * j + 256]
                xlk = lambda j: f"xlb{j}"
                ln_chunk("M1", xres, cc0, 256, xlb, xlk, clip8=False,
                         sqb=lambda i: arA[:, 4096 + 256 * i:4096 + 256 * i + 256])
                for j in range(NTD):
                    S.dma("sync", xlns[128 * j:128 * j + 128, cc0:cc0 + 256],
                          arFS[:, 256 * j:256 * j + 256],
                          r=[f"M1x{j}"], w=[f"xlns{j}"])
                R = [arRow[0:1, 512 * i:512 * i + 256] for i in range(4)]
                for j in range(NTD):
                    mm(psR[0:1, 0:256], onesb[:], xlb(j), st=(j == 0),
                       sp=(j == NTD - 1), r=["onesb", xlk(j)], w=["psR"])
                S.v(lambda e: e.tensor_scalar(R[0], psR[0:1, 0:256], 1.0 / D,
                                              None, ALU.mult), r=["psR"], w=["L2mu"])
                for j in range(NTD):
                    sq = arW2[:, 20480 + 256 * (j % 2):20480 + 256 * (j % 2) + 256]
                    S.v(lambda e, j=j, sq=sq: e.tensor_mul(sq, xlb(j), xlb(j)),
                        r=[xlk(j)], w=[f"sqt{j % 2}"])
                    mm(psA[0:1, 0:256], onesb[:], sq, st=(j == 0),
                       sp=(j == NTD - 1), r=["onesb", f"sqt{j % 2}"], w=["psA"])
                S.v(lambda e: e.tensor_scalar(R[1], psA[0:1, 0:256], 1.0 / D,
                                              None, ALU.mult), r=["psA"], w=["L2e2"])
                S.v(lambda e: e.tensor_mul(R[2], R[0], R[0]), r=["L2mu"], w=["L2m2"])
                S.v(lambda e: e.tensor_sub(R[1], R[1], R[2]), r=["L2e2", "L2m2"],
                    w=["L2v"])
                act(R[1], R[1], AF.Ln, bias=cst[0:1, 1:2], r=["L2v", "cst1"],
                    w=["L2v"])
                act(R[1], R[1], AF.Exp, scale=-0.5, r=["L2v"], w=["L2rs"])
                S.v(lambda e: e.tensor_mul(R[0], R[0], R[1]), r=["L2mu", "L2rs"],
                    w=["L2mr"])
                S.v(lambda e: e.tensor_scalar(R[0], R[0], -1.0, None, ALU.mult),
                    r=["L2mr"], w=["L2nm"])
                mm(psL[:, 0:256], onesrF[0:1, :], R[1], st=True, sp=True,
                   r=["onesrF", "L2rs"], w=["psL0"])
                mm(psL[:, 256:512], onesrF[0:1, :], R[0], st=True, sp=True,
                   r=["onesrF", "L2nm"], w=["psL0"])
                for j in range(NTD):
                    fs = arFS[:, 256 * j:256 * j + 256]
                    fk2 = f"M1x{j}"
                    S.v(lambda e, fs=fs: e.tensor_mul(fs, fs, psL[:, 0:256]),
                        r=[fk2, "psL0", f"xlns{j}"], w=[fk2])
                    S.v(lambda e, fs=fs: e.tensor_add(fs, fs, psL[:, 256:512]),
                        r=[fk2, "psL0"], w=[fk2])
                    dst = arS[:, 512 * j + 256 * half:512 * j + 256 * half + 256]
                    S.v(lambda e, fs=fs, dst=dst, j=j: e.tensor_scalar(
                        dst, fs, nw8[:, j:j + 1], nb8[:, j:j + 1], ALU.mult,
                        ALU.add), r=[fk2, "nwv", "nbv"], w=[f"xn2{j}"])
            S.prewait("tensor", ["mWin"] + [f"xn2{k}" for k in range(NTD)])
            for m in range(2 * NTI):
                pp, pk = (psA, "psA") if m % 2 == 0 else (psB, "psB")
                for k in range(NTD):
                    mm(pp[:, 0:512], arW1[:, 4096 * k + 128 * m:4096 * k + 128 * m + 128],
                       arS[:, 512 * k:512 * k + 512], st=(k == 0),
                       sp=(k == NTD - 1), r=["mWin", f"xn2{k}"], w=[pk])
                if m < NTI:
                    S.v(lambda e, m=m, pp=pp: e.tensor_copy(
                        arW2[:, 515 * m + 3:515 * m + 515], pp[:, 0:512]),
                        r=[pk], w=[f"xp{m}"])
                else:
                    mz = m - NTI
                    zb = arA[:, 2048 + 512 * (mz % 2):2048 + 512 * (mz % 2) + 512]
                    S.s(lambda e, pp=pp, zb=zb: e.activation(zb, pp[:, 0:512],
                                                             AF.Silu),
                        r=[pk], w=[f"zsl{mz % 2}"])
                    S.dma("sync", szs[128 * mz:128 * mz + 128, c0:c0 + 512], zb,
                          r=[f"zsl{mz % 2}"], w=["szs"])
            for j in range(NTI):
                acc = arR2[:, 0:512]
                S.v(lambda e, j=j, acc=acc: e.tensor_scalar(
                    acc, arW2[:, 515 * j:515 * j + 512], cw16[:, 4 * j:4 * j + 1],
                    cb16[:, j:j + 1], ALU.mult, ALU.add),
                    r=[f"xp{j}", "cw16", "convb"], w=["cacc"])
                for kk in range(1, 4):
                    S.v(lambda e, j=j, kk=kk, acc=acc: e.scalar_tensor_tensor(
                        acc, arW2[:, 515 * j + kk:515 * j + kk + 512],
                        cw16[:, 4 * j + kk:4 * j + kk + 1], acc, ALU.mult,
                        ALU.add), r=[f"xp{j}", "cw16", "cacc"], w=["cacc"])
                xct = arA[:, 3072 + 512 * (j % 2):3072 + 512 * (j % 2) + 512]
                S.s(lambda e, acc=acc, xct=xct: e.activation(xct, acc, AF.Silu),
                    r=["cacc"], w=[f"xcb{j % 2}"])
                S.dma("sync", xcs[128 * j:128 * j + 128, c0:c0 + 512], xct,
                      r=[f"xcb{j % 2}"], w=["xcs"])
                S.v(lambda e, j=j: e.tensor_copy(arW2[:, 515 * j:515 * j + 3],
                                                 arW2[:, 515 * j + 512:515 * j + 515]),
                    r=[f"xp{j}"], w=[f"xp{j}"])
        S.fence(cells[0:1, 14:15])
        S.dma("gpsimd", arW1[:, 0:32768].rearrange("p (j o) -> p j o", o=DI),
              I["WdtT"].rearrange("(j p) o -> p j o", p=128), w=["mWdt"])
        for c in range(4):
            c0 = 512 * c
            for k in range(NTI):
                S.dma("sync", arW2[:, 512 * k:512 * k + 512],
                      xcs[128 * k:128 * k + 128, c0:c0 + 512], w=[f"m2xc{k}"])
            S.prewait("tensor", ["mWdt"] + [f"m2xc{k}" for k in range(NTI)])
            for m in range(NTI):
                pp, pk = (psA, "psA") if m % 2 == 0 else (psB, "psB")
                for k in range(NTI):
                    mm(pp[:, 0:512], arW1[:, 2048 * k + 128 * m:2048 * k + 128 * m + 128],
                       arW2[:, 512 * k:512 * k + 512], st=(k == 0),
                       sp=(k == NTI - 1), r=["mWdt", f"m2xc{k}"], w=[pk])
                tb = arS[:, 512 * (m % 2):512 * (m % 2) + 512]
                S.s(lambda e, pp=pp, tb=tb, m=m: e.activation(
                    tb, pp[:, 0:512], AF.Exp, bias=bdt16[:, m:m + 1], scale=1.0),
                    r=[pk, "bdtv"], w=[f"m2e{m % 2}"])
                dtt = arS[:, 1024 + 512 * (m % 2):1024 + 512 * (m % 2) + 512]
                S.s(lambda e, tb=tb, dtt=dtt: e.activation(
                    dtt, tb, AF.Ln, bias=cst[:, 0:1], scale=1.0),
                    r=[f"m2e{m % 2}", "cst0"], w=[f"m2d{m % 2}"])
                S.dma("sync", dts[128 * m:128 * m + 128, c0:c0 + 512], dtt,
                      r=[f"m2d{m % 2}"], w=["dts"])
        S.fence(cells[0:1, 14:15])
        S.dma("gpsimd", arW1[:, 0:16384].rearrange("p (j o) -> p j o", o=D),
              I["WoutT"].rearrange("(j p) o -> p j o", p=128), w=["mWo"])
        for c in range(4):
            c0 = 512 * c
            for k in range(NTI):
                S.dma("sync", arW2[:, 512 * k:512 * k + 512],
                      xcs[128 * k:128 * k + 128, c0:c0 + 512], w=[f"m3xc{k}"])
                S.dma("sync", arW2[:, 8192 + 512 * k:8192 + 512 * k + 512],
                      dts[128 * k:128 * k + 128, c0:c0 + 512], w=[f"m3dt{k}"])
            S.prewait("tensor", ["wbc"] + [f"m3xc{k}" for k in range(NTI)])
            for bi, wcol in ((0, 0), (1, 256)):
                for k in range(NTI):
                    mm(psR[0:16, 0:512], wbc[:, wcol + 16 * k:wcol + 16 * k + 16],
                       arW2[:, 512 * k:512 * k + 512], st=(k == 0),
                       sp=(k == NTI - 1), r=["wbc", f"m3xc{k}"], w=["psR"])
                S.v(lambda e, bi=bi: e.tensor_copy(
                    brow[0:16, 512 * bi:512 * bi + 512], psR[0:16, 0:512]),
                    r=["psR"], w=[f"brow{bi}"])
            for n in range(16):
                for bi, dst0 in ((0, 0), (1, 8192)):
                    mm(psL[:, 0:512], sel16[:, 128 * n:128 * n + 128],
                       brow[0:16, 512 * bi:512 * bi + 512], st=True, sp=True,
                       r=["sel16H", f"brow{bi}"], w=["psL0"])
                    S.v(lambda e, n=n, dst0=dst0: e.tensor_copy(
                        arA[:, dst0 + 512 * n:dst0 + 512 * n + 512], psL[:, 0:512]),
                        r=["psL0"], w=[f"bc{dst0}:{n}"])
            LB = [dict(u=arS[:, 0:512],
                       a=[arS[:, 512:1024], arS[:, 1024:1536]],
                       b=arS[:, 1536:2048],
                       h=[arS[:, 2048:2560], arS[:, 2560:3072]],
                       sz=arS[:, 3072:3584],
                       y=arW2[:, 24064:24576]),
                  dict(u=arQK2[:, 0:512],
                       a=[arQK2[:, 512:1024], arQK2[:, 1024:1536]],
                       b=arQK2[:, 1536:2048],
                       h=[arQK2[:, 2048:2560], arQK2[:, 2560:3072]],
                       sz=arQK2[:, 3072:3584],
                       y=arQK2[:, 3584:4096])]

            def scan_u(j, L):
                B = LB[L]
                S.v(lambda e, j=j, B=B: e.tensor_mul(
                    B["u"], arW2[:, 8192 + 512 * j:8192 + 512 * j + 512],
                    arW2[:, 512 * j:512 * j + 512]),
                    r=[f"m3dt{j}", f"m3xc{j}"], w=[f"m3u{L}"])
                S.dma("sync", B["sz"], szs[128 * j:128 * j + 128, c0:c0 + 512],
                      w=[f"m3sz{L}"])

            def scan_n_front(j, n, L):
                B = LB[L]
                ab, hb, bb = B["a"][n % 2], B["h"][n % 2], B["b"]
                crk = f"cr{16 * j + n}"
                S.s(lambda e, j=j, n=n, ab=ab: e.activation(
                    ab, arW2[:, 8192 + 512 * j:8192 + 512 * j + 512], AF.Exp,
                    scale=negA[:, 16 * j + n:16 * j + n + 1]),
                    r=[f"m3dt{j}", "negA"], w=[f"m3a{L}{n % 2}"])
                S.v(lambda e, n=n, bb=bb, B=B: e.tensor_tensor(
                    bb, B["u"], arA[:, 512 * n:512 * n + 512], ALU.mult),
                    r=[f"m3u{L}", f"bc0:{n}"], w=[f"m3b{L}"])
                S.v(lambda e, j=j, n=n, ab=ab, bb=bb, hb=hb: e.tensor_tensor_scan(
                    hb, ab, bb, carry[:, 16 * j + n:16 * j + n + 1], ALU.mult,
                    ALU.add), r=[f"m3a{L}{n % 2}", f"m3b{L}", crk],
                    w=[f"m3h{L}{n % 2}"])

            def scan_n_back(j, n, L):
                B = LB[L]
                hb = B["h"][n % 2]
                crk = f"cr{16 * j + n}"
                S.s(lambda e, j=j, n=n, hb=hb: e.activation(
                    carry[:, 16 * j + n:16 * j + n + 1], hb[:, 511:512],
                    AF.Copy), r=[f"m3h{L}{n % 2}"], w=[crk])
                S.v(lambda e, n=n, hb=hb: e.tensor_mul(
                    hb, hb, arA[:, 8192 + 512 * n:8192 + 512 * n + 512]),
                    r=[f"m3h{L}{n % 2}", f"bc8192:{n}"], w=[f"m3h{L}{n % 2}"])
                yac = B["y"]
                if n == 0:
                    S.v(lambda e, hb=hb, yac=yac: e.tensor_copy(yac, hb),
                        r=[f"m3h{L}{n % 2}"], w=[f"m3y{L}"])
                else:
                    S.v(lambda e, hb=hb, yac=yac: e.tensor_add(yac, yac, hb),
                        r=[f"m3h{L}{n % 2}", f"m3y{L}"], w=[f"m3y{L}"])

            def scan_tail(j, L):
                B = LB[L]
                yac = B["y"]
                S.v(lambda e, j=j, yac=yac: e.scalar_tensor_tensor(
                    yac, arW2[:, 512 * j:512 * j + 512],
                    dd16[:, j:j + 1], yac, ALU.mult, ALU.add),
                    r=[f"m3xc{j}", "Dv", f"m3y{L}"], w=[f"m3y{L}"])
                S.v(lambda e, j=j, B=B, yac=yac: e.tensor_mul(
                    arW2[:, 16384 + 512 * j:16384 + 512 * j + 512], yac,
                    B["sz"]), r=[f"m3y{L}", f"m3sz{L}"], w=[f"y3{j}"])

            for jp in range(NTI // 2):
                jA, jB = 2 * jp, 2 * jp + 1
                scan_u(jA, 0)
                scan_u(jB, 1)
                for n in range(16):
                    scan_n_front(jA, n, 0)
                    scan_n_front(jB, n, 1)
                    scan_n_back(jA, n, 0)
                    scan_n_back(jB, n, 1)
                scan_tail(jA, 0)
                scan_tail(jB, 1)
            S.prewait("tensor", ["mWo"] + [f"y3{k}" for k in range(NTI)])
            for m in range(NTD):
                pp, pk = (psA, "psA") if m % 2 == 0 else (psB, "psB")
                for k in range(NTI):
                    mm(pp[:, 0:512], arW1[:, 1024 * k + 128 * m:1024 * k + 128 * m + 128],
                       arW2[:, 16384 + 512 * k:16384 + 512 * k + 512],
                       st=(k == 0), sp=(k == NTI - 1),
                       r=["mWo", f"y3{k}"], w=[pk])
                S.dma("sync", arR2[:, 0:512],
                      xres[128 * m:128 * m + 128, c0:c0 + 512], w=["m3x"])
                S.dma("sync", arR2[:, 512:1024],
                      xlns[128 * m:128 * m + 128, c0:c0 + 512], w=["m3xl"])
                S.v(lambda e, pp=pp: e.tensor_add(arR2[:, 0:512], pp[:, 0:512],
                                                  arR2[:, 0:512]),
                    r=[pk, "m3x"], w=["m3x"])
                S.v(lambda e: e.tensor_add(arR2[:, 0:512], arR2[:, 0:512],
                                           arR2[:, 512:1024]),
                    r=["m3x", "m3xl"], w=["m3x"])
                S.dma("sync", xres[128 * m:128 * m + 128, c0:c0 + 512],
                      arR2[:, 0:512], r=["m3x"], w=[f"m3out{m}.{c}"])

    sub_ffn(True, last=False)
    S.fence(cells[0:1, 14:15])
    if n_sub < 3:
        _copy_out(S, E, xres, y_cm)
        return
    sub3()
    S.fence(cells[0:1, 14:15])
    if n_sub < 4:
        _copy_out(S, E, xres, y_cm)
        return
    sub_ffn(False, last=True)


def _copy_out(S, E, src, y_cm):
    arFS = E["arFS"]
    for c in range(8):
        c0 = 256 * c
        for m in range(NTD):
            S.dma("sync", arFS[:, 256 * m:256 * m + 256],
                  src[128 * m:128 * m + 128, c0:c0 + 256], w=[f"co{m}"])
            S.dma("sync", y_cm[128 * m:128 * m + 128, c0:c0 + 256],
                  arFS[:, 256 * m:256 * m + 256], r=[f"co{m}"],
                  w=[f"yo{m}.{c}"])


def _host_inputs(ins):
    import ml_dtypes
    bf = ml_dtypes.bfloat16
    tr = lambda a: np.ascontiguousarray(np.asarray(a).T)
    col = lambda a: np.ascontiguousarray(np.asarray(a).reshape(-1, 1))
    selH = np.zeros((16, NTD * 128), dtype=np.float32)
    for k in range(NTD):
        selH[2 * k, 128 * k:128 * k + 64] = 1.0
        selH[2 * k + 1, 128 * k + 64:128 * k + 128] = 1.0
    sel16H = np.zeros((16, 16 * 128), dtype=np.float32)
    for n in range(16):
        sel16H[n, 128 * n:128 * n + 128] = 1.0
    place2H = np.zeros((2, 128), dtype=np.float32)
    place2H[0, 0] = 1.0
    place2H[1, 64] = 1.0
    mask2H = np.zeros((128, 2), dtype=np.float32)
    mask2H[1:64, 0] = 1.0
    mask2H[65:128, 1] = 1.0
    maskPH = (mask2H[:, 0] + mask2H[:, 1]).reshape(128, 1)
    maskDH = np.ones((128, 1), dtype=np.float32)
    maskDH[0, 0] = 0.0
    com = dict(
        WqT=tr(ins["L0_attn_a_Wq"]), WkT=tr(ins["L0_attn_a_Wk"]),
        WvT=tr(ins["L0_attn_a_Wv"]), WoT=tr(ins["L0_attn_a_Wo"]),
        W1T=tr(ins["L0_ffn_W1"]), W2T=tr(ins["L0_ffn_W2"]),
        WinT=tr(ins["L0_attn_b_W_in"]), WdtT=tr(ins["L0_attn_b_W_dt"]),
        WBT=tr(ins["L0_attn_b_W_B"]), WCT=tr(ins["L0_attn_b_W_C"]),
        WoutT=tr(ins["L0_attn_b_W_out"]),
        b1v=col(ins["L0_ffn_b1"]), b2v=col(ins["L0_ffn_b2"]),
        nwv=col(ins["L0_attn_b_norm_w"]), nbv=col(ins["L0_attn_b_norm_b"]),
        convw=np.ascontiguousarray(np.asarray(ins["L0_attn_b_conv_w"])),
        convb=col(ins["L0_attn_b_conv_b"]), bdtv=col(ins["L0_attn_b_b_dt"]),
        Dv=col(ins["L0_attn_b_D"]),
        Alog=np.ascontiguousarray(np.asarray(ins["L0_attn_b_A_log"])),
        identH=np.eye(128, dtype=bf), place2H=place2H.astype(bf),
        selHF=np.ascontiguousarray(selH),
        mask2H=mask2H.astype(bf), maskPH=maskPH.astype(bf),
        maskDH=maskDH.astype(bf), selH=selH.astype(bf),
        sel16H=sel16H.astype(bf),
    )
    xs = np.asarray(ins["L0_attn_a_x"])
    in_maps = []
    for c in range(8):
        m = dict(com)
        m["x_cm"] = np.ascontiguousarray(xs[c].T)
        in_maps.append(m)
    return in_maps


N_SUB = 4


def kernel(**ins):
    in_maps = _host_inputs(ins)
    nc = build_program(n_sub=N_SUB)
    res = run_bass_kernel_spmd(nc, in_maps, core_ids=list(range(8)))
    out = np.stack([np.asarray(res.results[c]["y_cm"]).T for c in range(8)], 0)
    return np.ascontiguousarray(out.astype(np.float32))


# version 07 (final)
# speedup vs baseline: 1.0305x; 1.0010x over previous
import numpy as np

import concourse.bass as bass
import concourse.mybir as mybir
from concourse.bass_utils import run_bass_kernel_spmd

F32 = mybir.dt.float32
BF16 = mybir.dt.bfloat16
AF = mybir.ActivationFunctionType
ALU = mybir.AluOpType
AX = mybir.AxisListType.X

T = 2048
D = 1024
DI = 2048
DFF = 4096
NH = 16
DH = 64
NTD = D // 128
NTI = DI // 128
NTF = DFF // 128
EPSA = 1e-6
EPSLN = 1e-5

ENGINES = ("sync", "tensor", "vector", "scalar", "gpsimd")
_DMA_QUEUES = ("sync", "gpsimd")


DMA_POOL = {"sync": 18, "gpsimd": 8}


class Op:
    __slots__ = ("eng", "fn", "reads", "writes", "idx", "is_dma", "need_inc",
                 "cum", "deps", "sem_name", "pin", "epos")

    def __init__(self, eng, fn, reads, writes, idx, is_dma):
        self.eng = eng
        self.fn = fn
        self.reads = reads
        self.writes = writes
        self.idx = idx
        self.is_dma = is_dma
        self.need_inc = False
        self.cum = 0
        self.deps = []
        self.sem_name = None
        self.pin = False
        self.epos = 0


class Sched:

    MILE_EVERY = 12

    def __init__(self):
        self.ops = []
        self.last_w = {}
        self.readers = {}
        self.gfence = None
        self.ecount = {e: 0 for e in ENGINES}
        self.miles = {e: [] for e in ENGINES}
        self.last_dma = {}
        self.dcnt = {}
        self.last_real = {}
        self.barriers = []

    def prewait(self, eng, keys):
        return self._add(eng, lambda e: e.nop(), tuple(keys), (), False)

    def fence(self, fdum=None):
        keys = tuple(self.last_w.keys())
        fn = (lambda e: e.nop()) if fdum is None else (
            lambda e: e.memset(fdum, 0.0))
        op = self._add("vector", fn, (), keys, False)
        ded = {d.idx: d for d in op.deps}
        for d in self.last_dma.values():
            ded[d.idx] = d
        op.deps = [ded[i] for i in sorted(ded)]
        self.barriers.append(dict(idx=op.idx, last=dict(self.last_real),
                                  dmas=list(self.last_dma.values())))
        return op

    def _add(self, eng, fn, reads, writes, is_dma, _mile=False):
        op = Op(eng, fn, tuple(reads), tuple(writes), len(self.ops), is_dma)
        op.epos = self.ecount[eng]
        self.ecount[eng] += 1
        if not is_dma and op.epos % self.MILE_EVERY == self.MILE_EVERY - 1:
            self.miles[eng].append(op)
        deps = []
        for k in op.reads:
            w = self.last_w.get(k)
            if w is not None:
                deps.append(w)
        for k in op.writes:
            w = self.last_w.get(k)
            if w is not None:
                deps.append(w)
            deps.extend(self.readers.get(k, ()))

        ded = {}
        for d in deps:
            if d is op:
                continue
            if (d.eng == eng and not d.is_dma and not is_dma
                    and eng in ("tensor", "gpsimd")):
                continue
            ded[d.idx] = d
        op.deps = [ded[i] for i in sorted(ded)]
        for k in op.reads:
            self.readers.setdefault(k, []).append(op)
        for k in op.writes:
            self.last_w[k] = op
            self.readers[k] = [op]
        self.ops.append(op)
        if is_dma:
            self.dcnt[eng] = self.dcnt.get(eng, 0) + 1
            self.last_dma[(eng, (self.dcnt[eng] - 1) % DMA_POOL[eng])] = op
        else:
            self.last_real[eng] = op
        return op

    def t(self, fn, r=(), w=()):
        return self._add("tensor", fn, r, w, False)

    def v(self, fn, r=(), w=()):
        return self._add("vector", fn, r, w, False)

    def s(self, fn, r=(), w=()):
        return self._add("scalar", fn, r, w, False)

    def g(self, fn, r=(), w=()):
        return self._add("gpsimd", fn, r, w, False)

    def dma(self, queue, out, in_, r=(), w=(), slow=False):
        if slow:
            fn = lambda e, o=out, i=in_: e.dma_start(
                out=o, in_=i, allow_slow_non_contiguous=True)
        else:
            fn = lambda e, o=out, i=in_: e.dma_start(out=o, in_=i)
        return self._add(queue, fn, r, w, True)

    def finalize(self, sems):
        import bisect
        mpos = {e: [m.epos for m in self.miles[e]] for e in ENGINES}
        for op in self.ops:
            nd = {}
            for d in op.deps:
                if not d.is_dma:
                    i = bisect.bisect_right(mpos[d.eng], d.epos)
                    if i < len(self.miles[d.eng]):
                        m = self.miles[d.eng][i]
                        if m.idx < op.idx:
                            d = m
                nd[d.idx] = d
            op.deps = [nd[i] for i in sorted(nd)]
        for spec in self.barriers:
            for o in spec["last"].values():
                o.need_inc = True
                o.pin = True
        for op in self.ops:
            for d in op.deps:
                d.need_inc = True
                if op.eng != d.eng or op.is_dma or d.is_dma:
                    d.pin = True
        cums = {e: 0 for e in ENGINES}
        qidx = {q: 0 for q in _DMA_QUEUES}
        finals = {}
        for op in self.ops:
            if op.is_dma:
                qi = qidx[op.eng]
                qidx[op.eng] += 1
                P = DMA_POOL[op.eng]
                op.sem_name = f"dq_{op.eng}_{qi % P}"
                op.cum = 16 * (qi // P + 1)
                op.need_inc = True
                finals[op.sem_name] = op.cum
            elif op.need_inc:
                cums[op.eng] += 1
                op.cum = cums[op.eng]
        self.totals = cums
        bodies = {}
        for e in ENGINES:
            eops = [op for op in self.ops if op.eng == e]

            barr = list(self.barriers)

            def body(eng, eops=eops, e=e, barr=barr):
                seen = {}
                bi = 0
                for op in eops:
                    while bi < len(barr) and barr[bi]["idx"] < op.idx:
                        for e2, o in barr[bi]["last"].items():
                            if e2 == e or not o.need_inc:
                                continue
                            if seen.get(e2, -1) < o.cum:
                                eng.wait_ge(sems[e2], o.cum)
                                seen[e2] = o.cum
                        for d in barr[bi]["dmas"]:
                            if seen.get(d.sem_name, -1) < d.cum:
                                eng.wait_ge(sems[d.sem_name], d.cum)
                                seen[d.sem_name] = d.cum
                        bi += 1
                    if op.is_dma and op.cum > 16:
                        if seen.get(op.sem_name, -1) < op.cum - 16:
                            eng.wait_ge(sems[op.sem_name], op.cum - 16)
                            seen[op.sem_name] = op.cum - 16
                    for d in op.deps:
                        sname = d.sem_name if d.is_dma else d.eng
                        if seen.get(sname, -1) < d.cum:
                            eng.wait_ge(sems[sname], d.cum)
                            seen[sname] = d.cum
                    try:
                        inst = op.fn(eng)
                    except Exception:
                        print("FAILING OP idx", op.idx, "eng", op.eng,
                              "reads", op.reads, "writes", op.writes)
                        raise
                    if op.need_inc:
                        if op.is_dma:
                            inst.then_inc(sems[op.sem_name], 16)
                        else:
                            if inst is None:
                                inst = eng.nop()
                            inst.then_inc(sems[e], 1)
                            if op.pin:
                                eng.wait_ge(sems[e], op.cum)
                                if seen.get(e, -1) < op.cum:
                                    seen[e] = op.cum
                if e == "sync":
                    for sn, v in sorted(finals.items()):
                        if seen.get(sn, -1) < v:
                            eng.wait_ge(sems[sn], v)
            bodies[e] = body
        return bodies


def build_program(n_sub=4):
    nc = bass.Bass("TRN2", target_bir_lowering=False)

    def din(name, shape, dt=F32):
        return nc.dram_tensor(name, list(shape), dt, kind="ExternalInput").ap()

    I = {}
    for nm, shp in (("x_cm", (D, T)), ("WqT", (D, D)), ("WkT", (D, D)),
                    ("WvT", (D, D)), ("WoT", (D, D)), ("W1T", (D, DFF)),
                    ("W2T", (DFF, D)), ("WinT", (D, 2 * DI)), ("WdtT", (DI, DI)),
                    ("WBT", (DI, 16)), ("WCT", (DI, 16)), ("WoutT", (DI, D)),
                    ("b1v", (DFF, 1)), ("b2v", (D, 1)), ("nwv", (D, 1)),
                    ("nbv", (D, 1)), ("convw", (DI, 4)), ("convb", (DI, 1)),
                    ("bdtv", (DI, 1)), ("Dv", (DI, 1)), ("Alog", (DI, 16))):
        I[nm] = din(nm, shp)
    for nm, shp in (("identH", (128, 128)), ("place2H", (2, 128)),
                    ("mask2H", (128, 2)), ("maskPH", (128, 1)),
                    ("maskDH", (128, 1)), ("selH", (16, NTD * 128)),
                    ("sel16H", (16, 16 * 128))):
        I[nm] = din(nm, shp, BF16)
    I["selHF"] = din("selHF", (16, NTD * 128))

    y_cm = nc.dram_tensor("y_cm", [D, T], F32, kind="ExternalOutput").ap()

    xres = nc.dram_tensor("xres", [D, T], F32, kind="Internal").ap()
    Osp = nc.dram_tensor("Osp", [D, T], BF16, kind="Internal").ap()
    Wq1s = nc.dram_tensor("Wq1s", [D, DFF], BF16, kind="Internal").ap()
    Wq2s = nc.dram_tensor("Wq2s", [DFF, D], BF16, kind="Internal").ap()
    xcs = nc.dram_tensor("xcs", [DI, T], BF16, kind="Internal").ap()
    szs = nc.dram_tensor("szs", [DI, T], BF16, kind="Internal").ap()
    dts = nc.dram_tensor("dts", [DI, T], BF16, kind="Internal").ap()
    xlns = nc.dram_tensor("xlns", [D, T], BF16, kind="Internal").ap()

    from contextlib import ExitStack
    st = ExitStack()
    with st:
        sb = lambda *a: st.enter_context(nc.sbuf_tensor(*a))
        ps_ = lambda *a: st.enter_context(nc.psum_tensor(*a))
        E = {}
        E["arW1"] = sb("arW1", [128, 32768], BF16)
        E["arW2"] = sb("arW2", [128, 24576], BF16)
        E["arA"] = sb("arA", [128, 16640], BF16)
        E["arS"] = sb("arS", [128, 4096], BF16)
        E["arFS"] = sb("arFS", [128, 2048], F32)
        E["arRow"] = sb("arRow", [128, 2048], F32)
        E["arR2"] = sb("arR2", [128, 1024], F32)
        E["den16"] = sb("den16", [16, 2048], F32)
        E["arQK2"] = sb("arQK2", [128, 4096], BF16)
        E["sqS"] = sb("sqS", [128, 1024], BF16)
        E["cells"] = sb("cells", [128, 16], F32)
        E["cst"] = sb("cst", [128, 8], F32)
        E["onesb"] = sb("onesb", [128, 1], BF16)
        E["onesrF"] = sb("onesrF", [1, 128], F32)
        E["onesbF"] = sb("onesbF", [128, 1], F32)
        E["onesr"] = sb("onesr", [1, 128], BF16)
        E["ident"] = sb("ident", [128, 128], BF16)
        E["place2"] = sb("place2", [2, 128], BF16)
        E["mask2"] = sb("mask2", [128, 2], BF16)
        E["maskP"] = sb("maskP", [128, 1], BF16)
        E["maskD"] = sb("maskD", [128, 1], BF16)
        E["selO"] = sb("selO", [16, NTD * 128], BF16)
        E["selOF"] = sb("selOF", [16, NTD * 128], F32)
        E["sel16"] = sb("sel16", [16, 16 * 128], BF16)
        E["nw8"] = sb("nw8", [128, 8], F32)
        E["nb8"] = sb("nb8", [128, 8], F32)
        E["b18"] = sb("b18", [128, 32], F32)
        E["b28"] = sb("b28", [128, 8], F32)
        E["cw16"] = sb("cw16", [128, 64], F32)
        E["cb16"] = sb("cb16", [128, 16], F32)
        E["bdt16"] = sb("bdt16", [128, 16], F32)
        E["dd16"] = sb("dd16", [128, 16], F32)
        E["negA"] = sb("negA", [128, 256], F32)
        E["gamr"] = sb("gamr", [128, 4], F32)
        E["biasx"] = sb("biasx", [128, 8], F32)
        E["carry"] = sb("carry", [128, 256], F32)
        E["brow"] = sb("brow", [16, 1024], BF16)
        E["wbc"] = sb("wbc", [128, 512], BF16)

        E["psA"] = ps_("psA", [128, 512], F32)
        E["psB"] = ps_("psB", [128, 512], F32)
        E["psR"] = ps_("psR", [128, 512], F32)
        E["psL"] = ps_("psL", [128, 512], F32)
        E["psC"] = ps_("psC", [128, 1024], F32)
        E["psT"] = ps_("psT", [128, 2048], BF16)

        E.update(I=I, y_cm=y_cm, xres=xres, Osp=Osp, Wq1s=Wq1s, Wq2s=Wq2s,
                 xcs=xcs, szs=szs, dts=dts, xlns=xlns)

        sems = {}
        for e in ENGINES:
            sems[e] = st.enter_context(nc.semaphore("s_" + e))
        for q in _DMA_QUEUES:
            for i in range(DMA_POOL[q]):
                sems[f"dq_{q}_{i}"] = st.enter_context(
                    nc.semaphore(f"dq_{q}_{i}"))

        S = Sched()
        _emit(S, E, n_sub)
        bodies = S.finalize(sems)
        with nc.Block() as block:
            block.sync(bodies["sync"])
            block.tensor(bodies["tensor"])
            block.vector(bodies["vector"])
            block.scalar(bodies["scalar"])
            block.gpsimd(bodies["gpsimd"])
    return nc


def _emit(S, E, n_sub):
    I = E["I"]
    arW1, arW2, arA, arS, arFS = (E["arW1"], E["arW2"], E["arA"], E["arS"],
                                  E["arFS"])
    arRow, arR2, den16, cells = (E["arRow"], E["arR2"], E["den16"],
                                 E["cells"])
    arQK2 = E["arQK2"]
    sqS = E["sqS"]
    cst, onesb, onesr, onesrF = E["cst"], E["onesb"], E["onesr"], E["onesrF"]
    onesbF = E["onesbF"]
    ident, place2, mask2, maskP, maskD = (E["ident"], E["place2"], E["mask2"],
                                          E["maskP"], E["maskD"])
    selO, sel16, selOF = E["selO"], E["sel16"], E["selOF"]
    nw8, nb8, b18, b28 = E["nw8"], E["nb8"], E["b18"], E["b28"]
    cw16, cb16, bdt16, dd16 = E["cw16"], E["cb16"], E["bdt16"], E["dd16"]
    negA, gamr, biasx, carry = E["negA"], E["gamr"], E["biasx"], E["carry"]
    brow, wbc = E["brow"], E["wbc"]
    psA, psB, psR, psL, psC, psT = (E["psA"], E["psB"], E["psR"], E["psL"],
                                    E["psC"], E["psT"])
    y_cm, xres, Osp = E["y_cm"], E["xres"], E["Osp"]
    Wq1s, Wq2s, xcs, szs, dts, xlns = (E["Wq1s"], E["Wq2s"], E["xcs"],
                                       E["szs"], E["dts"], E["xlns"])

    RS = [arRow[0:1, 512 * i:512 * (i + 1)] for i in range(4)]
    RK = [f"RS{i}" for i in range(4)]
    QR = [arR2[0:2, 512 * i:512 * (i + 1)] for i in range(2)]
    QK = [f"QR{i}" for i in range(2)]
    CL = lambda i: cells[0:1, i:i + 1]
    CK = lambda i: f"CL{i}"

    def mm(out, lhsT, rhs, *, st, sp, r, w):
        return S.t(lambda e, o=out, l=lhsT, rr=rhs, st_=st, sp_=sp:
                   e.matmul(o, l, rr, start=st_, stop=sp_),
                   r=tuple(r), w=tuple(w))

    def act(out, in_, func, *, r, w, bias=None, scale=1.0):
        if bias is None:
            fn = lambda e, o=out, i=in_, f=func, sc=scale: e.activation(
                o, i, f, scale=sc)
        else:
            fn = lambda e, o=out, i=in_, f=func, bb=bias, sc=scale: e.activation(
                o, i, f, bias=bb, scale=sc)
        return S.s(fn, r=tuple(r), w=tuple(w))

    def vop(fn, r, w):
        return S.v(fn, r=tuple(r), w=tuple(w))

    csts = {0: 1.0, 1: EPSLN, 2: 1.0 + EPSA, 3: EPSA, 4: 0.001, 5: 5.0,
            6: 0.125, 7: 1e-30}
    for i, vv in csts.items():
        S.g(lambda e, i=i, vv=vv: e.memset(cst[:, i:i + 1], vv), w=[f"cst{i}"])
    S.g(lambda e: e.memset(onesb[:], 1.0), w=["onesb"])
    S.g(lambda e: e.memset(onesr[:], 1.0), w=["onesr"])
    S.g(lambda e: e.memset(onesrF[:], 1.0), w=["onesrF"])
    S.g(lambda e: e.memset(onesbF[:], 1.0), w=["onesbF"])
    S.g(lambda e: e.memset(carry[:], 0.0),
        w=["carry"] + [f"cr{s}" for s in range(256)])

    for nm, dst in (("identH", ident), ("place2H", place2), ("mask2H", mask2),
                    ("maskPH", maskP), ("maskDH", maskD), ("selH", selO),
                    ("selHF", E["selOF"]), ("sel16H", sel16)):
        S.dma("sync", dst[:], I[nm], w=[nm])
    for nm, dst in (("nwv", nw8), ("nbv", nb8), ("b1v", b18), ("b2v", b28),
                    ("convb", cb16), ("bdtv", bdt16), ("Dv", dd16)):
        S.dma("sync", dst[:], I[nm].rearrange("(j p) o -> p (j o)", p=128),
              w=[nm], slow=True)
    S.dma("sync", cw16[:].rearrange("p (j k) -> p j k", k=4),
          I["convw"].rearrange("(j p) k -> p j k", p=128), w=["cw16"])
    S.dma("sync", negA[:].rearrange("p (j n) -> p j n", n=16),
          I["Alog"].rearrange("(j p) n -> p j n", p=128), w=["negA"])
    S.s(lambda e: e.activation(negA[:], negA[:], AF.Exp), r=["negA"], w=["negA"])
    S.v(lambda e: e.tensor_scalar(negA[:], negA[:], -1.0, None, ALU.mult),
        r=["negA"], w=["negA"])
    S.dma("gpsimd", wbc[:, 0:256].rearrange("p (j n) -> p j n", n=16),
          I["WBT"].rearrange("(j p) n -> p j n", p=128), w=["wbc"])
    S.dma("gpsimd", wbc[:, 256:512].rearrange("p (j n) -> p j n", n=16),
          I["WCT"].rearrange("(j p) n -> p j n", p=128), w=["wbc"])
    for wi, nm in enumerate(("WqT", "WkT", "WvT", "WoT")):
        S.dma("gpsimd", arW1[:, 8192 * wi:8192 * (wi + 1)].rearrange(
                  "p (j o) -> p j o", o=D),
              I[nm].rearrange("(j p) o -> p j o", p=128), w=[f"aW{wi}"])

    def ln_chunk(tag, src, c0, CH, out_bf, out_key, *, clip8, fkey=None,
                 sqb=None):
        fk = fkey or (lambda j: f"{tag}x{j}")
        if sqb is None:
            sqb = lambda i: arS[:, 256 + 512 * i:256 + 512 * i + CH]
        R = [arRow[0:1, 512 * i:512 * i + CH] for i in range(4)]
        for j in range(NTD):
            S.dma("gpsimd", out_bf(j), src[128 * j:128 * (j + 1), c0:c0 + CH],
                  w=[out_key(j)])
        S.prewait("tensor", ["onesb"] + [out_key(j) for j in range(NTD)])
        for j in range(NTD):
            mm(psR[0:1, 0:CH], onesb[:], out_bf(j), st=(j == 0),
               sp=(j == NTD - 1), r=["onesb", out_key(j)], w=["psR"])
        vop(lambda e: e.tensor_scalar(R[0], psR[0:1, 0:CH], 1.0 / D, None,
                                      ALU.mult), r=["psR"], w=[RK[0]])
        for j in range(NTD):
            sq = sqb(j % 2)
            S.v(lambda e, j=j, sq=sq: e.tensor_mul(sq, out_bf(j), out_bf(j)),
                r=[out_key(j)], w=[f"sqt{j % 2}"])
            mm(psA[0:1, 0:CH], onesb[:], sq, st=(j == 0), sp=(j == NTD - 1),
               r=["onesb", f"sqt{j % 2}"], w=["psA"])
        vop(lambda e: e.tensor_scalar(R[1], psA[0:1, 0:CH], 1.0 / D, None,
                                      ALU.mult), r=["psA"], w=[RK[1]])
        vop(lambda e: e.tensor_mul(R[2], R[0], R[0]), r=[RK[0]], w=[RK[2]])
        vop(lambda e: e.tensor_sub(R[1], R[1], R[2]), r=[RK[1], RK[2]],
            w=[RK[1]])
        act(R[1], R[1], AF.Ln, bias=cst[0:1, 1:2], r=[RK[1], "cst1"], w=[RK[1]])
        act(R[1], R[1], AF.Exp, scale=-0.5, r=[RK[1]], w=[RK[1]])
        vop(lambda e: e.tensor_mul(R[0], R[0], R[1]), r=[RK[0], RK[1]],
            w=[RK[0]])
        vop(lambda e: e.tensor_scalar(R[0], R[0], -1.0, None, ALU.mult),
            r=[RK[0]], w=[RK[0]])
        mm(psL[:, 0:CH], onesrF[0:1, :], R[1], st=True, sp=True,
           r=["onesrF", RK[1]], w=["psL0"])
        mm(psL[:, CH:2 * CH], onesrF[0:1, :], R[0], st=True, sp=True,
           r=["onesrF", RK[0]], w=["psL0"])
        for j in range(NTD):
            ob = out_bf(j)
            S.v(lambda e, ob=ob: e.tensor_mul(ob, ob, psL[:, 0:CH]),
                r=[out_key(j), "psL0"], w=[out_key(j)])
            S.v(lambda e, ob=ob: e.tensor_add(ob, ob, psL[:, CH:2 * CH]),
                r=[out_key(j), "psL0"], w=[out_key(j)])
            if clip8:
                S.v(lambda e, ob=ob: e.tensor_scalar(ob, ob, -8.0, 8.0,
                                                     ALU.max, ALU.min),
                    r=[out_key(j)], w=[out_key(j)])

    def pipe2(gens, depth=2):
        from collections import deque
        act = deque()
        gens = list(gens)
        while gens or act:
            while len(act) < depth and gens:
                act.append(gens.pop(0))
            g = act.popleft()
            try:
                next(g)
                act.append(g)
            except StopIteration:
                pass

    def ln_bf(tag, src, c0, CH, xb, xbk, par, *, clip8, sqb):
        yield from _ln_bf(tag, src, c0, CH, xb, xbk, par, clip8, sqb)

    def _ln_bf(tag, src, c0, CH, xb, xbk, par, clip8, sqb):
        R = [arRow[0:1, 256 * (4 * par + i):256 * (4 * par + i) + CH]
             for i in range(4)]
        RKp = [f"row{par}{i}" for i in range(4)]
        pstat = psC[0:1, 512 * par:512 * par + CH]
        psk = f"psC{par}"
        for j in range(NTD):
            S.dma("gpsimd", xb(j), src[128 * j:128 * (j + 1), c0:c0 + CH],
                  w=[xbk(j)])
        yield
        S.prewait("tensor", ["onesb"] + [xbk(j) for j in range(NTD)])
        for j in range(NTD):
            mm(pstat, onesb[:], xb(j), st=(j == 0), sp=(j == NTD - 1),
               r=["onesb", xbk(j)], w=[psk])
        vop(lambda e: e.tensor_scalar(R[0], pstat, 1.0 / D, None, ALU.mult),
            r=[psk], w=[RKp[0]])
        yield
        for j in range(NTD):
            sq = sqb(j % 2)
            S.v(lambda e, j=j, sq=sq: e.tensor_mul(sq, xb(j), xb(j)),
                r=[xbk(j)], w=[f"{tag}sq{par}{j % 2}"])
            mm(pstat, onesb[:], sq, st=(j == 0), sp=(j == NTD - 1),
               r=["onesb", f"{tag}sq{par}{j % 2}"], w=[psk])
        vop(lambda e: e.tensor_scalar(R[1], pstat, 1.0 / D, None, ALU.mult),
            r=[psk], w=[RKp[1]])
        yield
        vop(lambda e: e.tensor_mul(R[2], R[0], R[0]), r=[RKp[0]], w=[RKp[2]])
        vop(lambda e: e.tensor_sub(R[1], R[1], R[2]), r=[RKp[1], RKp[2]],
            w=[RKp[1]])
        act(R[1], R[1], AF.Ln, bias=cst[0:1, 1:2], r=[RKp[1], "cst1"], w=[RKp[1]])
        act(R[1], R[1], AF.Exp, scale=-0.5, r=[RKp[1]], w=[RKp[1]])
        vop(lambda e: e.tensor_mul(R[0], R[0], R[1]), r=[RKp[0], RKp[1]],
            w=[RKp[0]])
        vop(lambda e: e.tensor_scalar(R[0], R[0], -1.0, None, ALU.mult),
            r=[RKp[0]], w=[RKp[0]])
        pbank = psL if par == 0 else psR
        pb1 = pbank[:, 0:CH]
        pb2 = pbank[:, 256:256 + CH]
        bk1 = ("bcL0", "bcR0")[par]
        bk2 = ("bcL1", "bcR1")[par]
        mm(pb1, onesrF[0:1, :], R[1], st=True, sp=True, r=["onesrF", RKp[1]],
           w=[bk1])
        mm(pb2, onesrF[0:1, :], R[0], st=True, sp=True, r=["onesrF", RKp[0]],
           w=[bk2])
        yield
        for j in range(NTD):
            S.v(lambda e, j=j, pb1=pb1: e.tensor_mul(xb(j), xb(j), pb1),
                r=[xbk(j), bk1], w=[xbk(j)])
            S.v(lambda e, j=j, pb2=pb2: e.tensor_add(xb(j), xb(j), pb2),
                r=[xbk(j), bk2], w=[xbk(j)])
            if clip8:
                S.v(lambda e, j=j: e.tensor_scalar(xb(j), xb(j), -8.0, 8.0,
                                                   ALU.max, ALU.min),
                    r=[xbk(j)], w=[xbk(j)])
            if j % 3 == 2:
                yield

    def sub1():
        CH = 256
        vk = lambda j: f"v{j}"

        for tt in range(16):
            ap = arA[:, 1040 * tt:1040 * tt + 1040].rearrange(
                "p (h c) -> p h c", c=65)[:, :, 64:65]
            S.g(lambda e, ap=ap: e.memset(ap, 1.0),
                w=[f"V65:{tt}.{pp}" for pp in range(8)])

        def a1_chunk(c):
            c0 = c * CH
            par = c % 2
            spb = lambda j, par=par: arS[:, 256 * (2 * j + par):256 * (2 * j + par) + CH]
            spk = lambda j, par=par: f"sp{par}.{j}"
            RL = [arRow[0:1, 256 * (4 * par + i):256 * (4 * par + i) + CH]
                  for i in range(4)]
            RKL = [f"row{par}{i}" for i in range(4)]
            yield from ln_bf("A1", I["x_cm"], c0, CH, spb, spk, par, clip8=True,
                  sqb=lambda i, par=par: arW2[:, 20480 + 256 * (2 * par + i):
                                              20480 + 256 * (2 * par + i) + CH])
            for j in range(NTD):
                sq = arW2[:, 20480 + 256 * (2 * par + j % 2):20480 + 256 * (2 * par + j % 2) + CH]
                S.v(lambda e, j=j, sq=sq: e.tensor_mul(sq, spb(j), spb(j)),
                    r=[spk(j)], w=[f"A1sq{par}{j % 2}"])
                mm(psC[0:1, 512 * par:512 * par + CH], maskD[:] if j == 0 else onesb[:], sq,
                   st=(j == 0), sp=(j == NTD - 1),
                   r=["maskDH" if j == 0 else "onesb", f"A1sq{par}{j % 2}"], w=[f"psC{par}"])
            yield
            vop(lambda e, par=par: e.tensor_copy(RL[0], psC[0:1, 512 * par:512 * par + CH]), r=[f"psC{par}"], w=[RKL[0]])
            act(RL[1], RL[0], AF.Ln, bias=cst[0:1, 2:3], r=[RKL[0], "cst2"],
                w=[RKL[1]])
            act(RL[1], RL[1], AF.Exp, scale=0.5, r=[RKL[1]], w=[RKL[1]])
            vop(lambda e: e.tensor_scalar(RL[2], RL[1], 1.0 + EPSA, None,
                                          ALU.max), r=[RKL[1]], w=[RKL[2]])
            vop(lambda e: e.tensor_mul(RL[3], RL[2], RL[2]), r=[RKL[2]], w=[RKL[3]])
            vop(lambda e: e.tensor_scalar(RL[3], RL[3], -1.0, None, ALU.add),
                r=[RKL[3]], w=[RKL[3]])
            act(RL[3], RL[3], AF.Ln, r=[RKL[3]], w=[RKL[3]])
            act(RL[3], RL[3], AF.Exp, scale=0.5, r=[RKL[3]], w=[RKL[3]])
            vop(lambda e: e.tensor_add(RL[3], RL[3], RL[2]), r=[RKL[3], RKL[2]],
                w=[RKL[3]])
            act(RL[3], RL[3], AF.Ln, r=[RKL[3]], w=[RKL[3]])
            vop(lambda e: e.tensor_scalar(RL[3], RL[3], 0.001, None, ALU.max),
                r=[RKL[3]], w=[RKL[3]])
            yield
            vop(lambda e: e.tensor_scalar(RL[2], RL[1], -(1.0 + EPSA), 0.0,
                                          ALU.add, ALU.min), r=[RKL[1]], w=[RKL[2]])
            vop(lambda e: e.tensor_mul(RL[1], RL[2], RL[2]), r=[RKL[2]], w=[RKL[1]])
            vop(lambda e: e.tensor_sub(RL[0], RL[0], RL[1]), r=[RKL[0], RKL[1]],
                w=[RKL[0]])
            vop(lambda e: e.tensor_scalar(RL[0], RL[0], EPSA, None, ALU.max),
                r=[RKL[0]], w=[RKL[0]])
            act(RL[0], RL[0], AF.Ln, r=[RKL[0]], w=[RKL[0]])
            act(RL[0], RL[0], AF.Exp, scale=0.5, r=[RKL[0]], w=[RKL[0]])
            vop(lambda e: e.tensor_scalar(RL[0], RL[0], EPSA, None, ALU.add),
                r=[RKL[0]], w=[RKL[0]])
            vop(lambda e: e.reciprocal(RL[0], RL[0]), r=[RKL[0]], w=[RKL[0]])
            vop(lambda e: e.tensor_mul(RL[0], RL[0], RL[3]), r=[RKL[0], RKL[3]],
                w=[RKL[0]])
            vop(lambda e: e.tensor_mul(RL[2], RL[2], RL[0]), r=[RKL[2], RKL[0]],
                w=[RKL[2]])
            yield
            pbs = (psL if par == 0 else psR)[:, 0:CH]
            pbsk = ("bcL0", "bcR0")[par]
            mm(pbs, onesrF[0:1, :], RL[0], st=True, sp=True,
               r=["onesrF", RKL[0]], w=[pbsk])
            for j in range(NTD):
                S.v(lambda e, j=j, c0=c0, pbs=pbs: e.tensor_mul(
                    arW2[:, 2048 * j + c0:2048 * j + c0 + CH], spb(j), pbs),
                    r=[spk(j), pbsk], w=[vk(j)])
            S.v(lambda e, c0=c0: e.tensor_copy(arW2[0:1, c0:c0 + CH], RL[2]),
                r=[RKL[2]], w=[vk(0)])


        pipe2([a1_chunk(c) for c in range(T // CH)], depth=1)

        S.fence(cells[0:1, 14:15])
        QpB = [arW2[:, 16384:18432], arQK2[:, 0:2048]]
        KpB = [arW2[:, 18432:20480], arQK2[:, 2048:4096]]
        Pb = [arW2[:, 20480 + 512 * i:20992 + 512 * i] for i in range(2)] + [
            arW2[:, 23040 + 512 * i:23552 + 512 * i] for i in range(2)]
        Vcm = arS[:, 0:2048]
        Ostg2 = [arS[:, 2048:2560], arS[:, 2560:3072]]
        OsK = ["Ostg", "OstgB"]
        sqp2 = [arW2[:, 21504 + 512 * i:22016 + 512 * i] for i in range(2)]
        x0c = arW2[0:2, 22528:23040]
        x0cs = [x0c, arW2[0:2, 24064:24576]]
        PP = [(psA, "psA"), (psB, "psB")]

        def proj_nq(p, wslot, dstap, dstkey, mode, nq):
            pp, pk = PP[nq % 2]
            for k in range(NTD):
                mm(pp[:, 0:512], arW1[:, 8192 * wslot + 1024 * k + 128 * p:
                                      8192 * wslot + 1024 * k + 128 * p + 128],
                   arW2[:, 2048 * k + 512 * nq:2048 * k + 512 * nq + 512],
                   st=(k == 0), sp=(k == NTD - 1),
                   r=[f"aW{wslot}", vk(k)], w=[pk])
            if mode == "clip5":
                S.v(lambda e, nq=nq, pp=pp: e.tensor_scalar(
                    dstap[:, 512 * nq:512 * nq + 512], pp[:, 0:512], -5.0,
                    5.0, ALU.max, ALU.min), r=[pk], w=[dstkey])
            else:
                S.v(lambda e, nq=nq, pp=pp: e.tensor_copy(
                    dstap[:, 512 * nq:512 * nq + 512], pp[:, 0:512]),
                    r=[pk], w=[dstkey])

        def head_rows_nq(p, srcap, srckey, negate, maxcell, nq):
            sqp = sqp2[nq % 2]
            xc = x0cs[nq % 2]
            xck = f"x0c{nq % 2}"
            S.v(lambda e, nq=nq, sqp=sqp: e.tensor_mul(
                sqp, srcap[:, 512 * nq:512 * nq + 512],
                srcap[:, 512 * nq:512 * nq + 512]), r=[srckey],
                w=[f"sqp{nq % 2}"])
            mm(psA[0:2, 0:512], mask2[:], sqp, st=True, sp=True,
               r=["mask2H", f"sqp{nq % 2}"], w=["psA"])
            mm(psB[0:1, 0:512], maskP[:], sqp, st=True, sp=True,
               r=["maskPH", f"sqp{nq % 2}"], w=["psB"])
            vop(lambda e: e.reduce_max(CL(15), psB[0:1, 0:512], axis=AX),
                r=["psB"], w=[CK(15)])
            vop(lambda e, mc=maxcell: e.tensor_max(CL(mc), CL(mc), CL(15)),
                r=[CK(15), CK(maxcell)], w=[CK(maxcell)])
            act(xc, psA[0:2, 0:512], AF.Ln, bias=cst[0:2, 2:3],
                r=["psA", "cst2"], w=[xck])
            act(xc, xc, AF.Exp, scale=0.5, r=[xck], w=[xck])
            if negate:
                vop(lambda e, xc=xc: e.tensor_scalar(xc, xc, -1.0, None,
                                                     ALU.mult), r=[xck], w=[xck])
            S.v(lambda e, nq=nq, xc=xc: e.tensor_copy(
                srcap[0:1, 512 * nq:512 * nq + 512], xc[0:1, :]),
                r=[xck], w=[srckey])
            S.dma("sync", srcap[64:65, 512 * nq:512 * nq + 512], xc[1:2, :],
                  r=[xck], w=[srckey], slow=True)

        def prep_thunks(p):
            par = p % 2
            Qp, Kp = QpB[par], KpB[par]
            qk, kk = f"Qp{par}", f"Kp{par}"
            th = []
            th.append(lambda: (S.prewait("tensor", ["aW0", "aW1", "aW2"] +
                                         [vk(k) for k in range(NTD)]),
                               S.v(lambda e: e.memset(CL(0), 0.0), r=(), w=[CK(0)]),
                               S.v(lambda e: e.memset(CL(1), 0.0), r=(), w=[CK(1)])))
            for nq in range(4):
                th.append(lambda nq=nq: proj_nq(p, 0, Qp, qk, "clip5", nq))
            for nq in range(4):
                th.append(lambda nq=nq: head_rows_nq(p, Qp, qk, False, 0, nq))
            for nq in range(4):
                th.append(lambda nq=nq: proj_nq(p, 1, Kp, kk, "clip5", nq))
            for nq in range(4):
                th.append(lambda nq=nq: head_rows_nq(p, Kp, kk, True, 1, nq))

            def bounds():
                vop(lambda e: e.tensor_scalar(CL(0), CL(0), 2.0, 1.0 + EPSA,
                                              ALU.mult, ALU.add), r=[CK(0)], w=[CK(0)])
                vop(lambda e: e.tensor_scalar(CL(1), CL(1), 2.0, 1.0 + EPSA,
                                              ALU.mult, ALU.add), r=[CK(1)], w=[CK(1)])
                vop(lambda e: e.tensor_mul(CL(0), CL(0), CL(1)), r=[CK(0), CK(1)],
                    w=[CK(0)])
                act(CL(0), CL(0), AF.Ln, bias=cst[0:1, 7:8],
                    r=[CK(0), "cst7"], w=[CK(0)])
                act(CL(0), CL(0), AF.Exp, scale=0.5, r=[CK(0)], w=[CK(0)])
                vop(lambda e: e.tensor_scalar(CL(0), CL(0), -0.125, None, ALU.mult),
                    r=[CK(0)], w=[CK(0)])
                mm(psA[:, 0:1], onesrF[0:1, :], CL(0), st=True, sp=True,
                   r=["onesrF", CK(0)], w=["psA"])
                S.v(lambda e, p=p: e.tensor_copy(biasx[:, p:p + 1], psA[:, 0:1]),
                    r=["psA"], w=[f"bx{p}"])
            th.append(bounds)
            for nq in range(4):
                th.append(lambda nq=nq: proj_nq(p, 2, Vcm, "Vcm", "copy", nq))

            def vtrans(tt):
                half = tt % 2
                S.t(lambda e, tt=tt, half=half: e.transpose(
                    psT[:, 1024 * half:1024 * half + 128],
                    Vcm[:, 128 * tt:128 * tt + 128], ident[:]),
                    r=["Vcm", "identH"], w=[f"psT{half}"])
                for hl in range(2):
                    h = 2 * p + hl
                    S.v(lambda e, tt=tt, half=half, hl=hl, h=h: e.tensor_copy(
                        arA[:, 1040 * tt + 65 * h:1040 * tt + 65 * h + 64],
                        psT[:, 1024 * half + 64 * hl:1024 * half + 64 * hl + 64]),
                        r=[f"psT{half}"], w=[f"V65:{tt}.{p}"])
            for tt in range(16):
                th.append(lambda tt=tt: vtrans(tt))
            return th

        def attention(p, prep_iter):
            par = p % 2
            Qp, Kp = QpB[par], KpB[par]
            qk, kk = f"Qp{par}", f"Kp{par}"
            AVb = [(psC[0:65, 0:512], "psC0"), (psC[0:65, 512:1024], "psC1"),
                   (psR[0:65, 0:512], "psR"), (psL[0:65, 0:512], "psL0")]
            SBb = [(psA, "psA"), (psB, "psB")]

            def hblock(hl):
                h = 2 * p + hl

                def emit_S(i, q):
                    pp, pk = SBb[q % 2]
                    mm(pp[0:128, 0:512],
                       Kp[64 * hl:64 * hl + 64, 128 * i:128 * i + 128],
                       Qp[64 * hl:64 * hl + 64, 512 * q:512 * q + 512],
                       st=True, sp=True, r=[kk, qk], w=[pk])

                def emit_E(i, q):
                    pp, pk = SBb[q % 2]
                    act(Pb[q], pp[0:128, 0:512], AF.Exp,
                        bias=biasx[:, p:p + 1], scale=0.125,
                        r=[pk, f"bx{p}"], w=[f"Pb{q}"])

                def emit_AV(i, q):
                    ap, ak = AVb[q]
                    mm(ap, arA[:, 1040 * i + 65 * h:1040 * i + 65 * h + 65],
                       Pb[q], st=(i == 0), sp=(i == 15),
                       r=[f"V65:{i}.{p}", f"Pb{q}"], w=[ak])

                for i in range(16):
                    for q in range(4):
                        emit_S(i, q)
                        emit_E(i, q)
                    if i >= 1:
                        for q in range(4):
                            emit_AV(i - 1, q)
                    t = next(prep_iter, None)
                    if t is not None:
                        t()
                for q in range(4):
                    emit_AV(15, q)
                for q in range(4):
                    ap, ak = AVb[q]
                    S.v(lambda e, ap=ap: e.tensor_copy(
                        arR2[64:65, 0:512], ap[64:65, 0:512]),
                        r=[ak], w=["dstg"])
                    S.dma("sync", den16[h:h + 1, 512 * q:512 * q + 512],
                          arR2[64:65, 0:512], r=["dstg"], w=["den16"])
                    S.v(lambda e, ap=ap, hl=hl: e.tensor_copy(
                        Ostg2[hl][0:64, 0:512], ap[0:64, 0:512]),
                        r=[ak], w=[OsK[hl]])
                    S.dma("sync", Osp[64 * h:64 * h + 64, 512 * q:512 * q + 512],
                          Ostg2[hl][0:64, 0:512], r=[OsK[hl]],
                          w=[f"Osp{h}"])
            return [lambda hl=hl: hblock(hl) for hl in range(2)]

        for fn in prep_thunks(0):
            fn()
        for p in range(8):
            pi = iter(prep_thunks(p + 1) if p < 7 else [])
            for qf in attention(p, pi):
                qf()
            for t in pi:
                t()

        S.fence(cells[0:1, 14:15])
        vop(lambda e: e.tensor_scalar(den16[:], den16[:], 1e-30, None, ALU.add),
            r=["den16"], w=["den16"])
        vop(lambda e: e.reciprocal(den16[:], den16[:]), r=["den16"], w=["den16"])

        for c in range(4):
            c0 = 512 * c
            for k in range(NTD):
                S.dma("sync", arS[:, 512 * k:512 * k + 512],
                      Osp[128 * k:128 * k + 128, c0:c0 + 512],
                      r=[f"Osp{2 * k}", f"Osp{2 * k + 1}", "fence_a3"],
                      w=[f"Ob{k}"])
                mm(psL[:, 0:512], selOF[:, 128 * k:128 * k + 128],
                   den16[0:16, c0:c0 + 512], st=True, sp=True,
                   r=["selHF", "den16"], w=["psL0"])
                S.v(lambda e, k=k: e.tensor_mul(arS[:, 512 * k:512 * k + 512],
                                                arS[:, 512 * k:512 * k + 512],
                                                psL[:, 0:512]),
                    r=[f"Ob{k}", "psL0"], w=[f"Ob{k}"])
            S.prewait("tensor", ["aW3"] + [f"Ob{k}" for k in range(NTD)])
            for m in range(NTD):
                pp, pk = (psA, "psA") if m % 2 == 0 else (psB, "psB")
                for k in range(NTD):
                    mm(pp[:, 0:512], arW1[:, 8192 * 3 + 1024 * k + 128 * m:
                                          8192 * 3 + 1024 * k + 128 * m + 128],
                       arS[:, 512 * k:512 * k + 512], st=(k == 0),
                       sp=(k == NTD - 1), r=["aW3", f"Ob{k}"], w=[pk])
                S.v(lambda e, m=m, pp=pp: e.tensor_copy(
                    arA[:, 512 * m:512 * m + 512], pp[:, 0:512]),
                    r=[pk, "fence_a3"], w=[f"o2{m}"])
            for m in range(NTD):
                sq = arW2[:, 20480 + 512 * (m % 2):20480 + 512 * (m % 2) + 512]
                S.v(lambda e, m=m, sq=sq: e.tensor_mul(
                    sq, arA[:, 512 * m:512 * m + 512], arA[:, 512 * m:512 * m + 512]),
                    r=[f"o2{m}"], w=[f"sqt{m % 2}"])
                mm(psR[0:1, 0:512], maskD[:] if m == 0 else onesb[:], sq,
                   st=(m == 0), sp=(m == NTD - 1),
                   r=["maskDH" if m == 0 else "onesb", f"sqt{m % 2}"], w=["psR"])
            vop(lambda e: e.tensor_scalar(RS[0], psR[0:1, 0:512], 0.0, None,
                                          ALU.max), r=["psR"], w=[RK[0]])
            act(RS[1], RS[0], AF.Ln, bias=cst[0:1, 3:4], r=[RK[0], "cst3"],
                w=[RK[1]])
            act(RS[1], RS[1], AF.Exp, scale=0.5, r=[RK[1]], w=[RK[1]])
            vop(lambda e: e.tensor_scalar(RS[1], RS[1], 5.0, None, ALU.min),
                r=[RK[1]], w=[RK[1]])
            act(RS[2], RS[1], AF.Exp, r=[RK[1]], w=[RK[2]])
            act(RS[3], RS[1], AF.Exp, scale=-1.0, r=[RK[1]], w=[RK[3]])
            vop(lambda e: e.tensor_sub(RS[2], RS[2], RS[3]), r=[RK[2], RK[3]],
                w=[RK[2]])
            vop(lambda e: e.tensor_scalar(RS[1], RS[1], EPSA, None, ALU.add),
                r=[RK[1]], w=[RK[1]])
            vop(lambda e: e.reciprocal(RS[1], RS[1]), r=[RK[1]], w=[RK[1]])
            vop(lambda e: e.tensor_mul(RS[2], RS[2], RS[1]), r=[RK[2], RK[1]],
                w=[RK[2]])
            vop(lambda e: e.tensor_scalar(RS[2], RS[2], 0.5, None, ALU.mult),
                r=[RK[2]], w=[RK[2]])
            mm(psL[:, 0:512], onesrF[0:1, :], RS[2], st=True, sp=True,
               r=["onesrF", RK[2]], w=["psL0"])
            for m in range(NTD):
                S.v(lambda e, m=m: e.tensor_mul(arA[:, 512 * m:512 * m + 512],
                                                arA[:, 512 * m:512 * m + 512],
                                                psL[:, 0:512]),
                    r=[f"o2{m}", "psL0"], w=[f"o2{m}"])
                S.v(lambda e, m=m: e.tensor_scalar(arA[:, 512 * m:512 * m + 512],
                                                   arA[:, 512 * m:512 * m + 512],
                                                   -8.0, 8.0, ALU.max, ALU.min),
                    r=[f"o2{m}"], w=[f"o2{m}"])
                sq = arW2[:, 20480 + 512 * (m % 2):20480 + 512 * (m % 2) + 512]
                S.v(lambda e, m=m, sq=sq: e.tensor_mul(
                    sq, arA[:, 512 * m:512 * m + 512], arA[:, 512 * m:512 * m + 512]),
                    r=[f"o2{m}"], w=[f"sqt{m % 2}"])
                mm(psR[0:1, 0:512], maskD[:] if m == 0 else onesb[:], sq,
                   st=(m == 0), sp=(m == NTD - 1),
                   r=["maskDH" if m == 0 else "onesb", f"sqt{m % 2}"], w=["psR"])
            act(RS[0], psR[0:1, 0:512], AF.Ln, bias=cst[0:1, 2:3],
                r=["psR", "cst2"], w=[RK[0]])
            act(RS[0], RS[0], AF.Exp, scale=0.5, r=[RK[0]], w=[RK[0]])
            for m in range(NTD):
                xb = arFS[:, 512 * (m % 4):512 * (m % 4) + 512]
                xk = f"A3x{m % 4}"
                S.dma("sync", xb, I["x_cm"][128 * m:128 * m + 128, c0:c0 + 512],
                      w=[xk])
                if m == 0:
                    S.v(lambda e, xb=xb: e.tensor_add(RS[1], xb[0:1, :], RS[0]),
                        r=[xk, RK[0]], w=[RK[1]])
                S.v(lambda e, m=m, xb=xb: e.tensor_add(
                    xb, xb, arA[:, 512 * m:512 * m + 512]),
                    r=[xk, f"o2{m}"], w=[xk])
                if m == 0:
                    S.v(lambda e, xb=xb: e.tensor_copy(xb[0:1, :], RS[1]),
                        r=[RK[1]], w=[xk])
                S.dma("sync", xres[128 * m:128 * m + 128, c0:c0 + 512], xb,
                      r=[xk], w=[f"xr{m}.{c}"])

    def sub_ffn(first, last):
        CH = 256
        wq2t = lambda k: (arW2[:, 1024 * k:1024 * k + 1024] if k < 24 else
                          arA[:, 1024 * (k - 24):1024 * (k - 24) + 1024])
        wq2k = lambda k: f"wq2:{k}"
        if first:
            acc1 = arR2[:, 768:769]
            acc2 = arR2[:, 769:770]
            tmp1 = arR2[:, 770:771]
            S.v(lambda e: e.memset(acc1, 0.0), w=["acc1"])
            S.v(lambda e: e.memset(acc2, 0.0), w=["acc2"])
            for k in range(NTD):
                for hh in range(2):
                    S.dma("sync", arFS[:, 0:2048],
                          I["W1T"][128 * k:128 * k + 128, 2048 * hh:2048 * hh + 2048],
                          w=["gstg"])
                    S.v(lambda e: e.reduce_sum(tmp1, arFS[:, 0:2048], axis=AX,
                                               apply_absolute_value=True),
                        r=["gstg"], w=["tmp1"])
                    S.v(lambda e: e.tensor_add(acc1, acc1, tmp1),
                        r=["acc1", "tmp1"], w=["acc1"])
            for k in range(NTF):
                S.dma("sync", arFS[:, 0:1024],
                      I["W2T"][128 * k:128 * k + 128, :], w=["gstg"])
                S.v(lambda e: e.reduce_sum(tmp1, arFS[:, 0:1024], axis=AX,
                                           apply_absolute_value=True),
                    r=["gstg"], w=["tmp1"])
                S.v(lambda e: e.tensor_add(acc2, acc2, tmp1),
                    r=["acc2", "tmp1"], w=["acc2"])
            for gi, (acck, col0) in enumerate((("acc1", 0), ("acc2", 2))):
                accap = arR2[:, 768 + gi:769 + gi]
                mm(psR[0:1, 0:1], onesbF[:], accap, st=True, sp=True,
                   r=["onesbF", acck], w=["psR"])
                S.v(lambda e: e.tensor_scalar(CL(2), psR[0:1, 0:1],
                                              1.0 / (DFF * D), EPSLN,
                                              ALU.mult, ALU.add),
                    r=["psR"], w=[CK(2)])
                S.v(lambda e: e.reciprocal(CL(2), CL(2)), r=[CK(2)], w=[CK(2)])
                S.v(lambda e: e.tensor_scalar(CL(3), CL(2), -1.0, None,
                                              ALU.mult), r=[CK(2)], w=[CK(3)])
                for dcol, ck in ((col0, CK(2)), (col0 + 1, CK(3))):
                    cap = cells[0:1, (2 if dcol in (col0,) else 3):
                                (3 if dcol in (col0,) else 4)]
                    mm(psR[:, 1:2], onesrF[0:1, :],
                       CL(2) if dcol == col0 else CL(3), st=True, sp=True,
                       r=["onesrF", ck], w=["psRb"])
                    S.v(lambda e, dcol=dcol: e.tensor_copy(
                        gamr[:, dcol:dcol + 1], psR[:, 1:2]),
                        r=["psRb"], w=[f"gamr{dcol}"])
            S.fence(cells[0:1, 14:15])
            for k in range(NTD):
                for hh in range(2):
                    S.dma("sync", arFS[:, 0:2048],
                          I["W1T"][128 * k:128 * k + 128, 2048 * hh:2048 * hh + 2048],
                          w=["qstg"])
                    S.v(lambda e: e.tensor_scalar(arS[:, 0:2048], arFS[:, 0:2048],
                                                  gamr[:, 0:1], 0.5, ALU.mult,
                                                  ALU.is_ge),
                        r=["qstg", "gamr0"], w=["cA"])
                    S.v(lambda e: e.tensor_scalar(arS[:, 2048:4096], arFS[:, 0:2048],
                                                  gamr[:, 1:2], 0.5, ALU.mult,
                                                  ALU.is_ge),
                        r=["qstg", "gamr1"], w=["cB"])
                    dst = arW1[:, 4096 * k + 2048 * hh:4096 * k + 2048 * hh + 2048]
                    S.v(lambda e, dst=dst: e.tensor_sub(dst, arS[:, 0:2048],
                                                        arS[:, 2048:4096]),
                        r=["cA", "cB"], w=[f"wq1:{k}"])
                    S.dma("sync", Wq1s[128 * k:128 * k + 128,
                                       2048 * hh:2048 * hh + 2048], dst,
                          r=[f"wq1:{k}"], w=["wq1s"])
            for k in range(NTF):
                S.dma("sync", arFS[:, 0:1024],
                      I["W2T"][128 * k:128 * k + 128, :], w=["qstg"])
                S.v(lambda e: e.tensor_scalar(arS[:, 0:1024], arFS[:, 0:1024],
                                              gamr[:, 2:3], 0.5, ALU.mult,
                                              ALU.is_ge),
                    r=["qstg", "gamr2"], w=["cA"])
                S.v(lambda e: e.tensor_scalar(arS[:, 1024:2048], arFS[:, 0:1024],
                                              gamr[:, 3:4], 0.5, ALU.mult,
                                              ALU.is_ge),
                    r=["qstg", "gamr3"], w=["cB"])
                S.v(lambda e, k=k: e.tensor_sub(wq2t(k), arS[:, 0:1024],
                                                arS[:, 1024:2048]),
                    r=["cA", "cB"], w=[wq2k(k)])
                S.dma("sync", Wq2s[128 * k:128 * k + 128, :], wq2t(k),
                      r=[wq2k(k)], w=["wq2s"])
        else:
            for k in range(NTD):
                S.dma("sync", arW1[:, 4096 * k:4096 * k + 4096],
                      Wq1s[128 * k:128 * k + 128, :], w=[f"wq1:{k}"])
            for k in range(NTF):
                S.dma("sync", wq2t(k), Wq2s[128 * k:128 * k + 128, :],
                      w=[wq2k(k)])
        S.fence(cells[0:1, 14:15])

        out_dram = y_cm if last else xres

        def ffn_chunk(c):
            c0 = c * CH
            par = c % 2
            xnb = lambda j, par=par: arS[:, 256 * (2 * j + par):256 * (2 * j + par) + CH]
            xnk = lambda j, par=par: f"fx{par}.{j}"
            yield from ln_bf("F", xres, c0, CH, xnb, xnk, par, clip8=False,
                  sqb=lambda i, par=par: sqS[:, 256 * (2 * par + i):
                                             256 * (2 * par + i) + 256])
            S.prewait("tensor", [f"wq1:{k}" for k in range(NTD)] +
                      [xnk(k) for k in range(NTD)])
            for m in range(NTF):
                pp, pk = (psA, "psA") if m % 2 == 0 else (psB, "psB")
                for k in range(NTD):
                    mm(pp[:, 0:CH], arW1[:, 4096 * k + 128 * m:4096 * k + 128 * m + 128],
                       xnb(k), st=(k == 0), sp=(k == NTD - 1),
                       r=[f"wq1:{k}", xnk(k)], w=[pk])
                act(arA[:, 8192 + 256 * m:8192 + 256 * m + CH], pp[:, 0:CH],
                    AF.Relu, bias=b18[:, m:m + 1], r=[pk, "b1v"], w=[f"fh{m}"])
            S.prewait("tensor", [wq2k(k) for k in range(NTF)] +
                      [f"fh{k}" for k in range(NTF)])
            for m in range(NTD):
                pp, pk = (psA, "psA") if m % 2 == 0 else (psB, "psB")
                for k in range(NTF):
                    mm(pp[:, 0:CH], wq2t(k)[:, 128 * m:128 * m + 128],
                       arA[:, 8192 + 256 * k:8192 + 256 * k + CH],
                       st=(k == 0), sp=(k == NTF - 1),
                       r=[wq2k(k), f"fh{k}"], w=[pk])
                fxo = arR2[:, 256 * par:256 * par + CH]
                fxn = arR2[:, 512 + 256 * par:512 + 256 * par + CH]
                S.dma("sync", fxo,
                      xres[128 * m:128 * m + 128, c0:c0 + CH], w=[f"fxo{par}"])
                S.v(lambda e, m=m, pp=pp, fxo=fxo, fxn=fxn: e.scalar_tensor_tensor(
                    fxn, pp[:, 0:CH], b28[:, m:m + 1],
                    fxo, ALU.add, ALU.add),
                    r=[pk, "b2v", f"fxo{par}"], w=[f"fxn{par}"])
                S.dma("sync", out_dram[128 * m:128 * m + 128, c0:c0 + CH],
                      fxn, r=[f"fxn{par}"],
                      w=[f"fout{m}.{c}"])

        pipe2([ffn_chunk(c) for c in range(T // CH)], depth=1)

    sub1()
    S.fence(cells[0:1, 14:15])
    if n_sub < 2:
        _copy_out(S, E, xres, y_cm)
        return
    def sub3():
        S.dma("gpsimd", arW1[:, 0:32768].rearrange("p (j o) -> p j o", o=2 * DI),
              I["WinT"].rearrange("(j p) o -> p j o", p=128), w=["mWin"])
        for j in range(NTI):
            S.v(lambda e, j=j: e.memset(arW2[0:128, 515 * j:515 * j + 3], 0.0),
                w=[f"xp{j}"])
        for c in range(4):
            c0 = 512 * c
            for half in range(2):
                cc0 = c0 + 256 * half
                xlb = lambda j: arA[:, 256 * j:256 * j + 256]
                xlk = lambda j: f"xlb{j}"
                ln_chunk("M1", xres, cc0, 256, xlb, xlk, clip8=False,
                         sqb=lambda i: arA[:, 4096 + 256 * i:4096 + 256 * i + 256])
                for j in range(NTD):
                    S.dma("sync", xlns[128 * j:128 * j + 128, cc0:cc0 + 256],
                          xlb(j), r=[xlk(j)], w=[f"xlns{j}"])
                R = [arRow[0:1, 512 * i:512 * i + 256] for i in range(4)]
                for j in range(NTD):
                    mm(psR[0:1, 0:256], onesb[:], xlb(j), st=(j == 0),
                       sp=(j == NTD - 1), r=["onesb", xlk(j)], w=["psR"])
                S.v(lambda e: e.tensor_scalar(R[0], psR[0:1, 0:256], 1.0 / D,
                                              None, ALU.mult), r=["psR"], w=["L2mu"])
                for j in range(NTD):
                    sq = arW2[:, 20480 + 256 * (j % 2):20480 + 256 * (j % 2) + 256]
                    S.v(lambda e, j=j, sq=sq: e.tensor_mul(sq, xlb(j), xlb(j)),
                        r=[xlk(j)], w=[f"sqt{j % 2}"])
                    mm(psA[0:1, 0:256], onesb[:], sq, st=(j == 0),
                       sp=(j == NTD - 1), r=["onesb", f"sqt{j % 2}"], w=["psA"])
                S.v(lambda e: e.tensor_scalar(R[1], psA[0:1, 0:256], 1.0 / D,
                                              None, ALU.mult), r=["psA"], w=["L2e2"])
                S.v(lambda e: e.tensor_mul(R[2], R[0], R[0]), r=["L2mu"], w=["L2m2"])
                S.v(lambda e: e.tensor_sub(R[1], R[1], R[2]), r=["L2e2", "L2m2"],
                    w=["L2v"])
                act(R[1], R[1], AF.Ln, bias=cst[0:1, 1:2], r=["L2v", "cst1"],
                    w=["L2v"])
                act(R[1], R[1], AF.Exp, scale=-0.5, r=["L2v"], w=["L2rs"])
                S.v(lambda e: e.tensor_mul(R[0], R[0], R[1]), r=["L2mu", "L2rs"],
                    w=["L2mr"])
                S.v(lambda e: e.tensor_scalar(R[0], R[0], -1.0, None, ALU.mult),
                    r=["L2mr"], w=["L2nm"])
                mm(psL[:, 0:256], onesrF[0:1, :], R[1], st=True, sp=True,
                   r=["onesrF", "L2rs"], w=["psL0"])
                mm(psL[:, 256:512], onesrF[0:1, :], R[0], st=True, sp=True,
                   r=["onesrF", "L2nm"], w=["psL0"])
                for j in range(NTD):
                    fs = xlb(j)
                    fk2 = xlk(j)
                    S.v(lambda e, fs=fs: e.tensor_mul(fs, fs, psL[:, 0:256]),
                        r=[fk2, "psL0", f"xlns{j}"], w=[fk2])
                    S.v(lambda e, fs=fs: e.tensor_add(fs, fs, psL[:, 256:512]),
                        r=[fk2, "psL0"], w=[fk2])
                    dst = arS[:, 512 * j + 256 * half:512 * j + 256 * half + 256]
                    S.v(lambda e, fs=fs, dst=dst, j=j: e.tensor_scalar(
                        dst, fs, nw8[:, j:j + 1], nb8[:, j:j + 1], ALU.mult,
                        ALU.add), r=[fk2, "nwv", "nbv"], w=[f"xn2{j}"])
            S.prewait("tensor", ["mWin"] + [f"xn2{k}" for k in range(NTD)])
            for m in range(2 * NTI):
                pp, pk = (psA, "psA") if m % 2 == 0 else (psB, "psB")
                for k in range(NTD):
                    mm(pp[:, 0:512], arW1[:, 4096 * k + 128 * m:4096 * k + 128 * m + 128],
                       arS[:, 512 * k:512 * k + 512], st=(k == 0),
                       sp=(k == NTD - 1), r=["mWin", f"xn2{k}"], w=[pk])
                if m < NTI:
                    S.v(lambda e, m=m, pp=pp: e.tensor_copy(
                        arW2[:, 515 * m + 3:515 * m + 515], pp[:, 0:512]),
                        r=[pk], w=[f"xp{m}"])
                else:
                    mz = m - NTI
                    zb = arA[:, 2048 + 512 * (mz % 2):2048 + 512 * (mz % 2) + 512]
                    S.s(lambda e, pp=pp, zb=zb: e.activation(zb, pp[:, 0:512],
                                                             AF.Silu),
                        r=[pk], w=[f"zsl{mz % 2}"])
                    S.dma("sync", szs[128 * mz:128 * mz + 128, c0:c0 + 512], zb,
                          r=[f"zsl{mz % 2}"], w=["szs"])
            for j in range(NTI):
                acc = arR2[:, 0:512]
                S.v(lambda e, j=j, acc=acc: e.tensor_scalar(
                    acc, arW2[:, 515 * j:515 * j + 512], cw16[:, 4 * j:4 * j + 1],
                    cb16[:, j:j + 1], ALU.mult, ALU.add),
                    r=[f"xp{j}", "cw16", "convb"], w=["cacc"])
                for kk in range(1, 4):
                    S.v(lambda e, j=j, kk=kk, acc=acc: e.scalar_tensor_tensor(
                        acc, arW2[:, 515 * j + kk:515 * j + kk + 512],
                        cw16[:, 4 * j + kk:4 * j + kk + 1], acc, ALU.mult,
                        ALU.add), r=[f"xp{j}", "cw16", "cacc"], w=["cacc"])
                xct = arA[:, 3072 + 512 * (j % 2):3072 + 512 * (j % 2) + 512]
                S.s(lambda e, acc=acc, xct=xct: e.activation(xct, acc, AF.Silu),
                    r=["cacc"], w=[f"xcb{j % 2}"])
                S.dma("sync", xcs[128 * j:128 * j + 128, c0:c0 + 512], xct,
                      r=[f"xcb{j % 2}"], w=["xcs"])
                S.v(lambda e, j=j: e.tensor_copy(arW2[:, 515 * j:515 * j + 3],
                                                 arW2[:, 515 * j + 512:515 * j + 515]),
                    r=[f"xp{j}"], w=[f"xp{j}"])
        S.fence(cells[0:1, 14:15])
        S.dma("gpsimd", arW1[:, 0:32768].rearrange("p (j o) -> p j o", o=DI),
              I["WdtT"].rearrange("(j p) o -> p j o", p=128), w=["mWdt"])
        for c in range(4):
            c0 = 512 * c
            for k in range(NTI):
                S.dma("sync", arW2[:, 512 * k:512 * k + 512],
                      xcs[128 * k:128 * k + 128, c0:c0 + 512], w=[f"m2xc{k}"])
            S.prewait("tensor", ["mWdt"] + [f"m2xc{k}" for k in range(NTI)])
            for m in range(NTI):
                pp, pk = (psA, "psA") if m % 2 == 0 else (psB, "psB")
                for k in range(NTI):
                    mm(pp[:, 0:512], arW1[:, 2048 * k + 128 * m:2048 * k + 128 * m + 128],
                       arW2[:, 512 * k:512 * k + 512], st=(k == 0),
                       sp=(k == NTI - 1), r=["mWdt", f"m2xc{k}"], w=[pk])
                tb = arS[:, 512 * (m % 2):512 * (m % 2) + 512]
                S.s(lambda e, pp=pp, tb=tb, m=m: e.activation(
                    tb, pp[:, 0:512], AF.Exp, bias=bdt16[:, m:m + 1], scale=1.0),
                    r=[pk, "bdtv"], w=[f"m2e{m % 2}"])
                dtt = arS[:, 1024 + 512 * (m % 2):1024 + 512 * (m % 2) + 512]
                S.s(lambda e, tb=tb, dtt=dtt: e.activation(
                    dtt, tb, AF.Ln, bias=cst[:, 0:1], scale=1.0),
                    r=[f"m2e{m % 2}", "cst0"], w=[f"m2d{m % 2}"])
                S.dma("sync", dts[128 * m:128 * m + 128, c0:c0 + 512], dtt,
                      r=[f"m2d{m % 2}"], w=["dts"])
        S.fence(cells[0:1, 14:15])
        S.dma("gpsimd", arW1[:, 0:16384].rearrange("p (j o) -> p j o", o=D),
              I["WoutT"].rearrange("(j p) o -> p j o", p=128), w=["mWo"])
        for c in range(4):
            c0 = 512 * c
            for k in range(NTI):
                S.dma("sync", arW2[:, 512 * k:512 * k + 512],
                      xcs[128 * k:128 * k + 128, c0:c0 + 512], w=[f"m3xc{k}"])
                S.dma("sync", arW2[:, 8192 + 512 * k:8192 + 512 * k + 512],
                      dts[128 * k:128 * k + 128, c0:c0 + 512], w=[f"m3dt{k}"])
            S.prewait("tensor", ["wbc"] + [f"m3xc{k}" for k in range(NTI)])
            for bi, wcol in ((0, 0), (1, 256)):
                for k in range(NTI):
                    mm(psR[0:16, 0:512], wbc[:, wcol + 16 * k:wcol + 16 * k + 16],
                       arW2[:, 512 * k:512 * k + 512], st=(k == 0),
                       sp=(k == NTI - 1), r=["wbc", f"m3xc{k}"], w=["psR"])
                S.v(lambda e, bi=bi: e.tensor_copy(
                    brow[0:16, 512 * bi:512 * bi + 512], psR[0:16, 0:512]),
                    r=["psR"], w=[f"brow{bi}"])
            for n in range(16):
                for bi, dst0 in ((0, 0), (1, 8192)):
                    mm(psL[:, 0:512], sel16[:, 128 * n:128 * n + 128],
                       brow[0:16, 512 * bi:512 * bi + 512], st=True, sp=True,
                       r=["sel16H", f"brow{bi}"], w=["psL0"])
                    S.v(lambda e, n=n, dst0=dst0: e.tensor_copy(
                        arA[:, dst0 + 512 * n:dst0 + 512 * n + 512], psL[:, 0:512]),
                        r=["psL0"], w=[f"bc{dst0}:{n}"])
            LB = [dict(u=arS[:, 0:512],
                       a=[arS[:, 512:1024], arS[:, 1024:1536]],
                       b=arS[:, 1536:2048],
                       h=[arS[:, 2048:2560], arS[:, 2560:3072]],
                       sz=arS[:, 3072:3584],
                       y=arW2[:, 24064:24576]),
                  dict(u=arQK2[:, 0:512],
                       a=[arQK2[:, 512:1024], arQK2[:, 1024:1536]],
                       b=arQK2[:, 1536:2048],
                       h=[arQK2[:, 2048:2560], arQK2[:, 2560:3072]],
                       sz=arQK2[:, 3072:3584],
                       y=arQK2[:, 3584:4096])]

            def scan_u(j, L):
                B = LB[L]
                S.v(lambda e, j=j, B=B: e.tensor_mul(
                    B["u"], arW2[:, 8192 + 512 * j:8192 + 512 * j + 512],
                    arW2[:, 512 * j:512 * j + 512]),
                    r=[f"m3dt{j}", f"m3xc{j}"], w=[f"m3u{L}"])
                S.dma("sync", B["sz"], szs[128 * j:128 * j + 128, c0:c0 + 512],
                      w=[f"m3sz{L}"])

            def scan_n_front(j, n, L):
                B = LB[L]
                ab, hb, bb = B["a"][n % 2], B["h"][n % 2], B["b"]
                crk = f"cr{16 * j + n}"
                S.s(lambda e, j=j, n=n, ab=ab: e.activation(
                    ab, arW2[:, 8192 + 512 * j:8192 + 512 * j + 512], AF.Exp,
                    scale=negA[:, 16 * j + n:16 * j + n + 1]),
                    r=[f"m3dt{j}", "negA"], w=[f"m3a{L}{n % 2}"])
                S.v(lambda e, n=n, bb=bb, B=B: e.tensor_tensor(
                    bb, B["u"], arA[:, 512 * n:512 * n + 512], ALU.mult),
                    r=[f"m3u{L}", f"bc0:{n}"], w=[f"m3b{L}"])
                S.v(lambda e, j=j, n=n, ab=ab, bb=bb, hb=hb: e.tensor_tensor_scan(
                    hb, ab, bb, carry[:, 16 * j + n:16 * j + n + 1], ALU.mult,
                    ALU.add), r=[f"m3a{L}{n % 2}", f"m3b{L}", crk],
                    w=[f"m3h{L}{n % 2}"])

            def scan_n_back(j, n, L):
                B = LB[L]
                hb = B["h"][n % 2]
                crk = f"cr{16 * j + n}"
                S.s(lambda e, j=j, n=n, hb=hb: e.activation(
                    carry[:, 16 * j + n:16 * j + n + 1], hb[:, 511:512],
                    AF.Copy), r=[f"m3h{L}{n % 2}"], w=[crk])
                S.v(lambda e, n=n, hb=hb: e.tensor_mul(
                    hb, hb, arA[:, 8192 + 512 * n:8192 + 512 * n + 512]),
                    r=[f"m3h{L}{n % 2}", f"bc8192:{n}"], w=[f"m3h{L}{n % 2}"])
                yac = B["y"]
                if n == 0:
                    S.v(lambda e, hb=hb, yac=yac: e.tensor_copy(yac, hb),
                        r=[f"m3h{L}{n % 2}"], w=[f"m3y{L}"])
                else:
                    S.v(lambda e, hb=hb, yac=yac: e.tensor_add(yac, yac, hb),
                        r=[f"m3h{L}{n % 2}", f"m3y{L}"], w=[f"m3y{L}"])

            def scan_tail(j, L):
                B = LB[L]
                yac = B["y"]
                S.v(lambda e, j=j, yac=yac: e.scalar_tensor_tensor(
                    yac, arW2[:, 512 * j:512 * j + 512],
                    dd16[:, j:j + 1], yac, ALU.mult, ALU.add),
                    r=[f"m3xc{j}", "Dv", f"m3y{L}"], w=[f"m3y{L}"])
                S.v(lambda e, j=j, B=B, yac=yac: e.tensor_mul(
                    arW2[:, 16384 + 512 * j:16384 + 512 * j + 512], yac,
                    B["sz"]), r=[f"m3y{L}", f"m3sz{L}"], w=[f"y3{j}"])

            for jp in range(NTI // 2):
                jA, jB = 2 * jp, 2 * jp + 1
                scan_u(jA, 0)
                scan_u(jB, 1)
                for n in range(16):
                    scan_n_front(jA, n, 0)
                    scan_n_front(jB, n, 1)
                    scan_n_back(jA, n, 0)
                    scan_n_back(jB, n, 1)
                scan_tail(jA, 0)
                scan_tail(jB, 1)
            S.prewait("tensor", ["mWo"] + [f"y3{k}" for k in range(NTI)])
            for m in range(NTD):
                pp, pk = (psA, "psA") if m % 2 == 0 else (psB, "psB")
                for k in range(NTI):
                    mm(pp[:, 0:512], arW1[:, 1024 * k + 128 * m:1024 * k + 128 * m + 128],
                       arW2[:, 16384 + 512 * k:16384 + 512 * k + 512],
                       st=(k == 0), sp=(k == NTI - 1),
                       r=["mWo", f"y3{k}"], w=[pk])
                S.dma("sync", arR2[:, 0:512],
                      xres[128 * m:128 * m + 128, c0:c0 + 512], w=["m3x"])
                S.dma("gpsimd", arR2[:, 512:1024],
                      xlns[128 * m:128 * m + 128, c0:c0 + 512], w=["m3xl"])
                S.v(lambda e, pp=pp: e.tensor_add(arR2[:, 0:512], pp[:, 0:512],
                                                  arR2[:, 0:512]),
                    r=[pk, "m3x"], w=["m3x"])
                S.v(lambda e: e.tensor_add(arR2[:, 0:512], arR2[:, 0:512],
                                           arR2[:, 512:1024]),
                    r=["m3x", "m3xl"], w=["m3x"])
                S.dma("sync", xres[128 * m:128 * m + 128, c0:c0 + 512],
                      arR2[:, 0:512], r=["m3x"], w=[f"m3out{m}.{c}"])

    sub_ffn(True, last=False)
    S.fence(cells[0:1, 14:15])
    if n_sub < 3:
        _copy_out(S, E, xres, y_cm)
        return
    sub3()
    S.fence(cells[0:1, 14:15])
    if n_sub < 4:
        _copy_out(S, E, xres, y_cm)
        return
    sub_ffn(False, last=True)


def _copy_out(S, E, src, y_cm):
    arFS = E["arFS"]
    for c in range(8):
        c0 = 256 * c
        for m in range(NTD):
            S.dma("sync", arFS[:, 256 * m:256 * m + 256],
                  src[128 * m:128 * m + 128, c0:c0 + 256], w=[f"co{m}"])
            S.dma("sync", y_cm[128 * m:128 * m + 128, c0:c0 + 256],
                  arFS[:, 256 * m:256 * m + 256], r=[f"co{m}"],
                  w=[f"yo{m}.{c}"])


def _host_inputs(ins):
    import ml_dtypes
    bf = ml_dtypes.bfloat16
    tr = lambda a: np.ascontiguousarray(np.asarray(a).T)
    col = lambda a: np.ascontiguousarray(np.asarray(a).reshape(-1, 1))
    selH = np.zeros((16, NTD * 128), dtype=np.float32)
    for k in range(NTD):
        selH[2 * k, 128 * k:128 * k + 64] = 1.0
        selH[2 * k + 1, 128 * k + 64:128 * k + 128] = 1.0
    sel16H = np.zeros((16, 16 * 128), dtype=np.float32)
    for n in range(16):
        sel16H[n, 128 * n:128 * n + 128] = 1.0
    place2H = np.zeros((2, 128), dtype=np.float32)
    place2H[0, 0] = 1.0
    place2H[1, 64] = 1.0
    mask2H = np.zeros((128, 2), dtype=np.float32)
    mask2H[1:64, 0] = 1.0
    mask2H[65:128, 1] = 1.0
    maskPH = (mask2H[:, 0] + mask2H[:, 1]).reshape(128, 1)
    maskDH = np.ones((128, 1), dtype=np.float32)
    maskDH[0, 0] = 0.0
    com = dict(
        WqT=tr(ins["L0_attn_a_Wq"]), WkT=tr(ins["L0_attn_a_Wk"]),
        WvT=tr(ins["L0_attn_a_Wv"]), WoT=tr(ins["L0_attn_a_Wo"]),
        W1T=tr(ins["L0_ffn_W1"]), W2T=tr(ins["L0_ffn_W2"]),
        WinT=tr(ins["L0_attn_b_W_in"]), WdtT=tr(ins["L0_attn_b_W_dt"]),
        WBT=tr(ins["L0_attn_b_W_B"]), WCT=tr(ins["L0_attn_b_W_C"]),
        WoutT=tr(ins["L0_attn_b_W_out"]),
        b1v=col(ins["L0_ffn_b1"]), b2v=col(ins["L0_ffn_b2"]),
        nwv=col(ins["L0_attn_b_norm_w"]), nbv=col(ins["L0_attn_b_norm_b"]),
        convw=np.ascontiguousarray(np.asarray(ins["L0_attn_b_conv_w"])),
        convb=col(ins["L0_attn_b_conv_b"]), bdtv=col(ins["L0_attn_b_b_dt"]),
        Dv=col(ins["L0_attn_b_D"]),
        Alog=np.ascontiguousarray(np.asarray(ins["L0_attn_b_A_log"])),
        identH=np.eye(128, dtype=bf), place2H=place2H.astype(bf),
        selHF=np.ascontiguousarray(selH),
        mask2H=mask2H.astype(bf), maskPH=maskPH.astype(bf),
        maskDH=maskDH.astype(bf), selH=selH.astype(bf),
        sel16H=sel16H.astype(bf),
    )
    xs = np.asarray(ins["L0_attn_a_x"])
    in_maps = []
    for c in range(8):
        m = dict(com)
        m["x_cm"] = np.ascontiguousarray(xs[c].T)
        in_maps.append(m)
    return in_maps


N_SUB = 4


def kernel(**ins):
    in_maps = _host_inputs(ins)
    nc = build_program(n_sub=N_SUB)
    res = run_bass_kernel_spmd(nc, in_maps, core_ids=list(range(8)))
    out = np.stack([np.asarray(res.results[c]["y_cm"]).T for c in range(8)], 0)
    return np.ascontiguousarray(out.astype(np.float32))
